# version 04
# baseline (speedup 1.0000x reference)
import contextlib
import numpy as np
import concourse.bass as bass
import concourse.mybir as mybir
from concourse.bass_utils import run_bass_kernel_spmd

F32 = mybir.dt.float32
BF16 = mybir.dt.bfloat16
AF = mybir.ActivationFunctionType
ALU = mybir.AluOpType
AX = mybir.AxisListType
RC = float(3 * 2 ** 22)

NCORES = 8
BPC = 4
S = 2048
T = BPC * S
D = 1024
NT = T // 128
EPS = 1e-5
SERIAL_E = False



class Prog:
    ENGS = ("sp", "pe", "dve", "act", "pool")

    def __init__(self):
        self.items = {e: [] for e in self.ENGS}
        self.counts = {}

    def _tok(self, sem, delta):
        v = self.counts.get(sem, 0) + delta
        self.counts[sem] = v
        return (sem, v)

    def op(self, eng, emit, after=(), signal=False, dma=False):
        waits = []
        drain = False
        for tok in (after or ()):
            if tok is None:
                continue
            sem, val = tok
            if sem == "e." + eng:
                drain = True
            else:
                waits.append((sem, val))
        tok = None
        incs = []
        if dma:
            tok = self._tok("d." + eng, 16)
            incs.append(("d." + eng, 16))
        elif signal:
            tok = self._tok("e." + eng, 1)
            incs.append(("e." + eng, 1))
        self.items[eng].append((drain, waits, emit, incs))
        return tok

    def pe(self, emit, after=(), signal=False):
        return self.op("pe", emit, after, signal)

    def dve(self, emit, after=(), signal=False):
        return self.op("dve", emit, after, signal)

    def act(self, emit, after=(), signal=False):
        return self.op("act", emit, after, signal)

    def pool(self, emit, after=(), signal=False):
        return self.op("pool", emit, after, signal)

    def sp(self, emit, after=(), signal=False):
        return self.op("sp", emit, after, signal)

    def dma(self, eng, out, in_, after=(), **kw):
        return self.op(eng, (lambda h, o=out, i=in_: h.dma_start(out=o, in_=i, **kw)),
                       after, dma=True)

    def full_barrier(self):
        toks = [(sem, val) for sem, val in self.counts.items()]
        for eng in self.ENGS:
            self.op(eng, lambda h: h.nop(), after=toks)

    def finish(self, tokens):
        self.op("sp", lambda h: h.nop(), after=tokens)

    def emit(self, nc, es):
        sems = {name: es.enter_context(nc.semaphore(name.replace(".", "_")))
                for name in self.counts}
        block = es.enter_context(nc.Block())

        def runner(ename):
            def run(h):
                waited = {}
                for drain, waits, emitfn, incs in self.items[ename]:
                    if drain:
                        h.drain()
                    for sem, val in waits:
                        if waited.get(sem, 0) >= val:
                            continue
                        h.wait_ge(sems[sem], val)
                        waited[sem] = val
                    inst = emitfn(h)
                    for sem, delta in incs:
                        inst.then_inc(sems[sem], delta)
            return run

        block.sync(runner("sp"))
        block.tensor(runner("pe"))
        block.vector(runner("dve"))
        block.scalar(runner("act"))
        block.gpsimd(runner("pool"))



class Ctx:

    def __init__(self, nc, p, es, consts_dram):
        self.nc = nc
        self.p = p
        self.es = es
        sb = lambda name, shape, dt: es.enter_context(nc.sbuf_tensor(name, shape, dt))
        self.ident_f = sb("c_ident_f", [128, 128], F32)
        self.ident_b = sb("c_ident_b", [128, 128], BF16)
        self.ones_row = sb("c_ones_row", [1, 128], F32)
        self.eps_col = sb("c_eps_col", [128, 1], F32)
        self.one_col = sb("c_one_col", [128, 1], F32)
        toks = [
            p.dma("sp", self.ident_f[:], consts_dram["ident"]),
            p.dma("pool", self.ident_b[:], consts_dram["ident"]),
            p.dma("sp", self.ones_row[:], consts_dram["ones_row"]),
        ]
        toks.append(p.dve(lambda v: v.memset(self.eps_col[:], EPS), signal=True))
        toks.append(p.dve(lambda v: v.memset(self.one_col[:], 1.0), signal=True))
        self.ready = toks

    def ln_scratch(self, nslots, es=None):
        es = es or self.es
        sb = lambda name, shape, dt: es.enter_context(
            self.nc.sbuf_tensor(name, shape, dt))
        uid = f"{len(self.p.items['dve'])}"
        return dict(
            stats=sb(f"ln_stats{uid}", [128, 12 * nslots], F32),
            mv=sb(f"ln_mv{uid}", [128, 2 * nslots], F32),
            rstd=sb(f"ln_rstd{uid}", [128, nslots], F32),
            nmr=sb(f"ln_nmr{uid}", [128, nslots], F32),
            nslots=nslots,
            busy=[None] * nslots,
        )


def emit_ln(ctx, lnsc, slot, x_ap, out_ap, after=()):
    p = ctx.p
    st = lnsc["stats"][:, 12 * slot:12 * slot + 12]
    mv = lnsc["mv"][:, 2 * slot:2 * slot + 2]
    rs = lnsc["rstd"][:, slot:slot + 1]
    nm = lnsc["nmr"][:, slot:slot + 1]
    war = lnsc["busy"][slot]
    p.dve(lambda v: v.bn_stats(out=st[:, 0:6], in_=x_ap[:, 0:512]),
          after=list(after) + [war])
    p.dve(lambda v: v.bn_stats(out=st[:, 6:12], in_=x_ap[:, 512:1024]))
    t3 = p.dve(lambda v: v.bn_aggr(out=mv, in_=st), after=[("e.dve", -1)], signal=True)
    t4 = p.act(lambda a: a.activation(out=rs, in_=mv[:, 1:2], func=AF.Sqrt,
                                      bias=ctx.eps_col[:]), after=[t3], signal=True)
    t5 = p.dve(lambda v: v.reciprocal(out=rs, in_=rs), after=[t4], signal=True)
    t6 = p.dve(lambda v: v.tensor_scalar(out=nm, in0=mv[:, 0:1], scalar1=rs, scalar2=-1.0,
                                         op0=ALU.mult, op1=ALU.mult),
               after=[t5], signal=True)
    t7 = p.act(lambda a: a.activation(out=out_ap, in_=x_ap, func=AF.Identity,
                                      scale=rs, bias=nm),
               after=[t6], signal=True)
    lnsc["busy"][slot] = t7
    return t7


def tok_now(p, eng):
    c = p.counts.get("e." + eng, 0)
    return ("e." + eng, c) if c else None


def fence_all(p, tokens):
    for eng in Prog.ENGS:
        p.op(eng, lambda h: h.nop(), after=tokens)



def build_stage2(nc, p, ctx, es, x_in, x_out, w, first_chunk=0, nchunks=16):
    sb = lambda name, shape, dt: es.enter_context(nc.sbuf_tensor(name, shape, dt))
    ps = lambda name, shape: es.enter_context(nc.psum_tensor(name, shape, F32))
    psb = lambda name, shape: es.enter_context(nc.psum_tensor(name, shape, BF16))

    wiT = sb("s2_wiT", [128, 8 * 4096], BF16)
    wo = sb("s2_wo", [128, 32 * 1024], BF16)
    acat = sb("s2_acat", [128, 8 * 32], BF16)
    bcat = sb("s2_bcat", [32, 4096], BF16)
    gate = sb("s2_gate", [128, 8 * 8], BF16)
    wib = sb("s2_wib", [128, 32], F32)
    gbr = sb("s2_gbr", [1, 8], F32)
    wobr = sb("s2_wobr", [1, 1024], F32)
    wtok = []
    for c in range(8):
        wtok.append(p.dma("pool", wiT[:, c * 4096:(c + 1) * 4096],
                          w["wiT"][c * 128:(c + 1) * 128, :]))
    for f in range(0, 32, 8):
        wtok.append(p.dma("pool", wo.ap().rearrange("p (f n) -> p f n", n=1024)[:, f:f + 8, :],
                          w["wo_rhs"].rearrange("(f p) n -> p f n", p=128)[:, f:f + 8, :]))
    for c in range(8):
        wtok.append(p.dma("pool", acat[:, c * 32:(c + 1) * 32],
                          w["acat"][c * 128:(c + 1) * 128, :]))
        wtok.append(p.dma("pool", gate[:, c * 8:(c + 1) * 8],
                          w["gate_rhs"][c * 128:(c + 1) * 128, :]))
    wtok.append(p.dma("pool", bcat[:], w["bcat"]))
    wtok.append(p.dma("sp", wib[:], w["wi_b"]))
    wtok.append(p.dma("sp", gbr[:], w["gb_row"]))
    wtok.append(p.dma("sp", wobr[:], w["wob_row"]))
    wready = wtok[-1]
    wtok_all = list(wtok) + list(ctx.ready)
    fence_all(p, wtok_all)

    xt = sb("s2_xt", [128, 4 * 1024], F32)
    ln = sb("s2_ln", [128, 1024], BF16)
    ln2T = sb("s2_ln2T", [128, 8 * 512], BF16)
    interT = sb("s2_interT", [128, 32 * 512], BF16)
    mask8 = sb("s2_mask8", [128, 2 * 8], BF16)
    mask32 = sb("s2_mask32", [128, 4 * 32], BF16)
    mcol = sb("s2_mcol", [128, 4], F32)
    ttl = sb("s2_ttl", [128, 4 * 32], BF16)
    ttT = sb("s2_ttT", [32, 512], BF16)
    outt = sb("s2_outt", [128, 1024], F32)
    lnsc = ctx.ln_scratch(2, es)

    trp = [psb("s2_trp0", [128, 1024])]
    gt_ps = ps("s2_gt", [128, 512])
    ip = [ps(f"s2_ip{i}", [128, 512]) for i in range(2)]
    pp = [ps(f"s2_pp{i}", [128, 1024]) for i in range(2)]

    trp_busy = [None]
    ip_busy = [None, None]
    pp_busy = [None, None]
    xt_busy = [None] * 4
    ln_busy = [None]
    out_busy = [None]
    ln2T_busy = None
    gt_busy = None
    ttT_busy = None
    inter_busy = None

    for ch in range(first_chunk, first_chunk + nchunks):
        tt0 = ch * 4
        ln_toks = []
        for i in range(4):
            tt = tt0 + i
            xs = i
            xtok = p.dma("sp", xt[:, xs * 1024:(xs + 1) * 1024],
                         x_in[tt * 128:(tt + 1) * 128, :],
                         after=[xt_busy[xs]])
            lnt = emit_ln(ctx, lnsc, i % 2, xt[:, xs * 1024:(xs + 1) * 1024],
                          ln[:, 0:1024],
                          after=[xtok, ln_busy[0]])
            tb = 0
            trk = None
            for c in range(8):
                trk = p.pe(lambda t, c=c, tb=tb: t.transpose(
                    trp[tb][:, c * 128:(c + 1) * 128],
                    ln[:, c * 128:(c + 1) * 128],
                    ctx.ident_b[:]),
                    after=[lnt, trp_busy[tb], ln2T_busy if c == 0 else None],
                    signal=(c == 7))
            ev = p.dve(lambda v, tb=tb, i=i: v.tensor_copy(
                out=ln2T.ap().rearrange("p (c n) -> p c n", n=512)[:, :, i * 128:(i + 1) * 128],
                in_=trp[tb].ap().rearrange("p (c n) -> p c n", n=128)),
                after=[trk], signal=True)
            trp_busy[tb] = ev
            ln_busy[0] = trk
            ln_toks.append(ev)
            xt_busy[xs] = None

        ln_ready = ln_toks
        mask_toks = []
        tt_toks = []
        for i in range(4):
            gk = None
            for c in range(8):
                gk = p.pe(lambda t, c=c, i=i: t.matmul(
                    gt_ps[:, 0:8],
                    ln2T[:, c * 512 + i * 128: c * 512 + i * 128 + 128],
                    gate[:, c * 8:(c + 1) * 8],
                    start=(c == 0), stop=False),
                    after=[ln_toks[i] if c == 0 else None, gt_busy if c == 0 else None])
            gk = p.pe(lambda t: t.matmul(gt_ps[:, 0:8], ctx.ones_row[:], gbr[:],
                                         start=False, stop=True), signal=True)
            mx = p.dve(lambda v, i=i: v.tensor_reduce(
                out=mcol[:, i:i + 1], in_=gt_ps[:, 0:8], axis=AX.X, op=ALU.max),
                after=[gk], signal=True)
            mk = p.dve(lambda v, i=i: v.tensor_scalar(
                out=mask8[:, (i % 2) * 8:(i % 2) * 8 + 8], in0=gt_ps[:, 0:8],
                scalar1=mcol[:, i:i + 1], scalar2=None, op0=ALU.is_equal),
                after=[mx, mask_toks[-1] if i >= 2 else None], signal=True)
            for r in range(4):
                mk = p.dve(lambda v, i=i, r=r: v.tensor_copy(
                    out=mask32.ap().rearrange("p (i e r) -> p i e r", i=4, r=4)[:, i, :, r],
                    in_=mask8[:, (i % 2) * 8:(i % 2) * 8 + 8]),
                    after=[mk], signal=(r == 3))
            mask_toks.append(mk)
            tk = None
            for c in range(8):
                tk = p.pe(lambda t, c=c, i=i: t.matmul(
                    gt_ps[:, 64:96],
                    ln2T[:, c * 512 + i * 128: c * 512 + i * 128 + 128],
                    acat[:, c * 32:(c + 1) * 32],
                    start=(c == 0), stop=(c == 7)),
                    after=[mk if c == 0 else None], signal=(c == 7))
            tm = p.dve(lambda v, i=i: v.tensor_tensor(
                out=ttl[:, i * 32:(i + 1) * 32], in0=gt_ps[:, 64:96],
                in1=mask32[:, i * 32:(i + 1) * 32], op=ALU.mult),
                after=[tk, mk], signal=True)
            gt_busy = tm
            tr = p.pe(lambda t, i=i: t.transpose(
                trp[0][0:32, i * 128:(i + 1) * 128], ttl[:, i * 32:(i + 1) * 32],
                ctx.ident_b[:]), after=[tm, trp_busy[0] if i == 0 else None],
                signal=(i == 3))
            if i == 3:
                ev = p.dve(lambda v: v.tensor_copy(
                    out=ttT[:], in_=trp[0][0:32, 0:512]),
                    after=[tr, ttT_busy], signal=True)
                trp_busy[0] = ev
                tt_toks.append(ev)
        ttT_ready = tt_toks[-1]

        it_toks = []
        for f in range(32):
            b = f % 2
            mmk = None
            for c in range(8):
                mmk = p.pe(lambda t, f=f, c=c, b=b: t.matmul(
                    ip[b][:, :],
                    wiT[:, c * 4096 + f * 128: c * 4096 + (f + 1) * 128],
                    ln2T[:, c * 512:(c + 1) * 512],
                    start=(c == 0), stop=False),
                    after=([ip_busy[b]] + ln_ready + [ttT_ready]) if c == 0 else None)
            mmk = p.pe(lambda t, f=f, b=b: t.matmul(
                ip[b][:, :], bcat[0:32, f * 128:(f + 1) * 128], ttT[:],
                start=False, stop=True), signal=True)
            rk = p.act(lambda a, f=f, b=b: a.activation(
                out=interT[:, f * 512:(f + 1) * 512], in_=ip[b][:, :], func=AF.Relu,
                bias=wib[:, f:f + 1]),
                after=[mmk, inter_busy if f < 2 else None], signal=True)
            ip_busy[b] = rk
            it_toks.append(rk)

        ep_toks = []
        for i in range(4):
            b = i % 2
            ob = 0
            xs = i
            mmk = None
            for f in range(32):
                for nh in range(2):
                    mmk = p.pe(lambda t, f=f, i=i, b=b, nh=nh: t.matmul(
                        pp[b][:, nh * 512:(nh + 1) * 512],
                        interT[:, f * 512 + i * 128: f * 512 + (i + 1) * 128],
                        wo[:, f * 1024 + nh * 512: f * 1024 + nh * 512 + 512],
                        start=(f == 0), stop=False),
                        after=[pp_busy[b], it_toks[f]] if (f == 0 and nh == 0) else (
                            [it_toks[f]] if nh == 0 else None))
            for nh in range(2):
                mmk = p.pe(lambda t, nh=nh, b=b: t.matmul(
                    pp[b][:, nh * 512:(nh + 1) * 512], ctx.ones_row[:],
                    wobr[:, nh * 512:(nh + 1) * 512], start=False, stop=True),
                    signal=(nh == 1))
            ek = p.dve(lambda v, b=b, ob=ob, xs=xs: v.tensor_tensor(
                out=outt[:, ob * 1024:(ob + 1) * 1024], in0=pp[b][:, :],
                in1=xt[:, xs * 1024:(xs + 1) * 1024], op=ALU.add),
                after=[mmk, out_busy[ob]], signal=True)
            pp_busy[b] = ek
            tt = tt0 + i
            stok = p.dma("sp", x_out[tt * 128:(tt + 1) * 128, :],
                         outt[:, ob * 1024:(ob + 1) * 1024], after=[ek])
            out_busy[ob] = stok
            xt_busy[xs] = ek
            ep_toks.append(stok)
        inter_busy = ep_toks[-1]
        ln2T_busy = ep_toks[-1]
        ttT_busy = ep_toks[-1]
    return ep_toks





S1_KEYS = ("wqT", "wkT", "wvT", "woT", "bo_row")


def build_stage1(nc, p, ctx, es, x_in, x_out, w):
    sb = lambda name, shape, dt: es.enter_context(nc.sbuf_tensor(name, shape, dt))
    ps = lambda name, shape: es.enter_context(nc.psum_tensor(name, shape, F32))
    psb = lambda name, shape: es.enter_context(nc.psum_tensor(name, shape, BF16))

    lnT = sb("s1_lnT", [128, 16384], BF16)
    qT = sb("s1_qT", [128, 16384], BF16)
    vsb = sb("s1_v", [128, 16384], BF16)
    sksb = sb("s1_sk", [128, 16384], BF16)
    wsl = [sb(f"s1_w{i}", [128, 8192], BF16) for i in range(2)]
    bor = sb("s1_bor", [1, 1024], F32)
    xt = sb("s1_xt", [128, 2 * 1024], F32)
    xt2 = sb("s1_xt2", [128, 2 * 1024], F32)
    ln = sb("s1_ln", [128, 2 * 1024], BF16)
    qbf = sb("s1_qbf", [128, 1024], BF16)
    nmx = sb("s1_nmx", [128, 8], F32)
    qs = sb("s1_qs", [128, 8], F32)
    kraw = sb("s1_kraw", [128, 2048], F32)
    qexp = kraw
    kbf = sb("s1_kbf", [128, 2048], BF16)
    knm = sb("s1_knm", [128, 2], F32)
    ctxb = sb("s1_ctxb", [128, 1024], BF16)
    outt = sb("s1_outt", [128, 1024], F32)
    lnsc = ctx.ln_scratch(2, es)

    trp = [psb(f"s1_trp{i}", [128, 1024]) for i in range(2)]
    pq = ps("s1_pq", [128, 1024])
    pv = ps("s1_pv", [128, 1024])
    pk = ps("s1_pk", [128, 512])

    bo_tok = p.dma("sp", bor[:], w["bo_row"])
    fence_all(p, [bo_tok] + list(ctx.ready))

    trp_busy = [None, None]
    xt_busy = [None, None]
    xt2_busy = [None, None]
    ln_busy = [None, None]
    pq_busy = None
    pv_busy = None
    pk_busy = None
    wsl_busy = [None, None]
    lnT_busy = None
    qT_busy = None
    v_busy = None
    sk_busy = None
    qwork_busy = None
    ctx_busy = [None] * 4
    ctxb_busy = None
    out_busy = None
    last = []

    for b in range(BPC):
        base = b * S
        wq_t = p.dma("pool", wsl[0].ap().rearrange("p (c n) -> p c n", n=1024),
                     w["wqT"].rearrange("(c p) n -> p c n", p=128), after=[wsl_busy[0]])
        wv_t = p.dma("pool", wsl[1].ap().rearrange("p (c n) -> p c n", n=1024),
                     w["wvT"].rearrange("(c p) n -> p c n", p=128), after=[wsl_busy[1]])
        qmm_last = None
        vmm_last = None
        for tt in range(16):
            sl = tt % 2
            row0 = (base + tt * 128)
            xtok = p.dma("sp", xt[:, sl * 1024:(sl + 1) * 1024],
                         x_in[row0:row0 + 128, :], after=[xt_busy[sl]])
            lnt = emit_ln(ctx, lnsc, sl, xt[:, sl * 1024:(sl + 1) * 1024],
                          ln[:, sl * 1024:(sl + 1) * 1024],
                          after=[xtok, ln_busy[sl]])
            tb = tt % 2
            trk = None
            for c in range(8):
                trk = p.pe(lambda t, c=c, tb=tb, sl=sl: t.transpose(
                    trp[tb][:, c * 128:(c + 1) * 128],
                    ln[:, sl * 1024 + c * 128: sl * 1024 + (c + 1) * 128],
                    ctx.ident_b[:]),
                    after=[lnt, trp_busy[tb], lnT_busy if (tt == 0 and c == 0) else None],
                    signal=(c == 7))
            ln_busy[sl] = trk
            evl = p.dve(lambda v, tb=tb, tt=tt: v.tensor_copy(
                out=lnT.ap().rearrange("p (c n) -> p c n", n=2048)[:, :, tt * 128:(tt + 1) * 128],
                in_=trp[tb].ap().rearrange("p (c n) -> p c n", n=128)),
                after=[trk], signal=True)
            trp_busy[tb] = evl
            xt_busy[sl] = trk
            xt_busy[sl] = lnt
            qk = None
            for nh in range(2):
                for c in range(8):
                    qk = p.pe(lambda t, c=c, nh=nh, tt=tt: t.matmul(
                        pq[:, nh * 512:(nh + 1) * 512],
                        lnT.ap().rearrange("p (c n) -> p c n", n=2048)[:, c, tt * 128:(tt + 1) * 128],
                        wsl[0][:, c * 1024 + nh * 512: c * 1024 + nh * 512 + 512],
                        start=(c == 0), stop=(c == 7)),
                        after=[evl, pq_busy, wq_t] if (c == 0 and nh == 0) else None,
                        signal=(c == 7 and nh == 1))
            qmm_last = qk
            sf_toks = []
            for h in range(8):
                mxk = p.dve(lambda v, h=h: v.tensor_reduce(
                    out=nmx[:, h:h + 1], in_=pq[:, h * 128:(h + 1) * 128],
                    axis=AX.X, op=ALU.max, negate=True),
                    after=[qk, qwork_busy] if h == 0 else None, signal=(h == 7))
            for h in range(8):
                ek = p.act(lambda a, h=h: a.activation(
                    out=qexp[:, h * 128:(h + 1) * 128], in_=pq[:, h * 128:(h + 1) * 128],
                    func=AF.Exp, bias=nmx[:, h:h + 1], accum_out=qs[:, h:h + 1]),
                    after=[mxk] if h == 0 else None, signal=(h == 7))
            rk = p.dve(lambda v: v.reciprocal(out=qs[:], in_=qs[:]), after=[ek], signal=True)
            pq_busy = rk
            qn = None
            for h in range(8):
                qn = p.dve(lambda v, h=h: v.tensor_scalar(
                    out=qbf[:, h * 128:(h + 1) * 128], in0=qexp[:, h * 128:(h + 1) * 128],
                    scalar1=qs[:, h:h + 1], scalar2=None, op0=ALU.mult),
                    after=[rk] if h == 0 else None, signal=(h == 7))
            qwork_busy = qn
            tb2 = tt % 2
            trq = None
            for c in range(8):
                trq = p.pe(lambda t, c=c, tb2=tb2: t.transpose(
                    trp[tb2][:, c * 128:(c + 1) * 128],
                    qbf[:, c * 128:(c + 1) * 128], ctx.ident_b[:]),
                    after=[qn, trp_busy[tb2], qT_busy if (tt == 0 and c == 0) else None],
                    signal=(c == 7))
            evq = p.dve(lambda v, tb2=tb2, tt=tt: v.tensor_copy(
                out=qT.ap().rearrange("p (c n) -> p c n", n=2048)[:, :, tt * 128:(tt + 1) * 128],
                in_=trp[tb2].ap().rearrange("p (c n) -> p c n", n=128)),
                after=[trq], signal=True)
            trp_busy[tb2] = evq
            vk = None
            for nh in range(2):
                for c in range(8):
                    vk = p.pe(lambda t, c=c, nh=nh, tt=tt: t.matmul(
                        pv[:, nh * 512:(nh + 1) * 512],
                        lnT.ap().rearrange("p (c n) -> p c n", n=2048)[:, c, tt * 128:(tt + 1) * 128],
                        wsl[1][:, c * 1024 + nh * 512: c * 1024 + nh * 512 + 512],
                        start=(c == 0), stop=(c == 7)),
                        after=[evl, pv_busy, wv_t, v_busy if tt == 0 else None]
                        if (c == 0 and nh == 0) else None,
                        signal=(c == 7 and nh == 1))
            vmm_last = vk
            evv = p.act(lambda a, tt=tt: a.activation(
                out=vsb[:, tt * 1024:(tt + 1) * 1024], in_=pv[:], func=AF.Copy),
                after=[vk], signal=True)
            pv_busy = evv
        wk_t = p.dma("pool", wsl[0].ap().rearrange("p (c n) -> p c n", n=1024),
                     w["wkT"].rearrange("(c p) n -> p c n", p=128),
                     after=[qmm_last])
        sk_done = []
        prev_nkk = None
        prev_trk2 = None
        for ct in range(8):
            kevs = []
            for chq in range(4):
                kk = None
                for c in range(8):
                    kk = p.pe(lambda t, c=c, ct=ct, chq=chq: t.matmul(
                        pk[:, :],
                        wsl[0][:, c * 1024 + ct * 128: c * 1024 + ct * 128 + 128],
                        lnT.ap().rearrange("p (c n) -> p c n", n=2048)[:, c, chq * 512:(chq + 1) * 512],
                        start=(c == 0), stop=(c == 7)),
                        after=[wk_t, pk_busy] if c == 0 else None,
                        signal=(c == 7))
                kev = p.act(lambda a, chq=chq: a.activation(
                    out=kraw[:, chq * 512:(chq + 1) * 512], in_=pk[:], func=AF.Copy),
                    after=[kk, sk_busy if (ct == 0 and chq == 0) else None,
                           qwork_busy if (ct == 0 and chq == 0) else None,
                           prev_nkk if chq == 0 else None], signal=True)
                pk_busy = kev
                kevs.append(kev)
            mxk = p.dve(lambda v: v.tensor_reduce(
                out=knm[:, 0:1], in_=kraw[:], axis=AX.X, op=ALU.max, negate=True),
                after=kevs, signal=True)
            ekk = p.act(lambda a: a.activation(
                out=kraw[:], in_=kraw[:], func=AF.Exp, bias=knm[:, 0:1],
                accum_out=knm[:, 1:2]), after=[mxk], signal=True)
            rkk = p.dve(lambda v: v.reciprocal(out=knm[:, 1:2], in_=knm[:, 1:2]),
                        after=[ekk], signal=True)
            nkk = p.dve(lambda v: v.tensor_scalar(
                out=kbf[:], in0=kraw[:], scalar1=knm[:, 1:2], scalar2=None,
                op0=ALU.mult), after=[rkk, prev_trk2], signal=True)
            prev_nkk = nkk
            for half in range(2):
                trk2 = None
                for j in range(8):
                    tt = half * 8 + j
                    trk2 = p.pe(lambda t, j=j, tt=tt, half=half: t.transpose(
                        trp[half][:, j * 128:(j + 1) * 128],
                        kbf[:, tt * 128:(tt + 1) * 128], ctx.ident_b[:]),
                        after=[nkk, trp_busy[half]] if j == 0 else None,
                        signal=(j == 7))
                evk = p.dve(lambda v, half=half, ct=ct: v.tensor_copy(
                    out=sksb.ap().rearrange("p (n c) -> p n c", c=1024)[
                        :, half * 8:(half + 1) * 8, ct * 128:(ct + 1) * 128],
                    in_=trp[half].ap().rearrange("p (n c) -> p n c", c=128)),
                    after=[trk2], signal=True)
                trp_busy[half] = evk
                sk_done.append(evk)
                prev_trk2 = trk2
        ctx_toks = []
        prev_ce = None
        for h in range(8):
            cs = h % 4
            ck = None
            for tt in range(16):
                ck = p.pe(lambda t, h=h, tt=tt, cs=cs: t.matmul(
                    pk[:, cs * 128:(cs + 1) * 128],
                    sksb[:, tt * 1024 + h * 128: tt * 1024 + (h + 1) * 128],
                    vsb[:, tt * 1024 + h * 128: tt * 1024 + (h + 1) * 128],
                    start=(tt == 0), stop=(tt == 15)),
                    after=([prev_ce, pk_busy if h == 0 else None] + sk_done +
                           [evv]) if tt == 0 else None,
                    signal=(tt == 15))
            ce = p.dve(lambda v, h=h, cs=cs: v.tensor_copy(
                out=ctxb[:, h * 128:(h + 1) * 128], in_=pk[:, cs * 128:(cs + 1) * 128]),
                after=[ck, ctxb_busy if h == 0 else None], signal=True)
            prev_ce = ce
            ctx_toks.append(ce)
        v_busy = ctx_toks[-1]
        pk_busy = ctx_toks[-1]
        o2_toks = []
        for h in range(8):
            for chq in range(4):
                ok = p.pe(lambda t, h=h, chq=chq: t.matmul(
                    pq[:, (chq % 2) * 512:((chq % 2) + 1) * 512],
                    ctxb[:, h * 128:(h + 1) * 128],
                    qT.ap().rearrange("p (c n) -> p c n", n=2048)[:, h, chq * 512:(chq + 1) * 512],
                    start=True, stop=True),
                    after=[ctx_toks[h], pq_busy if (h == 0 and chq == 0) else None,
                           o2_toks[-2] if len(o2_toks) >= 2 else None],
                    signal=True)
                oe = p.act(lambda a, h=h, chq=chq: a.activation(
                    out=lnT[:, h * 2048 + chq * 512: h * 2048 + chq * 512 + 512],
                    in_=pq[:, (chq % 2) * 512:((chq % 2) + 1) * 512], func=AF.Copy),
                    after=[ok], signal=True)
                o2_toks.append(oe)
        qT_busy = o2_toks[-1]
        pq_busy = o2_toks[-1]
        lnT_busy = None
        wo_t = p.dma("pool", wsl[1].ap().rearrange("p (c n) -> p c n", n=1024),
                     w["woT"].rearrange("(c p) n -> p c n", p=128),
                     after=[vmm_last])
        wsl_busy[0] = None
        fin_toks = []
        for tt in range(16):
            sl = tt % 2
            row0 = base + tt * 128
            x2k = p.dma("sp", xt2[:, sl * 1024:(sl + 1) * 1024],
                        x_in[row0:row0 + 128, :], after=[xt2_busy[sl]])
            fk = None
            for nh in range(2):
                for c in range(8):
                    fk = p.pe(lambda t, c=c, nh=nh, tt=tt: t.matmul(
                        pv[:, nh * 512:(nh + 1) * 512],
                        lnT.ap().rearrange("p (c n) -> p c n", n=2048)[:, c, tt * 128:(tt + 1) * 128],
                        wsl[1][:, c * 1024 + nh * 512: c * 1024 + nh * 512 + 512],
                        start=(c == 0), stop=False),
                        after=[wo_t, pv_busy, o2_toks[-1]] if (c == 0 and nh == 0) else None)
                fk = p.pe(lambda t, nh=nh: t.matmul(
                    pv[:, nh * 512:(nh + 1) * 512], ctx.ones_row[:],
                    bor[:, nh * 512:(nh + 1) * 512], start=False, stop=True),
                    signal=(nh == 1))
            ek = p.dve(lambda v, sl=sl: v.tensor_tensor(
                out=outt[:], in0=pv[:], in1=xt2[:, sl * 1024:(sl + 1) * 1024],
                op=ALU.add), after=[fk, x2k, out_busy], signal=True)
            pv_busy = ek
            xt2_busy[sl] = ek
            stok = p.dma("sp", x_out[row0:row0 + 128, :], outt[:], after=[ek])
            out_busy = stok
            fin_toks.append(stok)
        lnT_busy = fin_toks[-1]
        wsl_busy[0] = fin_toks[-1]
        wsl_busy[1] = fin_toks[-1]
        qT_busy = fin_toks[-1]
        sk_busy = ctx_toks[-1]
        ctxb_busy = o2_toks[-1]
        last = fin_toks
    return last


def build_s1_program():
    nc = bass.Bass("TRN2", target_bir_lowering=False)
    es = contextlib.ExitStack()
    x_in = nc.dram_tensor("x_in", [T, D], F32, kind="ExternalInput").ap()
    x_out = nc.dram_tensor("x_out", [T, D], F32, kind="ExternalOutput").ap()
    shapes = dict(wqT=(1024, 1024), wkT=(1024, 1024), wvT=(1024, 1024),
                  woT=(1024, 1024), bo_row=(1, 1024),
                  ident=(128, 128), ones_row=(1, 128))
    dummy = {k: np.zeros(s, np.float32) for k, s in shapes.items()}
    w = _declare(nc, dummy, S1_KEYS)
    cd = _declare(nc, dummy, ("ident", "ones_row"))
    with es:
        p = Prog()
        ctx = Ctx(nc, p, es, cd)
        toks = build_stage1(nc, p, ctx, es, x_in, x_out, w)
        p.finish(toks)
        p.emit(nc, es)
    return nc



def build_s3_program(debug=False):
    nc = bass.Bass("TRN2", target_bir_lowering=False)
    es = contextlib.ExitStack()
    x_in = nc.dram_tensor("x_in", [T, D], F32, kind="ExternalInput").ap()
    x_out = nc.dram_tensor("x_out", [T, D], F32, kind="ExternalOutput").ap()
    shapes = dict(inprojT=(1024, 4096), xp1=(2048, 16), xp2=(2048, 16), xpd=(2048, 128),
                  alog_packed=(128, 256), conv_cols=(128, 64), d_cols=(128, 16),
                  outprojT=(2048, 1024), Rrep=(16, 512), sel=(128, 512),
                  ident=(128, 128), ones_row=(1, 128))
    dummy = {k: np.zeros(s, np.float32) for k, s in shapes.items()}
    w = _declare(nc, dummy, S3_KEYS)
    cd = _declare(nc, dummy, ("ident", "ones_row"))
    with es:
        p = Prog()
        ctx = Ctx(nc, p, es, cd)
        toks = build_stage3(nc, p, ctx, es, x_in, x_out, w)
        p.finish(toks)
        p.emit(nc, es)
    return nc



S4_KEYS = ("w4", "a4", "b4", "bias4_row")


def build_stage4(nc, p, ctx, es, x_in, x_out, w):
    sb = lambda name, shape, dt: es.enter_context(nc.sbuf_tensor(name, shape, dt))
    ps = lambda name, shape: es.enter_context(nc.psum_tensor(name, shape, F32))
    psb = lambda name, shape: es.enter_context(nc.psum_tensor(name, shape, BF16))

    w4q = sb("s4_w4q", [128, 8 * 1024], BF16)
    a4sb = sb("s4_a4", [128, 8 * 16], BF16)
    b4sb = sb("s4_b4", [16, 1024], BF16)
    bias4 = sb("s4_bias4", [1, 1024], F32)
    wst = sb("s4_wst", [128, 2 * 1024], F32)
    amax = sb("s4_amax", [128, 8], F32)
    am1 = sb("s4_am1", [128, 1], F32)
    amr = sb("s4_amr", [1, 128], F32)
    sc2 = sb("s4_sc2", [1, 2], F32)
    scc = sb("s4_scc", [128, 2], F32)

    wtok = [p.dma("pool", a4sb.ap().rearrange("p (c r) -> p c r", r=16),
                  w["a4"].rearrange("(c p) r -> p c r", p=128)),
            p.dma("pool", b4sb[:], w["b4"]),
            p.dma("sp", bias4[:], w["bias4_row"])]
    fence_all(p, wtok + list(ctx.ready))

    qps = ps("s4_qps", [128, 512])
    ldq = [None, None]
    mx_toks = []
    for j in range(8):
        sl = j % 2
        lt = p.dma("sp", wst[:, sl * 1024:(sl + 1) * 1024],
                   w["w4"][j * 128:(j + 1) * 128, :], after=[ldq[sl]])
        mt = p.dve(lambda v, j=j, sl=sl: v.tensor_reduce(
            out=amax[:, j:j + 1], in_=wst[:, sl * 1024:(sl + 1) * 1024],
            axis=AX.X, op=ALU.max, apply_absolute_value=True),
            after=[lt], signal=True)
        ldq[sl] = mt
        mx_toks.append(mt)
    t1 = p.dve(lambda v: v.tensor_reduce(out=am1[:], in_=amax[:], axis=AX.X,
                                         op=ALU.max), after=mx_toks, signal=True)
    t2 = p.pe(lambda t: t.transpose(qps[0:1, 0:128], am1[:], ctx.ident_f[:]),
              after=[t1], signal=True)
    t3 = p.dve(lambda v: v.tensor_copy(out=amr[:], in_=qps[0:1, 0:128]),
               after=[t2], signal=True)
    t4 = p.dve(lambda v: v.tensor_reduce(out=sc2[:, 0:1], in_=amr[:], axis=AX.X,
                                         op=ALU.max), after=[t3], signal=True)
    t5 = p.dve(lambda v: v.tensor_scalar(out=sc2[:, 0:1], in0=sc2[:, 0:1],
                                         scalar1=1.0 / 7.0, scalar2=None,
                                         op0=ALU.mult), after=[t4], signal=True)
    t6 = p.dve(lambda v: v.reciprocal(out=sc2[:, 1:2], in_=sc2[:, 0:1]),
               after=[t5], signal=True)
    t7 = p.pe(lambda t: t.matmul(qps[:, 0:2], ctx.ones_row[:], sc2[:],
                                 start=True, stop=True), after=[t6, t2], signal=True)
    t8 = p.dve(lambda v: v.tensor_copy(out=scc[:], in_=qps[:, 0:2]),
               after=[t7], signal=True)
    ldq = [t8, t8]
    qdone = []
    for j in range(8):
        sl = j % 2
        lt = p.dma("sp", wst[:, sl * 1024:(sl + 1) * 1024],
                   w["w4"][j * 128:(j + 1) * 128, :], after=[ldq[sl], t8])
        q1 = p.dve(lambda v, j=j, sl=sl: v.tensor_scalar(
            out=wst[:, sl * 1024:(sl + 1) * 1024],
            in0=wst[:, sl * 1024:(sl + 1) * 1024],
            scalar1=scc[:, 1:2], scalar2=RC, op0=ALU.mult, op1=ALU.add),
            after=[lt, t8], signal=True)
        q2 = p.dve(lambda v, j=j, sl=sl: v.tensor_scalar(
            out=w4q[:, j * 1024:(j + 1) * 1024],
            in0=wst[:, sl * 1024:(sl + 1) * 1024],
            scalar1=RC, scalar2=scc[:, 0:1], op0=ALU.subtract, op1=ALU.mult),
            after=[q1], signal=True)
        ldq[sl] = q2
        qdone.append(q2)
    fence_all(p, qdone)

    xt = sb("s4_xt", [128, 2 * 1024], F32)
    ln = sb("s4_ln", [128, 2 * 1024], BF16)
    lnT = sb("s4_lnT", [128, 2 * 1024], BF16)
    sbf = sb("s4_sbf", [128, 16], BF16)
    sT = sb("s4_sT", [16, 2 * 128], BF16)
    outt = sb("s4_outt", [128, 1024], F32)
    lnsc = ctx.ln_scratch(2, es)
    trp = psb("s4_trp", [128, 1024])
    trs = psb("s4_trs", [16, 128])
    sps = ps("s4_sps", [128, 512])
    op_ps = ps("s4_ops", [128, 1024])

    xt_busy = [None, None]
    ln_busy = [None, None]
    lnT_busy = [None, None]
    trp_busy = None
    trs_busy = None
    sps_busy = None
    sbf_busy = None
    sT_busy = [None, None]
    op_busy = None
    out_busy = None
    last = []
    for tt in range(NT):
        sl = tt % 2
        xtok = p.dma("sp", xt[:, sl * 1024:(sl + 1) * 1024],
                     x_in[tt * 128:(tt + 1) * 128, :], after=[xt_busy[sl]])
        lnt = emit_ln(ctx, lnsc, sl, xt[:, sl * 1024:(sl + 1) * 1024],
                      ln[:, sl * 1024:(sl + 1) * 1024],
                      after=[xtok, ln_busy[sl]])
        trk = None
        for c in range(8):
            trk = p.pe(lambda t, c=c, sl=sl: t.transpose(
                trp[:, c * 128:(c + 1) * 128],
                ln[:, sl * 1024 + c * 128: sl * 1024 + (c + 1) * 128],
                ctx.ident_b[:]),
                after=[lnt, trp_busy, lnT_busy[sl]] if c == 0 else None,
                signal=(c == 7))
        ln_busy[sl] = trk
        ev = p.dve(lambda v, sl=sl: v.tensor_copy(
            out=lnT[:, sl * 1024:(sl + 1) * 1024], in_=trp[:]),
            after=[trk], signal=True)
        trp_busy = ev
        sk = None
        for c in range(8):
            sk = p.pe(lambda t, c=c, sl=sl: t.matmul(
                sps[:, 0:16], lnT[:, sl * 1024 + c * 128: sl * 1024 + (c + 1) * 128],
                a4sb[:, c * 16:(c + 1) * 16], start=(c == 0), stop=(c == 7)),
                after=[ev, sps_busy] if c == 0 else None, signal=(c == 7))
        sv = p.dve(lambda v: v.tensor_copy(out=sbf[:], in_=sps[:, 0:16]),
                   after=[sk, sbf_busy], signal=True)
        sps_busy = sv
        st = p.pe(lambda t: t.transpose(trs[:], sbf[:], ctx.ident_b[:]),
                  after=[sv, trs_busy], signal=True)
        sbf_busy = st
        sv2 = p.dve(lambda v, sl=sl: v.tensor_copy(
            out=sT[:, sl * 128:(sl + 1) * 128], in_=trs[:]),
            after=[st, sT_busy[sl]], signal=True)
        trs_busy = sv2
        mmk = None
        for nh in range(2):
            for c in range(8):
                mmk = p.pe(lambda t, c=c, nh=nh, sl=sl: t.matmul(
                    op_ps[:, nh * 512:(nh + 1) * 512],
                    lnT[:, sl * 1024 + c * 128: sl * 1024 + (c + 1) * 128],
                    w4q[:, c * 1024 + nh * 512: c * 1024 + nh * 512 + 512],
                    start=(c == 0), stop=False),
                    after=[op_busy, ev] if (c == 0 and nh == 0) else None)
            mmk = p.pe(lambda t, nh=nh, sl=sl: t.matmul(
                op_ps[:, nh * 512:(nh + 1) * 512], sT[:, sl * 128:(sl + 1) * 128],
                b4sb[:, nh * 512:(nh + 1) * 512], start=False, stop=False),
                after=[sv2])
            mmk = p.pe(lambda t, nh=nh: t.matmul(
                op_ps[:, nh * 512:(nh + 1) * 512], ctx.ones_row[:],
                bias4[:, nh * 512:(nh + 1) * 512], start=False, stop=True),
                signal=(nh == 1))
        lnT_busy[sl] = mmk
        sT_busy[sl] = mmk
        ek = p.dve(lambda v, sl=sl: v.tensor_tensor(
            out=outt[:], in0=op_ps[:], in1=xt[:, sl * 1024:(sl + 1) * 1024],
            op=ALU.add), after=[mmk, out_busy], signal=True)
        op_busy = ek
        xt_busy[sl] = ek
        stok = p.dma("sp", x_out[tt * 128:(tt + 1) * 128, :], outt[:], after=[ek])
        out_busy = stok
        last = [stok]
    return last


def build_s4_program():
    nc = bass.Bass("TRN2", target_bir_lowering=False)
    es = contextlib.ExitStack()
    x_in = nc.dram_tensor("x_in", [T, D], F32, kind="ExternalInput").ap()
    x_out = nc.dram_tensor("x_out", [T, D], F32, kind="ExternalOutput").ap()
    shapes = dict(w4=(1024, 1024), a4=(1024, 16), b4=(16, 1024), bias4_row=(1, 1024),
                  ident=(128, 128), ones_row=(1, 128))
    dummy = {k: np.zeros(s, np.float32) for k, s in shapes.items()}
    w = _declare(nc, dummy, S4_KEYS)
    cd = _declare(nc, dummy, ("ident", "ones_row"))
    with es:
        p = Prog()
        ctx = Ctx(nc, p, es, cd)
        toks = build_stage4(nc, p, ctx, es, x_in, x_out, w)
        p.finish(toks)
        p.emit(nc, es)
    return nc




S3_KEYS = ("inprojT", "xp1", "xp2", "xpd", "alog_packed", "conv_cols", "d_cols",
           "outprojT", "Rrep", "sel")


def build_stage3(nc, p, ctx, es, x_in, x_out, w, dbg=None):
    sb = lambda name, shape, dt: es.enter_context(nc.sbuf_tensor(name, shape, dt))
    ps = lambda name, shape: es.enter_context(nc.psum_tensor(name, shape, F32))
    psb = lambda name, shape: es.enter_context(nc.psum_tensor(name, shape, BF16))

    ipT = sb("s3_ipT", [128, 8 * 4096], BF16)
    opT = sb("s3_opT", [128, 16 * 1024], BF16)
    xp1 = sb("s3_xp1", [128, 16 * 16], BF16)
    xp2 = sb("s3_xp2", [128, 16 * 16], BF16)
    xpd = sb("s3_xpd", [128, 16 * 128], BF16)
    ccol = sb("s3_ccol", [128, 64], F32)
    dcol = sb("s3_dcol", [128, 16], F32)
    atab = sb("s3_atab", [128, 256], F32)
    rrep = sb("s3_rrep", [16, 512], BF16)
    sel = sb("s3_sel", [128, 512], BF16)
    wtok = []
    wtok.append(p.dma("pool", ipT.ap().rearrange("p (c n) -> p c n", n=4096),
                      w["inprojT"].rearrange("(c p) n -> p c n", p=128)))
    wtok.append(p.dma("pool", opT.ap().rearrange("p (c n) -> p c n", n=1024),
                      w["outprojT"].rearrange("(c p) n -> p c n", p=128)))
    wtok.append(p.dma("pool", xp1.ap().rearrange("p (c n) -> p c n", n=16),
                      w["xp1"].rearrange("(c p) n -> p c n", p=128)))
    wtok.append(p.dma("pool", xpd.ap().rearrange("p (c n) -> p c n", n=128),
                      w["xpd"].rearrange("(c p) n -> p c n", p=128)))
    wtok.append(p.dma("pool", xp2.ap().rearrange("p (c n) -> p c n", n=16),
                      w["xp2"].rearrange("(c p) n -> p c n", p=128)))
    wtok.append(p.dma("sp", ccol[:], w["conv_cols"]))
    wtok.append(p.dma("sp", dcol[:], w["d_cols"]))
    atab_tok = p.dma("sp", atab[:], w["alog_packed"])
    wtok.append(atab_tok)
    wtok.append(p.dma("pool", rrep[:], w["Rrep"]))
    wtok.append(p.dma("pool", sel[:], w["sel"]))
    a1 = p.act(lambda a: a.activation(out=atab[:], in_=atab[:], func=AF.Exp),
               after=[atab_tok], signal=True)
    a2 = p.dve(lambda v: v.tensor_scalar(out=atab[:], in0=atab[:], scalar1=-1.0,
                                         scalar2=None, op0=ALU.mult),
               after=[a1], signal=True)
    fence_all(p, wtok + [a2] + list(ctx.ready))

    x3T = sb("s3_x3T", [128, 8 * 512], BF16)
    xh = sb("s3_xh", [128, 16 * 516], BF16)
    ybuf = sb("s3_y", [128, 16 * 512], BF16)
    halo = sb("s3_halo", [128, 16 * 4], BF16)
    szb = sb("s3_sz", [128, 2 * 16 * 512], BF16)
    xcv = sb("s3_xcv", [128, 2 * 16 * 512], BF16)
    dgb = sb("s3_dgb", [128, 2 * 4 * 128], BF16)
    bcB = sb("s3_bcB", [32, 512], F32)
    bcC = sb("s3_bcC", [16, 512], F32)
    bcp = sb("s3_bcp", [16, 512], BF16)
    srelu = sb("s3_srelu", [128, 512], F32)
    sfp = sb("s3_sfp", [128, 512], F32)
    dbc = sb("s3_dbc", [128, 2 * 512], BF16)
    bcpk = sb("s3_bcpk", [128, 2 * 4 * 512], BF16)
    ebuf = sb("s3_E", [128, 2 * 512], BF16)
    bbuf = sb("s3_EB", [128, 2 * 512], BF16)
    t1b = sb("s3_t1", [128, 2 * 512], BF16)
    xt = sb("s3_xt", [128, 2 * 1024], F32)
    xtf = sb("s3_xtf", [128, 2 * 1024], F32)
    ln = sb("s3_ln", [128, 2 * 1024], BF16)
    outt = sb("s3_outt", [128, 1024], F32)
    lnsc = ctx.ln_scratch(2, es)

    trp = psb("s3_trp", [128, 1024])
    pIP = [ps(f"s3_pIP{i}", [128, 512]) for i in range(2)]
    pSSM = [ps(f"s3_pSSM{i}", [128, 512]) for i in range(2)]
    pXP = ps("s3_pXP", [128, 512])
    pMISC = [ps(f"s3_pMISC{i}", [128, 512]) for i in range(2)]

    B = dict(trp=None, pIP=[None, None], pSSM=[None, None], pXP=None,
             pMISC=[None, None], xt=[None, None], xtf=[None, None],
             ln=[None, None], x3T=None, xh=None, halo=None,
             sz=[None, None], xcv=[None, None], dgb=[None, None],
             bc=None, bcp=None, sfp=None, srelu=None,
             dbc=[None, None], bcpk=[None, None],
             eb=[None, None], bb=[None, None], t1=[None, None],
             y=None, out=None)
    st = {}
    last = []

    def phases_AD(chk):
        cp = chk % 2
        ci = chk % 4
        tok0 = chk * 512
        evl_toks = []
        for i in range(4):
            sl = i % 2
            row0 = tok0 + i * 128
            xtok = p.dma("sp", xt[:, sl * 1024:(sl + 1) * 1024],
                         x_in[row0:row0 + 128, :], after=[B["xt"][sl]])
            lnt = emit_ln(ctx, lnsc, sl, xt[:, sl * 1024:(sl + 1) * 1024],
                          ln[:, sl * 1024:(sl + 1) * 1024],
                          after=[xtok, B["ln"][sl]])
            B["xt"][sl] = lnt
            trk = None
            for c in range(8):
                trk = p.pe(lambda t, c=c, sl=sl: t.transpose(
                    trp[:, c * 128:(c + 1) * 128],
                    ln[:, sl * 1024 + c * 128: sl * 1024 + (c + 1) * 128],
                    ctx.ident_b[:]),
                    after=[lnt, B["trp"], B["x3T"] if (i == 0 and c == 0) else None],
                    signal=(c == 7))
            B["ln"][sl] = trk
            evl = p.dve(lambda v, i=i: v.tensor_copy(
                out=x3T.ap().rearrange("p (c n) -> p c n", n=512)[:, :, i * 128:(i + 1) * 128],
                in_=trp.ap().rearrange("p (c n) -> p c n", n=128)),
                after=[trk], signal=True)
            B["trp"] = evl
            evl_toks.append(evl)
        if ci == 0:
            hk = p.dve(lambda v: v.memset(
                xh.ap().rearrange("p (c n) -> p c n", n=516)[:, :, 1:4], 0.0),
                after=[B["xh"]], signal=True)
        else:
            hk = p.dve(lambda v: v.tensor_copy(
                out=xh.ap().rearrange("p (c n) -> p c n", n=516)[:, :, 1:4],
                in_=halo.ap().rearrange("p (c n) -> p c n", n=4)[:, :, 0:3]),
                after=[B["halo"], B["xh"]], signal=True)
        ip_toks = []
        sz_toks = []
        for o in range(32):
            b = o % 2
            mm = None
            for c in range(8):
                mm = p.pe(lambda t, o=o, c=c, b=b: t.matmul(
                    pIP[b][:, :],
                    ipT[:, c * 4096 + o * 128: c * 4096 + (o + 1) * 128],
                    x3T[:, c * 512:(c + 1) * 512],
                    start=(c == 0), stop=(c == 7)),
                    after=([B["pIP"][b]] + evl_toks) if c == 0 else None,
                    signal=(c == 7))
            if o < 16:
                ev = p.act(lambda a, o=o, b=b: a.activation(
                    out=xh[:, o * 516 + 4: o * 516 + 516], in_=pIP[b][:, :],
                    func=AF.Copy),
                    after=[mm, hk if o < 2 else None, B["xh"] if o < 2 else None],
                    signal=True)
                ip_toks.append(ev)
            else:
                ev = p.act(lambda a, o=o, b=b, cp=cp: a.activation(
                    out=szb[:, cp * 8192 + (o - 16) * 512: cp * 8192 + (o - 15) * 512],
                    in_=pIP[b][:, :], func=AF.Silu),
                    after=[mm, B["sz"][cp] if o == 16 else None], signal=True)
                sz_toks.append(ev)
            B["pIP"][b] = ev
        hs = p.dve(lambda v: v.tensor_copy(
            out=halo.ap().rearrange("p (c n) -> p c n", n=4)[:, :, 0:3],
            in_=xh.ap().rearrange("p (c n) -> p c n", n=516)[:, :, 513:516]),
            after=ip_toks, signal=True)
        B["halo"] = hs
        cv_toks = []
        for c in range(16):
            b = c % 2
            dg = None
            for j in range(4):
                dg = p.dve(lambda v, c=c, j=j, b=b: v.tensor_scalar(
                    out=dgb[:, b * 512 + j * 128: b * 512 + (j + 1) * 128],
                    in0=ctx.ident_b[:], scalar1=ccol[:, c * 4 + j: c * 4 + j + 1],
                    scalar2=None, op0=ALU.mult),
                    after=[B["dgb"][b]] if j == 0 else None, signal=(j == 3))
            mm = None
            for j in range(4):
                mm = p.pe(lambda t, c=c, j=j, b=b: t.matmul(
                    pMISC[b][:, :],
                    dgb[:, b * 512 + j * 128: b * 512 + (j + 1) * 128],
                    xh[:, c * 516 + 1 + j: c * 516 + 513 + j],
                    start=(j == 0), stop=(j == 3)),
                    after=[dg, B["pMISC"][b], ip_toks[c], hk] if j == 0 else None,
                    signal=(j == 3))
            B["dgb"][b] = mm
            ev = p.act(lambda a, c=c, b=b, cp=cp: a.activation(
                out=xcv[:, cp * 8192 + c * 512: cp * 8192 + (c + 1) * 512],
                in_=pMISC[b][:, :], func=AF.Silu),
                after=[mm, B["xcv"][cp] if c < 2 else None], signal=True)
            B["pMISC"][b] = ev
            cv_toks.append(ev)
        m1 = None
        for c in range(16):
            m1 = p.pe(lambda t, c=c, cp=cp: t.matmul(
                pXP[0:16, :], xp1[:, c * 16:(c + 1) * 16],
                xcv[:, cp * 8192 + c * 512: cp * 8192 + (c + 1) * 512],
                start=(c == 0), stop=(c == 15)),
                after=[B["pXP"], cv_toks[c]] if c == 0 else [cv_toks[c]],
                signal=(c == 15))
        e1 = p.act(lambda a: a.activation(out=bcB[0:16, :], in_=pXP[0:16, :],
                                          func=AF.Copy),
                   after=[m1, B["bc"]], signal=True)
        m2 = None
        for c in range(16):
            m2 = p.pe(lambda t, c=c, cp=cp: t.matmul(
                pXP[0:16, :], xp2[:, c * 16:(c + 1) * 16],
                xcv[:, cp * 8192 + c * 512: cp * 8192 + (c + 1) * 512],
                start=(c == 0), stop=(c == 15)),
                after=[e1] if c == 0 else None,
                signal=(c == 15))
        e2 = p.act(lambda a: a.activation(out=bcC[:], in_=pXP[0:16, :], func=AF.Copy),
                   after=[m2], signal=True)
        m3 = None
        for c in range(16):
            m3 = p.pe(lambda t, c=c, cp=cp: t.matmul(
                pXP[:, :], xpd[:, c * 128:(c + 1) * 128],
                xcv[:, cp * 8192 + c * 512: cp * 8192 + (c + 1) * 512],
                start=(c == 0), stop=(c == 15)),
                after=[e2] if c == 0 else None,
                signal=(c == 15))
        d1 = p.act(lambda a: a.activation(out=sfp[:], in_=pXP[:, :], func=AF.Abs),
                   after=[m3, B["sfp"]], signal=True)
        d2 = p.act(lambda a: a.activation(out=sfp[:], in_=sfp[:], func=AF.Exp,
                                          scale=-1.0), after=[d1], signal=True)
        d3 = p.act(lambda a: a.activation(out=sfp[:], in_=sfp[:], func=AF.Ln,
                                          bias=ctx.one_col[:]), after=[d2], signal=True)
        d4 = p.act(lambda a: a.activation(out=srelu[:], in_=pXP[:, :], func=AF.Relu),
                   after=[d3, B["srelu"]], signal=True)
        B["pXP"] = d4
        bke = p.dve(lambda v, cp=cp: v.tensor_tensor(
            out=dbc[:, cp * 512:(cp + 1) * 512], in0=sfp[:], in1=srelu[:],
            op=ALU.add), after=[d4, B["dbc"][cp]], signal=True)
        B["srelu"] = bke
        B["sfp"] = bke
        bck = p.dve(lambda v: v.tensor_tensor(out=bcp[:], in0=bcB[0:16, :],
                                              in1=bcC[:], op=ALU.mult),
                    after=[e1, e2, B["bcp"]], signal=True)
        B["bc"] = bck
        bcpk_toks = []
        bcpk_mms = []
        for gidx in range(4):
            b = gidx % 2
            mk = p.pe(lambda t, gidx=gidx, b=b: t.matmul(
                pMISC[b][:, :], rrep[:, gidx * 128:(gidx + 1) * 128], bcp[:],
                start=True, stop=True),
                after=[bck, B["pMISC"][b]], signal=True)
            bcpk_mms.append(mk)
            ek = p.dve(lambda v, gidx=gidx, b=b, cp=cp: v.tensor_copy(
                out=bcpk[:, cp * 2048 + gidx * 512: cp * 2048 + (gidx + 1) * 512],
                in_=pMISC[b][:, :]),
                after=[mk, B["bcpk"][cp] if gidx < 2 else None], signal=True)
            B["pMISC"][b] = ek
            bcpk_toks.append(ek)
        B["bcp"] = bcpk_mms[-1]
        st[chk] = dict(cp=cp, tok0=tok0, bke=bke, bcpk_toks=bcpk_toks,
                       sz_toks=sz_toks, hs=hs, cv_last=cv_toks[-1])

    def phases_EF(chk):
        s = st.pop(chk)
        cp, tok0, bke = s["cp"], s["tok0"], s["bke"]
        bcpk_toks, sz_toks, hs = s["bcpk_toks"], s["sz_toks"], s["hs"]
        st_toks = []
        exp_last = None
        for it in range(16):
            b = it % 2
            scrub = p.dve(lambda v, b=b: v.memset(pSSM[b][:, :], 0.0),
                          after=[B["pSSM"][b]], signal=True)
            mm_last = None
            for lc in range(4):
                for gidx in range(4):
                    es_ = (it * 16 + lc * 4 + gidx) % 2
                    col = (it * 4 + lc) * 4 + gidx
                    ex = p.act(lambda a, col=col, es_=es_, cp=cp: a.activation(
                        out=ebuf[:, es_ * 512:(es_ + 1) * 512],
                        in_=dbc[:, cp * 512:(cp + 1) * 512],
                        func=AF.Exp, scale=atab[:, col:col + 1]),
                        after=[bke, B["eb"][es_]], signal=True)
                    exp_last = ex
                    eb = p.dve(lambda v, gidx=gidx, es_=es_, cp=cp: v.tensor_tensor(
                        out=bbuf[:, es_ * 512:(es_ + 1) * 512],
                        in0=ebuf[:, es_ * 512:(es_ + 1) * 512],
                        in1=bcpk[:, cp * 2048 + gidx * 512: cp * 2048 + (gidx + 1) * 512],
                        op=ALU.mult),
                        after=[ex, bcpk_toks[gidx], B["bb"][es_]], signal=True)
                    B["eb"][es_] = eb
                    mm = p.pe(lambda t, lc=lc, es_=es_, b=b: t.matmul(
                        pSSM[b][:, :], sel[:, lc * 128:(lc + 1) * 128],
                        bbuf[:, es_ * 512:(es_ + 1) * 512],
                        start=(lc == 0 and gidx == 0), stop=(lc == 3 and gidx == 3)),
                        after=[eb, scrub if (lc == 0 and gidx == 0) else None],
                        signal=True)
                    B["bb"][es_] = mm
                    mm_last = mm
            tb = it % 2
            s1 = p.dve(lambda v, it=it, b=b, tb=tb, cp=cp: v.scalar_tensor_tensor(
                out=t1b[:, tb * 512:(tb + 1) * 512], in0=pSSM[b][:, :],
                scalar=dcol[:, it:it + 1],
                in1=xcv[:, cp * 8192 + it * 512: cp * 8192 + (it + 1) * 512],
                op0=ALU.add, op1=ALU.mult),
                after=[mm_last, B["t1"][tb]], signal=True)
            B["pSSM"][b] = s1
            yk = p.dve(lambda v, it=it, tb=tb, cp=cp: v.tensor_tensor(
                out=ybuf[:, it * 512:(it + 1) * 512],
                in0=t1b[:, tb * 512:(tb + 1) * 512],
                in1=szb[:, cp * 8192 + it * 512: cp * 8192 + (it + 1) * 512],
                op=ALU.mult),
                after=[s1, sz_toks[it], B["y"] if it == 0 else None], signal=True)
            B["t1"][tb] = yk
            st_toks.append(yk)
        B["dbc"][cp] = exp_last
        B["sz"][cp] = st_toks[-1]
        B["xcv"][cp] = st_toks[-1]
        B["bcpk"][cp] = st_toks[-1]
        B["xh"] = hs
        f_toks = []
        fk_last = None
        for sub in range(4):
            sl = sub % 2
            row0 = tok0 + sub * 128
            x2k = p.dma("sp", xtf[:, sl * 1024:(sl + 1) * 1024],
                        x_in[row0:row0 + 128, :], after=[B["xtf"][sl]])
            eks = []
            for nh in range(2):
                fk = None
                for i in range(16):
                    fk = p.pe(lambda t, i=i, nh=nh, sub=sub: t.matmul(
                        pMISC[nh][:, :],
                        ybuf[:, i * 512 + sub * 128: i * 512 + (sub + 1) * 128],
                        opT[:, i * 1024 + nh * 512: i * 1024 + nh * 512 + 512],
                        start=(i == 0), stop=(i == 15)),
                        after=([B["pMISC"][nh]] + (st_toks if sub == 0 else []))
                        if i == 0 else None,
                        signal=(i == 15))
                ekn = p.dve(lambda v, sl=sl, nh=nh: v.tensor_tensor(
                    out=outt[:, nh * 512:(nh + 1) * 512], in0=pMISC[nh][:, :],
                    in1=xtf[:, sl * 1024 + nh * 512: sl * 1024 + nh * 512 + 512],
                    op=ALU.add),
                    after=[fk, x2k, B["out"] if nh == 0 else None], signal=True)
                B["pMISC"][nh] = ekn
                eks.append(ekn)
                fk_last = fk
            B["xtf"][sl] = eks[-1]
            stok = p.dma("sp", x_out[row0:row0 + 128, :], outt[:, 0:1024],
                         after=eks)
            B["out"] = stok
            f_toks.append(stok)
        B["y"] = fk_last
        return f_toks

    for chk in range(17):
        if chk < 16:
            phases_AD(chk)
        if chk >= 1:
            last = phases_EF(chk - 1)
    return last


def build_s3_program(debug=False):
    nc = bass.Bass("TRN2", target_bir_lowering=False)
    es = contextlib.ExitStack()
    x_in = nc.dram_tensor("x_in", [T, D], F32, kind="ExternalInput").ap()
    x_out = nc.dram_tensor("x_out", [T, D], F32, kind="ExternalOutput").ap()
    dbg = None
    if debug:
        dd = lambda n, s: nc.dram_tensor(n, list(s), F32, kind="ExternalOutput").ap()
        dbg = dict(xh=dd("d_xh", (128, 16 * 516)), sz=dd("d_sz", (128, 16 * 512)),
                   xcv=dd("d_xcv", (128, 16 * 512)), bcB=dd("d_bcB", (16, 512)),
                   bcC=dd("d_bcC", (16, 512)), ddel=dd("d_ddel", (1, 512)),
                   dbc=dd("d_dbc", (128, 512)), bcpk=dd("d_bcpk", (128, 4 * 512)),
                   y=dd("d_y", (128, 16 * 512)), ebuf=dd("d_ebuf", (128, 1024)),
                   bbuf=dd("d_bbuf", (128, 1024)), t1=dd("d_t1", (128, 512)),
                   sdump=dd("d_sdump", (128, 512)), atab=dd("d_atab", (128, 256)))
        if debug == "override":
            dbg["ov_dbc"] = nc.dram_tensor("ov_dbc", [128, 512], F32, kind="ExternalInput").ap()
            dbg["ov_bcpk"] = nc.dram_tensor("ov_bcpk", [128, 4 * 512], F32, kind="ExternalInput").ap()
    shapes = dict(inprojT=(1024, 4096), xp1=(2048, 16), xp2=(2048, 16), xpd=(2048, 128),
                  alog_packed=(128, 256), conv_cols=(128, 64), d_cols=(128, 16),
                  outprojT=(2048, 1024), Rrep=(16, 512), sel=(128, 512),
                  ident=(128, 128), ones_row=(1, 128))
    dummy = {k: np.zeros(s, np.float32) for k, s in shapes.items()}
    w = _declare(nc, dummy, S3_KEYS)
    cd = _declare(nc, dummy, ("ident", "ones_row"))
    with es:
        p = Prog()
        ctx = Ctx(nc, p, es, cd)
        toks = build_stage3(nc, p, ctx, es, x_in, x_out, w, dbg=dbg)
        p.finish(toks)
        p.emit(nc, es)
    return nc



S4_KEYS = ("w4", "a4", "b4", "bias4_row")


def build_stage4(nc, p, ctx, es, x_in, x_out, w):
    sb = lambda name, shape, dt: es.enter_context(nc.sbuf_tensor(name, shape, dt))
    ps = lambda name, shape: es.enter_context(nc.psum_tensor(name, shape, F32))
    psb = lambda name, shape: es.enter_context(nc.psum_tensor(name, shape, BF16))

    w4q = sb("s4_w4q", [128, 8 * 1024], BF16)
    a4sb = sb("s4_a4", [128, 8 * 16], BF16)
    b4sb = sb("s4_b4", [16, 1024], BF16)
    bias4 = sb("s4_bias4", [1, 1024], F32)
    wst = sb("s4_wst", [128, 2 * 1024], F32)
    amax = sb("s4_amax", [128, 8], F32)
    am1 = sb("s4_am1", [128, 1], F32)
    amr = sb("s4_amr", [1, 128], F32)
    sc2 = sb("s4_sc2", [1, 2], F32)
    scc = sb("s4_scc", [128, 2], F32)

    wtok = [p.dma("pool", a4sb.ap().rearrange("p (c r) -> p c r", r=16),
                  w["a4"].rearrange("(c p) r -> p c r", p=128)),
            p.dma("pool", b4sb[:], w["b4"]),
            p.dma("sp", bias4[:], w["bias4_row"])]
    fence_all(p, wtok + list(ctx.ready))

    qps = ps("s4_qps", [128, 512])
    ldq = [None, None]
    mx_toks = []
    for j in range(8):
        sl = j % 2
        lt = p.dma("sp", wst[:, sl * 1024:(sl + 1) * 1024],
                   w["w4"][j * 128:(j + 1) * 128, :], after=[ldq[sl]])
        mt = p.dve(lambda v, j=j, sl=sl: v.tensor_reduce(
            out=amax[:, j:j + 1], in_=wst[:, sl * 1024:(sl + 1) * 1024],
            axis=AX.X, op=ALU.max, apply_absolute_value=True),
            after=[lt], signal=True)
        ldq[sl] = mt
        mx_toks.append(mt)
    t1 = p.dve(lambda v: v.tensor_reduce(out=am1[:], in_=amax[:], axis=AX.X,
                                         op=ALU.max), after=mx_toks, signal=True)
    t2 = p.pe(lambda t: t.transpose(qps[0:1, 0:128], am1[:], ctx.ident_f[:]),
              after=[t1], signal=True)
    t3 = p.dve(lambda v: v.tensor_copy(out=amr[:], in_=qps[0:1, 0:128]),
               after=[t2], signal=True)
    t4 = p.dve(lambda v: v.tensor_reduce(out=sc2[:, 0:1], in_=amr[:], axis=AX.X,
                                         op=ALU.max), after=[t3], signal=True)
    t5 = p.dve(lambda v: v.tensor_scalar(out=sc2[:, 0:1], in0=sc2[:, 0:1],
                                         scalar1=1.0 / 7.0, scalar2=None,
                                         op0=ALU.mult), after=[t4], signal=True)
    t6 = p.dve(lambda v: v.reciprocal(out=sc2[:, 1:2], in_=sc2[:, 0:1]),
               after=[t5], signal=True)
    t7 = p.pe(lambda t: t.matmul(qps[:, 0:2], ctx.ones_row[:], sc2[:],
                                 start=True, stop=True), after=[t6, t2], signal=True)
    t8 = p.dve(lambda v: v.tensor_copy(out=scc[:], in_=qps[:, 0:2]),
               after=[t7], signal=True)
    ldq = [t8, t8]
    qdone = []
    for j in range(8):
        sl = j % 2
        lt = p.dma("sp", wst[:, sl * 1024:(sl + 1) * 1024],
                   w["w4"][j * 128:(j + 1) * 128, :], after=[ldq[sl], t8])
        q1 = p.dve(lambda v, j=j, sl=sl: v.tensor_scalar(
            out=wst[:, sl * 1024:(sl + 1) * 1024],
            in0=wst[:, sl * 1024:(sl + 1) * 1024],
            scalar1=scc[:, 1:2], scalar2=RC, op0=ALU.mult, op1=ALU.add),
            after=[lt, t8], signal=True)
        q2 = p.dve(lambda v, j=j, sl=sl: v.tensor_scalar(
            out=w4q[:, j * 1024:(j + 1) * 1024],
            in0=wst[:, sl * 1024:(sl + 1) * 1024],
            scalar1=RC, scalar2=scc[:, 0:1], op0=ALU.subtract, op1=ALU.mult),
            after=[q1], signal=True)
        ldq[sl] = q2
        qdone.append(q2)
    fence_all(p, qdone)

    xt = sb("s4_xt", [128, 2 * 1024], F32)
    ln = sb("s4_ln", [128, 2 * 1024], BF16)
    lnT = sb("s4_lnT", [128, 2 * 1024], BF16)
    sbf = sb("s4_sbf", [128, 16], BF16)
    sT = sb("s4_sT", [16, 2 * 128], BF16)
    outt = sb("s4_outt", [128, 1024], F32)
    lnsc = ctx.ln_scratch(2, es)
    trp = psb("s4_trp", [128, 1024])
    trs = psb("s4_trs", [16, 128])
    sps = ps("s4_sps", [128, 512])
    op_ps = ps("s4_ops", [128, 1024])

    xt_busy = [None, None]
    ln_busy = [None, None]
    lnT_busy = [None, None]
    trp_busy = None
    trs_busy = None
    sps_busy = None
    sbf_busy = None
    sT_busy = [None, None]
    op_busy = None
    out_busy = None
    last = []
    for tt in range(NT):
        sl = tt % 2
        xtok = p.dma("sp", xt[:, sl * 1024:(sl + 1) * 1024],
                     x_in[tt * 128:(tt + 1) * 128, :], after=[xt_busy[sl]])
        lnt = emit_ln(ctx, lnsc, sl, xt[:, sl * 1024:(sl + 1) * 1024],
                      ln[:, sl * 1024:(sl + 1) * 1024],
                      after=[xtok, ln_busy[sl]])
        trk = None
        for c in range(8):
            trk = p.pe(lambda t, c=c, sl=sl: t.transpose(
                trp[:, c * 128:(c + 1) * 128],
                ln[:, sl * 1024 + c * 128: sl * 1024 + (c + 1) * 128],
                ctx.ident_b[:]),
                after=[lnt, trp_busy, lnT_busy[sl]] if c == 0 else None,
                signal=(c == 7))
        ln_busy[sl] = trk
        ev = p.dve(lambda v, sl=sl: v.tensor_copy(
            out=lnT[:, sl * 1024:(sl + 1) * 1024], in_=trp[:]),
            after=[trk], signal=True)
        trp_busy = ev
        sk = None
        for c in range(8):
            sk = p.pe(lambda t, c=c, sl=sl: t.matmul(
                sps[:, 0:16], lnT[:, sl * 1024 + c * 128: sl * 1024 + (c + 1) * 128],
                a4sb[:, c * 16:(c + 1) * 16], start=(c == 0), stop=(c == 7)),
                after=[ev, sps_busy] if c == 0 else None, signal=(c == 7))
        sv = p.dve(lambda v: v.tensor_copy(out=sbf[:], in_=sps[:, 0:16]),
                   after=[sk, sbf_busy], signal=True)
        sps_busy = sv
        st = p.pe(lambda t: t.transpose(trs[:], sbf[:], ctx.ident_b[:]),
                  after=[sv, trs_busy], signal=True)
        sbf_busy = st
        sv2 = p.dve(lambda v, sl=sl: v.tensor_copy(
            out=sT[:, sl * 128:(sl + 1) * 128], in_=trs[:]),
            after=[st, sT_busy[sl]], signal=True)
        trs_busy = sv2
        mmk = None
        for nh in range(2):
            for c in range(8):
                mmk = p.pe(lambda t, c=c, nh=nh, sl=sl: t.matmul(
                    op_ps[:, nh * 512:(nh + 1) * 512],
                    lnT[:, sl * 1024 + c * 128: sl * 1024 + (c + 1) * 128],
                    w4q[:, c * 1024 + nh * 512: c * 1024 + nh * 512 + 512],
                    start=(c == 0), stop=False),
                    after=[op_busy, ev] if (c == 0 and nh == 0) else None)
            mmk = p.pe(lambda t, nh=nh, sl=sl: t.matmul(
                op_ps[:, nh * 512:(nh + 1) * 512], sT[:, sl * 128:(sl + 1) * 128],
                b4sb[:, nh * 512:(nh + 1) * 512], start=False, stop=False),
                after=[sv2])
            mmk = p.pe(lambda t, nh=nh: t.matmul(
                op_ps[:, nh * 512:(nh + 1) * 512], ctx.ones_row[:],
                bias4[:, nh * 512:(nh + 1) * 512], start=False, stop=True),
                signal=(nh == 1))
        lnT_busy[sl] = mmk
        sT_busy[sl] = mmk
        ek = p.dve(lambda v, sl=sl: v.tensor_tensor(
            out=outt[:], in0=op_ps[:], in1=xt[:, sl * 1024:(sl + 1) * 1024],
            op=ALU.add), after=[mmk, out_busy], signal=True)
        op_busy = ek
        xt_busy[sl] = ek
        stok = p.dma("sp", x_out[tt * 128:(tt + 1) * 128, :], outt[:], after=[ek])
        out_busy = stok
        last = [stok]
    return last


def build_s4_program():
    nc = bass.Bass("TRN2", target_bir_lowering=False)
    es = contextlib.ExitStack()
    x_in = nc.dram_tensor("x_in", [T, D], F32, kind="ExternalInput").ap()
    x_out = nc.dram_tensor("x_out", [T, D], F32, kind="ExternalOutput").ap()
    shapes = dict(w4=(1024, 1024), a4=(1024, 16), b4=(16, 1024), bias4_row=(1, 1024),
                  ident=(128, 128), ones_row=(1, 128))
    dummy = {k: np.zeros(s, np.float32) for k, s in shapes.items()}
    w = _declare(nc, dummy, S4_KEYS)
    cd = _declare(nc, dummy, ("ident", "ones_row"))
    with es:
        p = Prog()
        ctx = Ctx(nc, p, es, cd)
        toks = build_stage4(nc, p, ctx, es, x_in, x_out, w)
        p.finish(toks)
        p.emit(nc, es)
    return nc




S3_KEYS = ("inprojT", "xp1", "xp2", "xpd", "alog_packed", "conv_cols", "d_cols",
           "outprojT", "Rrep", "sel")


def build_stage3(nc, p, ctx, es, x_in, x_out, w, dbg=None):
    sb = lambda name, shape, dt: es.enter_context(nc.sbuf_tensor(name, shape, dt))
    ps = lambda name, shape: es.enter_context(nc.psum_tensor(name, shape, F32))
    psb = lambda name, shape: es.enter_context(nc.psum_tensor(name, shape, BF16))

    ipT = sb("s3_ipT", [128, 8 * 4096], BF16)
    opT = sb("s3_opT", [128, 16 * 1024], BF16)
    xp1 = sb("s3_xp1", [128, 16 * 16], BF16)
    xp2 = sb("s3_xp2", [128, 16 * 16], BF16)
    xpd = sb("s3_xpd", [128, 16 * 128], BF16)
    ccol = sb("s3_ccol", [128, 64], F32)
    dcol = sb("s3_dcol", [128, 16], F32)
    atab = sb("s3_atab", [128, 256], F32)
    rrep = sb("s3_rrep", [16, 512], BF16)
    sel = sb("s3_sel", [128, 512], BF16)
    wtok = []
    wtok.append(p.dma("pool", ipT.ap().rearrange("p (c n) -> p c n", n=4096),
                      w["inprojT"].rearrange("(c p) n -> p c n", p=128)))
    wtok.append(p.dma("pool", opT.ap().rearrange("p (c n) -> p c n", n=1024),
                      w["outprojT"].rearrange("(c p) n -> p c n", p=128)))
    wtok.append(p.dma("pool", xp1.ap().rearrange("p (c n) -> p c n", n=16),
                      w["xp1"].rearrange("(c p) n -> p c n", p=128)))
    wtok.append(p.dma("pool", xpd.ap().rearrange("p (c n) -> p c n", n=128),
                      w["xpd"].rearrange("(c p) n -> p c n", p=128)))
    wtok.append(p.dma("pool", xp2.ap().rearrange("p (c n) -> p c n", n=16),
                      w["xp2"].rearrange("(c p) n -> p c n", p=128)))
    wtok.append(p.dma("sp", ccol[:], w["conv_cols"]))
    wtok.append(p.dma("sp", dcol[:], w["d_cols"]))
    atab_tok = p.dma("sp", atab[:], w["alog_packed"])
    wtok.append(atab_tok)
    wtok.append(p.dma("pool", rrep[:], w["Rrep"]))
    wtok.append(p.dma("pool", sel[:], w["sel"]))
    a1 = p.act(lambda a: a.activation(out=atab[:], in_=atab[:], func=AF.Exp),
               after=[atab_tok], signal=True)
    a2 = p.dve(lambda v: v.tensor_scalar(out=atab[:], in0=atab[:], scalar1=-1.0,
                                         scalar2=None, op0=ALU.mult),
               after=[a1], signal=True)
    fence_all(p, wtok + [a2] + list(ctx.ready))

    x3T = sb("s3_x3T", [128, 8 * 512], BF16)
    xh = sb("s3_xh", [128, 16 * 516], BF16)
    halo = sb("s3_halo", [128, 16 * 4], BF16)
    szb = sb("s3_sz", [128, 16 * 512], BF16)
    xcv = sb("s3_xcv", [128, 16 * 512], BF16)
    dgb = sb("s3_dgb", [128, 2 * 4 * 128], BF16)
    bcB = sb("s3_bcB", [32, 512], F32)
    bcC = sb("s3_bcC", [16, 512], F32)
    bcp = sb("s3_bcp", [16, 512], BF16)
    srelu = sb("s3_srelu", [128, 512], F32)
    sfp = sb("s3_sfp", [128, 512], F32)
    dbc = sb("s3_dbc", [128, 512], BF16)
    bcpk = sb("s3_bcpk", [128, 4 * 512], BF16)
    ebuf = sb("s3_E", [128, 2 * 512], BF16)
    bbuf = sb("s3_EB", [128, 2 * 512], BF16)
    t1b = sb("s3_t1", [128, 2 * 512], BF16)
    sdump = sb("s3_sdump", [128, 512], F32) if dbg is not None else None
    xt = sb("s3_xt", [128, 2 * 1024], F32)
    ln = sb("s3_ln", [128, 2 * 1024], BF16)
    outt = sb("s3_outt", [128, 1024], F32)
    lnsc = ctx.ln_scratch(2, es)

    trp = [psb(f"s3_trp{i}", [128, 1024]) for i in range(2)]
    psA = [ps(f"s3_psA{i}", [128, 512]) for i in range(2)]
    psB = [ps(f"s3_psB{i}", [128, 512]) for i in range(2)]
    pC = ps("s3_pC", [32, 512])
    pD = ps("s3_pD", [128, 512])

    trp_busy = [None, None]
    psA_busy = [None, None]
    psB_busy = [None, None]
    pC_busy = None
    pD_busy = None
    xt_busy = [None, None]
    ln_busy = [None, None]
    x3T_busy = None
    xh_busy = None
    halo_busy = None
    sz_busy = None
    xcv_busy = None
    dgb_busy = [None, None]
    bc_busy = None
    bcp_busy = None
    dbc_busy = None
    srelu_busy = None
    sfp_busy = None
    bcpk_busy = None
    eb_busy = [None, None]
    bb_busy = [None, None]
    t1_busy = [None, None]
    out_busy = None
    y_busy = None
    last = []

    for chk in range(16):
        bi = chk // 4
        ci = chk % 4
        tok0 = chk * 512
        evl_toks = []
        for i in range(4):
            sl = i % 2
            row0 = tok0 + i * 128
            xtok = p.dma("sp", xt[:, sl * 1024:(sl + 1) * 1024],
                         x_in[row0:row0 + 128, :], after=[xt_busy[sl]])
            lnt = emit_ln(ctx, lnsc, sl, xt[:, sl * 1024:(sl + 1) * 1024],
                          ln[:, sl * 1024:(sl + 1) * 1024],
                          after=[xtok, ln_busy[sl]])
            trk = None
            for c in range(8):
                trk = p.pe(lambda t, c=c, sl=sl: t.transpose(
                    trp[sl][:, c * 128:(c + 1) * 128],
                    ln[:, sl * 1024 + c * 128: sl * 1024 + (c + 1) * 128],
                    ctx.ident_b[:]),
                    after=[lnt, trp_busy[sl], x3T_busy if (i == 0 and c == 0) else None],
                    signal=(c == 7))
            ln_busy[sl] = trk
            xt_busy[sl] = lnt
            evl = p.dve(lambda v, sl=sl, i=i: v.tensor_copy(
                out=x3T.ap().rearrange("p (c n) -> p c n", n=512)[:, :, i * 128:(i + 1) * 128],
                in_=trp[sl].ap().rearrange("p (c n) -> p c n", n=128)),
                after=[trk], signal=True)
            trp_busy[sl] = evl
            evl_toks.append(evl)
        if ci == 0:
            hk = p.dve(lambda v: v.memset(
                xh.ap().rearrange("p (c n) -> p c n", n=516)[:, :, 1:4], 0.0),
                after=[xh_busy, y_busy], signal=True)
        else:
            hk = p.dve(lambda v: v.tensor_copy(
                out=xh.ap().rearrange("p (c n) -> p c n", n=516)[:, :, 1:4],
                in_=halo.ap().rearrange("p (c n) -> p c n", n=4)[:, :, 0:3]),
                after=[halo_busy, xh_busy, y_busy], signal=True)
        ip_toks = []
        sz_toks = []
        for o in range(32):
            b = o % 2
            mm = None
            for c in range(8):
                mm = p.pe(lambda t, o=o, c=c, b=b: t.matmul(
                    psA[b][:, :],
                    ipT[:, c * 4096 + o * 128: c * 4096 + (o + 1) * 128],
                    x3T[:, c * 512:(c + 1) * 512],
                    start=(c == 0), stop=(c == 7)),
                    after=([psA_busy[b]] + evl_toks) if c == 0 else None,
                    signal=(c == 7))
            if o < 16:
                ev = p.act(lambda a, o=o, b=b: a.activation(
                    out=xh[:, o * 516 + 4: o * 516 + 516], in_=psA[b][:, :],
                    func=AF.Copy), after=[mm, hk if o < 2 else None], signal=True)
                ip_toks.append(ev)
            else:
                ev = p.act(lambda a, o=o, b=b: a.activation(
                    out=szb[:, (o - 16) * 512:(o - 15) * 512], in_=psA[b][:, :],
                    func=AF.Silu), after=[mm, sz_busy if o == 16 else None],
                    signal=True)
                sz_toks.append(ev)
            psA_busy[b] = ev
        hs = p.dve(lambda v: v.tensor_copy(
            out=halo.ap().rearrange("p (c n) -> p c n", n=4)[:, :, 0:3],
            in_=xh.ap().rearrange("p (c n) -> p c n", n=516)[:, :, 513:516]),
            after=ip_toks, signal=True)
        halo_busy = hs
        cv_toks = []
        for c in range(16):
            b = c % 2
            dg = None
            for j in range(4):
                dg = p.dve(lambda v, c=c, j=j, b=b: v.tensor_scalar(
                    out=dgb[:, b * 512 + j * 128: b * 512 + (j + 1) * 128],
                    in0=ctx.ident_b[:], scalar1=ccol[:, c * 4 + j: c * 4 + j + 1],
                    scalar2=None, op0=ALU.mult),
                    after=[dgb_busy[b]] if j == 0 else None, signal=(j == 3))
            mm = None
            for j in range(4):
                mm = p.pe(lambda t, c=c, j=j, b=b: t.matmul(
                    psB[b][:, :],
                    dgb[:, b * 512 + j * 128: b * 512 + (j + 1) * 128],
                    xh[:, c * 516 + 1 + j: c * 516 + 513 + j],
                    start=(j == 0), stop=(j == 3)),
                    after=[dg, psB_busy[b], ip_toks[c], hk] if j == 0 else None,
                    signal=(j == 3))
            dgb_busy[b] = mm
            ev = p.act(lambda a, c=c, b=b: a.activation(
                out=xcv[:, c * 512:(c + 1) * 512], in_=psB[b][:, :], func=AF.Silu),
                after=[mm, xcv_busy if c < 2 else None], signal=True)
            psB_busy[b] = ev
            cv_toks.append(ev)
        if dbg is not None and chk == 0:
            p.dma("pool", dbg["xh"], xh[:], after=ip_toks + [hk])
            p.dma("pool", dbg["sz"], szb[:], after=sz_toks)
            p.dma("pool", dbg["xcv"], xcv[:], after=cv_toks)
        m1 = None
        for c in range(16):
            m1 = p.pe(lambda t, c=c: t.matmul(
                pC[0:16, :], xp1[:, c * 16:(c + 1) * 16], xcv[:, c * 512:(c + 1) * 512],
                start=(c == 0), stop=(c == 15)),
                after=[pC_busy, cv_toks[c]] if c == 0 else [cv_toks[c]],
                signal=(c == 15))
        m2 = None
        for c in range(16):
            m2 = p.pe(lambda t, c=c: t.matmul(
                pD[0:16, :], xp2[:, c * 16:(c + 1) * 16], xcv[:, c * 512:(c + 1) * 512],
                start=(c == 0), stop=(c == 15)),
                after=[pD_busy] if c == 0 else None,
                signal=(c == 15))
        e1 = p.act(lambda a: a.activation(out=bcB[0:16, :], in_=pC[0:16, :], func=AF.Copy),
                   after=[m1, bc_busy], signal=True)
        e2 = p.act(lambda a: a.activation(out=bcC[:], in_=pD[0:16, :], func=AF.Copy),
                   after=[m2, bc_busy], signal=True)
        pC_busy = e1
        m3 = None
        for c in range(16):
            m3 = p.pe(lambda t, c=c: t.matmul(
                pD[:, :], xpd[:, c * 128:(c + 1) * 128], xcv[:, c * 512:(c + 1) * 512],
                start=(c == 0), stop=(c == 15)),
                after=[e2] if c == 0 else None,
                signal=(c == 15))
        d1 = p.act(lambda a: a.activation(out=sfp[:], in_=pD[:, :], func=AF.Abs),
                   after=[m3, sfp_busy], signal=True)
        d2 = p.act(lambda a: a.activation(out=sfp[:], in_=sfp[:], func=AF.Exp,
                                          scale=-1.0), after=[d1], signal=True)
        d3 = p.act(lambda a: a.activation(out=sfp[:], in_=sfp[:], func=AF.Ln,
                                          bias=ctx.one_col[:]),
                   after=[d2], signal=True)
        d4 = p.act(lambda a: a.activation(out=srelu[:], in_=pD[:, :], func=AF.Relu),
                   after=[d3, srelu_busy], signal=True)
        pD_busy = d4
        bke = p.dve(lambda v: v.tensor_tensor(out=dbc[:], in0=sfp[:], in1=srelu[:],
                                              op=ALU.add), after=[d4, dbc_busy], signal=True)
        srelu_busy = bke
        sfp_busy = bke
        bck = p.dve(lambda v: v.tensor_tensor(out=bcp[:], in0=bcB[0:16, :],
                                              in1=bcC[:], op=ALU.mult),
                    after=[e1, e2, bcp_busy], signal=True)
        bc_busy = bck
        bcpk_toks = []
        bcpk_mms = []
        for gidx in range(4):
            b = gidx % 2
            mk = p.pe(lambda t, gidx=gidx, b=b: t.matmul(
                psB[b][:, :], rrep[:, gidx * 128:(gidx + 1) * 128], bcp[:],
                start=True, stop=True),
                after=[bck, psB_busy[b]], signal=True)
            bcpk_mms.append(mk)
            ek = p.dve(lambda v, gidx=gidx, b=b: v.tensor_copy(
                out=bcpk[:, gidx * 512:(gidx + 1) * 512], in_=psB[b][:, :]),
                after=[mk, bcpk_busy if gidx < 2 else None], signal=True)
            psB_busy[b] = ek
            bcpk_toks.append(ek)
        bcp_busy = bcpk_mms[-1]
        if dbg is not None and chk == 0:
            p.dma("sp", dbg["atab"], atab[:], after=[a2])
            p.dma("sp", dbg["bcB"], bcB[0:16, :], after=[bck])
            p.dma("sp", dbg["bcC"], bcC[:], after=[bck])
            p.dma("pool", dbg["ddel"], dbc[0:1, :], after=[bke, ("e.dve", -1)])
            p.dma("pool", dbg["dbc"], dbc[:], after=[bke])
            p.dma("pool", dbg["bcpk"], bcpk[:], after=bcpk_toks)
        ov1 = ov2 = None
        if dbg is not None and chk == 0 and "ov_dbc" in dbg:
            ov1 = p.dma("sp", dbc[:], dbg["ov_dbc"], after=[bke])
            ov2 = p.dma("pool", bcpk[:], dbg["ov_bcpk"], after=bcpk_toks)
        st_toks = []
        exp_last = None
        for it in range(16):
            b = it % 2
            scrub = p.dve(lambda v, b=b: v.memset(psA[b][:, :], 0.0),
                          after=[psA_busy[b]], signal=True)
            psA_busy[b] = scrub
            mm_last = None
            for lc in range(4):
                for gidx in range(4):
                    es_ = (it * 16 + lc * 4 + gidx) % 2
                    col = (it * 4 + lc) * 4 + gidx
                    ex = p.act(lambda a, col=col, es_=es_: a.activation(
                        out=ebuf[:, es_ * 512:(es_ + 1) * 512], in_=dbc[:],
                        func=AF.Exp, scale=atab[:, col:col + 1]),
                        after=[bke, ov1, eb_busy[es_],
                               mm_last if SERIAL_E else None], signal=True)
                    exp_last = ex
                    eb = p.dve(lambda v, gidx=gidx, es_=es_: v.tensor_tensor(
                        out=bbuf[:, es_ * 512:(es_ + 1) * 512],
                        in0=ebuf[:, es_ * 512:(es_ + 1) * 512],
                        in1=bcpk[:, gidx * 512:(gidx + 1) * 512], op=ALU.mult),
                        after=[ex, bcpk_toks[gidx], ov2, bb_busy[es_]], signal=True)
                    eb_busy[es_] = eb
                    mm = p.pe(lambda t, lc=lc, es_=es_, b=b: t.matmul(
                        psA[b][:, :], sel[:, lc * 128:(lc + 1) * 128],
                        bbuf[:, es_ * 512:(es_ + 1) * 512],
                        start=(lc == 0 and gidx == 0), stop=(lc == 3 and gidx == 3)),
                        after=[eb, psA_busy[b] if (lc == 0 and gidx == 0) else None],
                        signal=True)
                    bb_busy[es_] = mm
                    mm_last = mm
                    if dbg is not None and chk == 0 and it == 0 and lc == 0 and gidx == 3:
                        dt1 = p.dma("pool", dbg["ebuf"], ebuf[:], after=[eb])
                        dt2 = p.dma("pool", dbg["bbuf"], bbuf[:], after=[mm])
                        eb_busy = [dt1, dt1]
                        bb_busy = [dt2, dt2]
            tb = it % 2
            if dbg is not None and chk == 0 and it == 0:
                sdk = p.dve(lambda v, b=b: v.tensor_copy(out=sdump[:], in_=psA[b][:, :]),
                            after=[mm_last], signal=True)
                p.dma("sp", dbg["sdump"], sdump[:], after=[sdk])
            s1 = p.dve(lambda v, it=it, b=b, tb=tb: v.scalar_tensor_tensor(
                out=t1b[:, tb * 512:(tb + 1) * 512], in0=psA[b][:, :],
                scalar=dcol[:, it:it + 1], in1=xcv[:, it * 512:(it + 1) * 512],
                op0=ALU.add, op1=ALU.mult),
                after=[mm_last, t1_busy[tb]], signal=True)
            psA_busy[b] = s1
            yk = p.dve(lambda v, it=it, tb=tb: v.tensor_tensor(
                out=xh[:, it * 516: it * 516 + 512],
                in0=t1b[:, tb * 512:(tb + 1) * 512],
                in1=szb[:, it * 512:(it + 1) * 512], op=ALU.mult),
                after=[s1, sz_toks[it], hs if it == 0 else None,
                       y_busy if it == 0 else None], signal=True)
            t1_busy[tb] = yk
            st_toks.append(yk)
            if dbg is not None and chk == 0 and it == 0:
                p.dma("pool", dbg["t1"], t1b[:, tb * 512:(tb + 1) * 512], after=[s1, yk])
        dbc_busy = exp_last
        sz_busy = st_toks[-1]
        xcv_busy = st_toks[-1]
        bcpk_busy = st_toks[-1]
        xh_busy = st_toks[-1]
        if dbg is not None and chk == 0:
            p.dma("pool", dbg["y"], xh.ap().rearrange("p (c n) -> p c n", n=516)[:, :, 0:512],
                  after=st_toks)
        f_toks = []
        for sub in range(4):
            sl = sub % 2
            row0 = tok0 + sub * 128
            x2k = p.dma("sp", xt[:, sl * 1024:(sl + 1) * 1024],
                        x_in[row0:row0 + 128, :], after=[xt_busy[sl]])
            eks = []
            for nh in range(2):
                fk = None
                for i in range(16):
                    fk = p.pe(lambda t, i=i, nh=nh, sub=sub: t.matmul(
                        psB[nh][:, :],
                        xh[:, i * 516 + sub * 128: i * 516 + (sub + 1) * 128],
                        opT[:, i * 1024 + nh * 512: i * 1024 + nh * 512 + 512],
                        start=(i == 0), stop=(i == 15)),
                        after=([psB_busy[nh]] + (st_toks if sub == 0 else []))
                        if i == 0 else None,
                        signal=(i == 15))
                ekn = p.dve(lambda v, sl=sl, nh=nh: v.tensor_tensor(
                    out=outt[:, nh * 512:(nh + 1) * 512], in0=psB[nh][:, :],
                    in1=xt[:, sl * 1024 + nh * 512: sl * 1024 + nh * 512 + 512],
                    op=ALU.add),
                    after=[fk, x2k, out_busy if nh == 0 else None], signal=True)
                psB_busy[nh] = ekn
                eks.append(ekn)
            xt_busy[sl] = eks[-1]
            stok = p.dma("sp", x_out[row0:row0 + 128, :], outt[:, 0:1024],
                         after=eks)
            out_busy = stok
            f_toks.append(stok)
        y_busy = f_toks[-1]
        x3T_busy = evl_toks[-1]
        last = [f_toks[-1]]
    return last



def _prep_host(inputs):
    g = {k: np.ascontiguousarray(np.asarray(v, dtype=np.float32)) for k, v in inputs.items()}
    h = {}
    h["x0"] = g["L0_attn_x"].reshape(32 * 2048, 1024)
    h["wqT"] = np.ascontiguousarray(g["L0_attn_Wq"].T)
    h["wkT"] = np.ascontiguousarray(g["L0_attn_Wk"].T)
    h["wvT"] = np.ascontiguousarray(g["L0_attn_Wv"].T)
    h["woT"] = np.ascontiguousarray(g["L0_attn_Wo"].T)
    h["bo_row"] = g["L0_attn_bo"].reshape(1, 1024)
    h["wiT"] = np.ascontiguousarray(g["L0_ffn_wi_W"].T)
    h["wo_rhs"] = np.ascontiguousarray(g["L0_ffn_wo_W"].T)
    h["acat"] = np.ascontiguousarray(g["L0_ffn_lora_A"].reshape(32, 1024).T)
    h["bcat"] = np.ascontiguousarray(
        g["L0_ffn_lora_B"].transpose(0, 2, 1).reshape(32, 4096))
    h["gate_rhs"] = np.ascontiguousarray(g["L0_ffn_gate_W"].T)
    h["wi_b"] = np.ascontiguousarray(g["L0_ffn_wi_b"].reshape(32, 128).T)
    h["gb_row"] = g["L0_ffn_gate_b"].reshape(1, 8)
    h["wob_row"] = g["L0_ffn_wo_b"].reshape(1, 1024)
    h["inprojT"] = np.ascontiguousarray(g["L1_attn_in_proj_w"].T)
    xpw = g["L1_attn_x_proj_w"]
    h["xp1"] = np.ascontiguousarray(xpw[1:17].T)
    h["xp2"] = np.ascontiguousarray(xpw[17:33].T)
    h["xpd"] = np.ascontiguousarray(np.tile(xpw[0:1].T, (1, 128)))
    alog = g["L1_attn_A_log"]
    pidx = np.arange(128)
    cols = np.arange(256)
    cc, gg = cols // 4, cols % 4
    h["alog_packed"] = np.ascontiguousarray(
        alog[32 * cc[None, :] + (pidx % 32)[:, None],
             4 * gg[None, :] + (pidx // 32)[:, None]])
    h["conv_cols"] = np.ascontiguousarray(
        g["L1_attn_conv_w"].reshape(16, 128, 4).transpose(1, 0, 2).reshape(128, 64))
    h["d_cols"] = np.ascontiguousarray(g["L1_attn_D"].reshape(16, 128).T)
    h["outprojT"] = np.ascontiguousarray(g["L1_attn_out_proj_w"].T)
    h["w4"] = g["L1_ffn_kernel"]
    h["a4"] = g["L1_ffn_a"]
    h["b4"] = g["L1_ffn_b"]
    h["bias4_row"] = g["L1_ffn_bias"].reshape(1, 1024)
    h["ident"] = np.eye(128, dtype=np.float32)
    h["ones_row"] = np.ones((1, 128), dtype=np.float32)
    R = np.zeros((16, 512), dtype=np.float32)
    for gidx in range(4):
        for pp_ in range(128):
            R[4 * gidx + pp_ // 32, gidx * 128 + pp_] = 1.0
    h["Rrep"] = R
    sel4 = np.zeros((128, 4, 128), dtype=np.float32)
    pidx2 = np.arange(128)
    for lc in range(4):
        sel4[pidx2, lc, 32 * lc + pidx2 % 32] = 1.0
    h["sel"] = np.ascontiguousarray(sel4.reshape(128, 512))
    return h


CONST_KEYS = ("ident", "ones_row", "Rrep", "sel")


def _declare(nc, h, names):
    return {k: nc.dram_tensor(k, list(h[k].shape), F32, kind="ExternalInput").ap()
            for k in names}


S2_KEYS = ("wiT", "wo_rhs", "acat", "bcat", "gate_rhs", "wi_b", "gb_row", "wob_row")


def build_s2_program():
    nc = bass.Bass("TRN2", target_bir_lowering=False)
    es = contextlib.ExitStack()
    h_shapes = None
    x_in = nc.dram_tensor("x_in", [T, D], F32, kind="ExternalInput").ap()
    x_out = nc.dram_tensor("x_out", [T, D], F32, kind="ExternalOutput").ap()
    import numpy as _np
    dummy = {k: _np.zeros(s, _np.float32) for k, s in dict(
        wiT=(1024, 4096), wo_rhs=(4096, 1024), acat=(1024, 32), bcat=(32, 4096),
        gate_rhs=(1024, 8), wi_b=(128, 32), gb_row=(1, 8), wob_row=(1, 1024),
        ident=(128, 128), ones_row=(1, 128)).items()}
    w = _declare(nc, dummy, S2_KEYS)
    cd = _declare(nc, dummy, ("ident", "ones_row"))
    with es:
        p = Prog()
        ctx = Ctx(nc, p, es, cd)
        toks = build_stage2(nc, p, ctx, es, x_in, x_out, w)
        p.finish(toks)
        p.emit(nc, es)
    return nc


ALL_W_KEYS = None


def build_full_program():
    nc = bass.Bass("TRN2", target_bir_lowering=False)
    es = contextlib.ExitStack()
    x0 = nc.dram_tensor("x_in", [T, D], F32, kind="ExternalInput").ap()
    y = nc.dram_tensor("y", [T, D], F32, kind="ExternalOutput").ap()
    x1 = nc.dram_tensor("x1", [T, D], F32, kind="Internal").ap()
    x2 = nc.dram_tensor("x2", [T, D], F32, kind="Internal").ap()
    x3 = nc.dram_tensor("x3", [T, D], F32, kind="Internal").ap()
    shapes = dict(
        wqT=(1024, 1024), wkT=(1024, 1024), wvT=(1024, 1024), woT=(1024, 1024),
        bo_row=(1, 1024),
        wiT=(1024, 4096), wo_rhs=(4096, 1024), acat=(1024, 32), bcat=(32, 4096),
        gate_rhs=(1024, 8), wi_b=(128, 32), gb_row=(1, 8), wob_row=(1, 1024),
        inprojT=(1024, 4096), xp1=(2048, 16), xp2=(2048, 16), xpd=(2048, 128),
        alog_packed=(128, 256), conv_cols=(128, 64), d_cols=(128, 16),
        outprojT=(2048, 1024), Rrep=(16, 512), sel=(128, 512),
        w4=(1024, 1024), a4=(1024, 16), b4=(16, 1024), bias4_row=(1, 1024),
        ident=(128, 128), ones_row=(1, 128))
    dummy = {k: np.zeros(s, np.float32) for k, s in shapes.items()}
    allk = S1_KEYS + S2_KEYS + S3_KEYS + S4_KEYS
    w = _declare(nc, dummy, allk)
    cd = _declare(nc, dummy, ("ident", "ones_row"))
    with es:
        p = Prog()
        ctx = Ctx(nc, p, es, cd)
        with contextlib.ExitStack() as es1:
            build_stage1(nc, p, ctx, es1, x0, x1, w)
        p.full_barrier()
        with contextlib.ExitStack() as es2:
            build_stage2(nc, p, ctx, es2, x1, x2, w)
        p.full_barrier()
        with contextlib.ExitStack() as es3:
            build_stage3(nc, p, ctx, es3, x2, x3, w)
        p.full_barrier()
        with contextlib.ExitStack() as es4:
            toks = build_stage4(nc, p, ctx, es4, x3, y, w)
        p.finish(toks)
        p.emit(nc, es)
    return nc


FUSED = True

_PROG_CACHE = {}


def _get_prog(name, builder):
    if name not in _PROG_CACHE:
        _PROG_CACHE[name] = builder()
    return _PROG_CACHE[name]


def kernel(**inputs):
    h = _prep_host(inputs)
    x = h["x0"]
    shards = [np.ascontiguousarray(x[c * T:(c + 1) * T]) for c in range(NCORES)]
    consts = {k: h[k] for k in ("ident", "ones_row")}

    if FUSED:
        nc = _get_prog("full", build_full_program)
        allk = S1_KEYS + S2_KEYS + S3_KEYS + S4_KEYS
        wall = {k: h[k] for k in allk}
        im = [dict(**wall, **consts, x_in=s) for s in shards]
        r = run_bass_kernel_spmd(nc, im, core_ids=list(range(NCORES)))
        ys = [r.results[c]["y"] for c in range(NCORES)]
        return np.concatenate(ys, axis=0).reshape(32, 2048, 1024).astype(np.float32)

    nc1 = _get_prog("s1", build_s1_program)
    w1 = {k: h[k] for k in S1_KEYS}
    im = [dict(**w1, **consts, x_in=s) for s in shards]
    r = run_bass_kernel_spmd(nc1, im, core_ids=list(range(NCORES)))
    xs1 = [r.results[c]["x_out"] for c in range(NCORES)]

    nc2 = _get_prog("s2", build_s2_program)
    w2 = {k: h[k] for k in S2_KEYS}
    im = [dict(**w2, **consts, x_in=s) for s in xs1]
    r = run_bass_kernel_spmd(nc2, im, core_ids=list(range(NCORES)))
    xs2 = [r.results[c]["x_out"] for c in range(NCORES)]

    nc3 = _get_prog("s3", build_s3_program)
    w3 = {k: h[k] for k in S3_KEYS}
    im = [dict(**w3, **consts, x_in=s) for s in xs2]
    r = run_bass_kernel_spmd(nc3, im, core_ids=list(range(NCORES)))
    xs3 = [r.results[c]["x_out"] for c in range(NCORES)]

    nc4 = _get_prog("s4", build_s4_program)
    w4 = {k: h[k] for k in S4_KEYS}
    im = [dict(**w4, **consts, x_in=s) for s in xs3]
    r = run_bass_kernel_spmd(nc4, im, core_ids=list(range(NCORES)))
    xs4 = [r.results[c]["x_out"] for c in range(NCORES)]

    out = np.concatenate(xs4, axis=0).reshape(32, 2048, 1024).astype(np.float32)
    return out


# version 05
# speedup vs baseline: 1.0089x; 1.0089x over previous
import contextlib
import numpy as np
import concourse.bass as bass
import concourse.mybir as mybir
from concourse.bass_utils import run_bass_kernel_spmd

F32 = mybir.dt.float32
BF16 = mybir.dt.bfloat16
AF = mybir.ActivationFunctionType
ALU = mybir.AluOpType
AX = mybir.AxisListType
RC = float(3 * 2 ** 22)

NCORES = 8
BPC = 4
S = 2048
T = BPC * S
D = 1024
NT = T // 128
EPS = 1e-5
SERIAL_E = False



class Prog:
    ENGS = ("sp", "pe", "dve", "act", "pool")

    def __init__(self):
        self.items = {e: [] for e in self.ENGS}
        self.counts = {}

    def _tok(self, sem, delta):
        v = self.counts.get(sem, 0) + delta
        self.counts[sem] = v
        return (sem, v)

    def op(self, eng, emit, after=(), signal=False, dma=False):
        waits = []
        drain = False
        for tok in (after or ()):
            if tok is None:
                continue
            sem, val = tok
            if sem == "e." + eng:
                drain = True
            else:
                waits.append((sem, val))
        tok = None
        incs = []
        if dma:
            tok = self._tok("d." + eng, 16)
            incs.append(("d." + eng, 16))
        elif signal:
            tok = self._tok("e." + eng, 1)
            incs.append(("e." + eng, 1))
        self.items[eng].append((drain, waits, emit, incs))
        return tok

    def pe(self, emit, after=(), signal=False):
        return self.op("pe", emit, after, signal)

    def dve(self, emit, after=(), signal=False):
        return self.op("dve", emit, after, signal)

    def act(self, emit, after=(), signal=False):
        return self.op("act", emit, after, signal)

    def pool(self, emit, after=(), signal=False):
        return self.op("pool", emit, after, signal)

    def sp(self, emit, after=(), signal=False):
        return self.op("sp", emit, after, signal)

    def dma(self, eng, out, in_, after=(), **kw):
        return self.op(eng, (lambda h, o=out, i=in_: h.dma_start(out=o, in_=i, **kw)),
                       after, dma=True)

    def full_barrier(self):
        toks = [(sem, val) for sem, val in self.counts.items()]
        for eng in self.ENGS:
            self.op(eng, lambda h: h.nop(), after=toks)

    def finish(self, tokens):
        self.op("sp", lambda h: h.nop(), after=tokens)

    def emit(self, nc, es):
        sems = {name: es.enter_context(nc.semaphore(name.replace(".", "_")))
                for name in self.counts}
        block = es.enter_context(nc.Block())

        def runner(ename):
            def run(h):
                waited = {}
                for drain, waits, emitfn, incs in self.items[ename]:
                    if drain:
                        h.drain()
                    for sem, val in waits:
                        if waited.get(sem, 0) >= val:
                            continue
                        h.wait_ge(sems[sem], val)
                        waited[sem] = val
                    inst = emitfn(h)
                    for sem, delta in incs:
                        inst.then_inc(sems[sem], delta)
            return run

        block.sync(runner("sp"))
        block.tensor(runner("pe"))
        block.vector(runner("dve"))
        block.scalar(runner("act"))
        block.gpsimd(runner("pool"))



class Ctx:

    def __init__(self, nc, p, es, consts_dram):
        self.nc = nc
        self.p = p
        self.es = es
        sb = lambda name, shape, dt: es.enter_context(nc.sbuf_tensor(name, shape, dt))
        self.ident_f = sb("c_ident_f", [128, 128], F32)
        self.ident_b = sb("c_ident_b", [128, 128], BF16)
        self.ones_row = sb("c_ones_row", [1, 128], F32)
        self.eps_col = sb("c_eps_col", [128, 1], F32)
        self.one_col = sb("c_one_col", [128, 1], F32)
        toks = [
            p.dma("sp", self.ident_f[:], consts_dram["ident"]),
            p.dma("pool", self.ident_b[:], consts_dram["ident"]),
            p.dma("sp", self.ones_row[:], consts_dram["ones_row"]),
        ]
        toks.append(p.dve(lambda v: v.memset(self.eps_col[:], EPS), signal=True))
        toks.append(p.dve(lambda v: v.memset(self.one_col[:], 1.0), signal=True))
        self.ready = toks

    def ln_scratch(self, nslots, es=None):
        es = es or self.es
        sb = lambda name, shape, dt: es.enter_context(
            self.nc.sbuf_tensor(name, shape, dt))
        uid = f"{len(self.p.items['dve'])}"
        return dict(
            stats=sb(f"ln_stats{uid}", [128, 12 * nslots], F32),
            mv=sb(f"ln_mv{uid}", [128, 2 * nslots], F32),
            rstd=sb(f"ln_rstd{uid}", [128, nslots], F32),
            nmr=sb(f"ln_nmr{uid}", [128, nslots], F32),
            nslots=nslots,
            busy=[None] * nslots,
        )


def emit_ln(ctx, lnsc, slot, x_ap, out_ap, after=()):
    p = ctx.p
    st = lnsc["stats"][:, 12 * slot:12 * slot + 12]
    mv = lnsc["mv"][:, 2 * slot:2 * slot + 2]
    rs = lnsc["rstd"][:, slot:slot + 1]
    nm = lnsc["nmr"][:, slot:slot + 1]
    war = lnsc["busy"][slot]
    p.dve(lambda v: v.bn_stats(out=st[:, 0:6], in_=x_ap[:, 0:512]),
          after=list(after) + [war])
    p.dve(lambda v: v.bn_stats(out=st[:, 6:12], in_=x_ap[:, 512:1024]))
    t3 = p.dve(lambda v: v.bn_aggr(out=mv, in_=st), after=[("e.dve", -1)], signal=True)
    t4 = p.act(lambda a: a.activation(out=rs, in_=mv[:, 1:2], func=AF.Sqrt,
                                      bias=ctx.eps_col[:]), after=[t3], signal=True)
    t5 = p.dve(lambda v: v.reciprocal(out=rs, in_=rs), after=[t4], signal=True)
    t6 = p.dve(lambda v: v.tensor_scalar(out=nm, in0=mv[:, 0:1], scalar1=rs, scalar2=-1.0,
                                         op0=ALU.mult, op1=ALU.mult),
               after=[t5], signal=True)
    t7 = p.act(lambda a: a.activation(out=out_ap, in_=x_ap, func=AF.Identity,
                                      scale=rs, bias=nm),
               after=[t6], signal=True)
    lnsc["busy"][slot] = t7
    return t7


def tok_now(p, eng):
    c = p.counts.get("e." + eng, 0)
    return ("e." + eng, c) if c else None


def fence_all(p, tokens):
    for eng in Prog.ENGS:
        p.op(eng, lambda h: h.nop(), after=tokens)



def build_stage2(nc, p, ctx, es, x_in, x_out, w, first_chunk=0, nchunks=16):
    sb = lambda name, shape, dt: es.enter_context(nc.sbuf_tensor(name, shape, dt))
    ps = lambda name, shape: es.enter_context(nc.psum_tensor(name, shape, F32))
    psb = lambda name, shape: es.enter_context(nc.psum_tensor(name, shape, BF16))

    wiT = sb("s2_wiT", [128, 8 * 4096], BF16)
    wo = sb("s2_wo", [128, 32 * 1024], BF16)
    acat = sb("s2_acat", [128, 8 * 32], BF16)
    bcat = sb("s2_bcat", [32, 4096], BF16)
    gate = sb("s2_gate", [128, 8 * 8], BF16)
    wib = sb("s2_wib", [128, 32], F32)
    gbr = sb("s2_gbr", [1, 8], F32)
    wobr = sb("s2_wobr", [1, 1024], F32)
    wtok = []
    for c in range(8):
        wtok.append(p.dma("pool", wiT[:, c * 4096:(c + 1) * 4096],
                          w["wiT"][c * 128:(c + 1) * 128, :]))
    for f in range(0, 32, 8):
        wtok.append(p.dma("pool", wo.ap().rearrange("p (f n) -> p f n", n=1024)[:, f:f + 8, :],
                          w["wo_rhs"].rearrange("(f p) n -> p f n", p=128)[:, f:f + 8, :]))
    for c in range(8):
        wtok.append(p.dma("pool", acat[:, c * 32:(c + 1) * 32],
                          w["acat"][c * 128:(c + 1) * 128, :]))
        wtok.append(p.dma("pool", gate[:, c * 8:(c + 1) * 8],
                          w["gate_rhs"][c * 128:(c + 1) * 128, :]))
    wtok.append(p.dma("pool", bcat[:], w["bcat"]))
    wtok.append(p.dma("sp", wib[:], w["wi_b"]))
    wtok.append(p.dma("sp", gbr[:], w["gb_row"]))
    wtok.append(p.dma("sp", wobr[:], w["wob_row"]))
    wready = wtok[-1]
    wtok_all = list(wtok) + list(ctx.ready)
    fence_all(p, wtok_all)

    xt = sb("s2_xt", [128, 4 * 1024], F32)
    ln = sb("s2_ln", [128, 1024], BF16)
    ln2T = sb("s2_ln2T", [128, 8 * 512], BF16)
    interT = sb("s2_interT", [128, 32 * 512], BF16)
    mask8 = sb("s2_mask8", [128, 2 * 8], BF16)
    mask32 = sb("s2_mask32", [128, 4 * 32], BF16)
    mcol = sb("s2_mcol", [128, 4], F32)
    ttl = sb("s2_ttl", [128, 4 * 32], BF16)
    ttT = sb("s2_ttT", [32, 512], BF16)
    outt = sb("s2_outt", [128, 1024], F32)
    lnsc = ctx.ln_scratch(2, es)

    trp = [psb("s2_trp0", [128, 1024])]
    gt_ps = ps("s2_gt", [128, 512])
    ip = [ps(f"s2_ip{i}", [128, 512]) for i in range(2)]
    pp = [ps(f"s2_pp{i}", [128, 1024]) for i in range(2)]

    trp_busy = [None]
    ip_busy = [None, None]
    pp_busy = [None, None]
    xt_busy = [None] * 4
    ln_busy = [None]
    out_busy = [None]
    ln2T_busy = None
    gt_busy = None
    ttT_busy = None
    inter_busy = None

    for ch in range(first_chunk, first_chunk + nchunks):
        tt0 = ch * 4
        ln_toks = []
        for i in range(4):
            tt = tt0 + i
            xs = i
            xtok = p.dma("sp", xt[:, xs * 1024:(xs + 1) * 1024],
                         x_in[tt * 128:(tt + 1) * 128, :],
                         after=[xt_busy[xs]])
            lnt = emit_ln(ctx, lnsc, i % 2, xt[:, xs * 1024:(xs + 1) * 1024],
                          ln[:, 0:1024],
                          after=[xtok, ln_busy[0]])
            tb = 0
            trk = None
            for c in range(8):
                trk = p.pe(lambda t, c=c, tb=tb: t.transpose(
                    trp[tb][:, c * 128:(c + 1) * 128],
                    ln[:, c * 128:(c + 1) * 128],
                    ctx.ident_b[:]),
                    after=[lnt, trp_busy[tb], ln2T_busy if c == 0 else None],
                    signal=(c == 7))
            ev = p.dve(lambda v, tb=tb, i=i: v.tensor_copy(
                out=ln2T.ap().rearrange("p (c n) -> p c n", n=512)[:, :, i * 128:(i + 1) * 128],
                in_=trp[tb].ap().rearrange("p (c n) -> p c n", n=128)),
                after=[trk], signal=True)
            trp_busy[tb] = ev
            ln_busy[0] = trk
            ln_toks.append(ev)
            xt_busy[xs] = None

        ln_ready = ln_toks
        mask_toks = []
        tt_toks = []
        for i in range(4):
            gk = None
            for c in range(8):
                gk = p.pe(lambda t, c=c, i=i: t.matmul(
                    gt_ps[:, 0:8],
                    ln2T[:, c * 512 + i * 128: c * 512 + i * 128 + 128],
                    gate[:, c * 8:(c + 1) * 8],
                    start=(c == 0), stop=False),
                    after=[ln_toks[i] if c == 0 else None, gt_busy if c == 0 else None])
            gk = p.pe(lambda t: t.matmul(gt_ps[:, 0:8], ctx.ones_row[:], gbr[:],
                                         start=False, stop=True), signal=True)
            mx = p.dve(lambda v, i=i: v.tensor_reduce(
                out=mcol[:, i:i + 1], in_=gt_ps[:, 0:8], axis=AX.X, op=ALU.max),
                after=[gk], signal=True)
            mk = p.dve(lambda v, i=i: v.tensor_scalar(
                out=mask8[:, (i % 2) * 8:(i % 2) * 8 + 8], in0=gt_ps[:, 0:8],
                scalar1=mcol[:, i:i + 1], scalar2=None, op0=ALU.is_equal),
                after=[mx, mask_toks[-1] if i >= 2 else None], signal=True)
            for r in range(4):
                mk = p.dve(lambda v, i=i, r=r: v.tensor_copy(
                    out=mask32.ap().rearrange("p (i e r) -> p i e r", i=4, r=4)[:, i, :, r],
                    in_=mask8[:, (i % 2) * 8:(i % 2) * 8 + 8]),
                    after=[mk], signal=(r == 3))
            mask_toks.append(mk)
            tk = None
            for c in range(8):
                tk = p.pe(lambda t, c=c, i=i: t.matmul(
                    gt_ps[:, 64:96],
                    ln2T[:, c * 512 + i * 128: c * 512 + i * 128 + 128],
                    acat[:, c * 32:(c + 1) * 32],
                    start=(c == 0), stop=(c == 7)),
                    after=[mk if c == 0 else None], signal=(c == 7))
            tm = p.dve(lambda v, i=i: v.tensor_tensor(
                out=ttl[:, i * 32:(i + 1) * 32], in0=gt_ps[:, 64:96],
                in1=mask32[:, i * 32:(i + 1) * 32], op=ALU.mult),
                after=[tk, mk], signal=True)
            gt_busy = tm
            tr = p.pe(lambda t, i=i: t.transpose(
                trp[0][0:32, i * 128:(i + 1) * 128], ttl[:, i * 32:(i + 1) * 32],
                ctx.ident_b[:]), after=[tm, trp_busy[0] if i == 0 else None],
                signal=(i == 3))
            if i == 3:
                ev = p.dve(lambda v: v.tensor_copy(
                    out=ttT[:], in_=trp[0][0:32, 0:512]),
                    after=[tr, ttT_busy], signal=True)
                trp_busy[0] = ev
                tt_toks.append(ev)
        ttT_ready = tt_toks[-1]

        it_toks = []
        for f in range(32):
            b = f % 2
            mmk = None
            for c in range(8):
                mmk = p.pe(lambda t, f=f, c=c, b=b: t.matmul(
                    ip[b][:, :],
                    wiT[:, c * 4096 + f * 128: c * 4096 + (f + 1) * 128],
                    ln2T[:, c * 512:(c + 1) * 512],
                    start=(c == 0), stop=False),
                    after=([ip_busy[b]] + ln_ready + [ttT_ready]) if c == 0 else None)
            mmk = p.pe(lambda t, f=f, b=b: t.matmul(
                ip[b][:, :], bcat[0:32, f * 128:(f + 1) * 128], ttT[:],
                start=False, stop=True), signal=True)
            rk = p.act(lambda a, f=f, b=b: a.activation(
                out=interT[:, f * 512:(f + 1) * 512], in_=ip[b][:, :], func=AF.Relu,
                bias=wib[:, f:f + 1]),
                after=[mmk, inter_busy if f < 2 else None], signal=True)
            ip_busy[b] = rk
            it_toks.append(rk)

        ep_toks = []
        for i in range(4):
            b = i % 2
            ob = 0
            xs = i
            mmk = None
            for f in range(32):
                for nh in range(2):
                    mmk = p.pe(lambda t, f=f, i=i, b=b, nh=nh: t.matmul(
                        pp[b][:, nh * 512:(nh + 1) * 512],
                        interT[:, f * 512 + i * 128: f * 512 + (i + 1) * 128],
                        wo[:, f * 1024 + nh * 512: f * 1024 + nh * 512 + 512],
                        start=(f == 0), stop=False),
                        after=[pp_busy[b], it_toks[f]] if (f == 0 and nh == 0) else (
                            [it_toks[f]] if nh == 0 else None))
            for nh in range(2):
                mmk = p.pe(lambda t, nh=nh, b=b: t.matmul(
                    pp[b][:, nh * 512:(nh + 1) * 512], ctx.ones_row[:],
                    wobr[:, nh * 512:(nh + 1) * 512], start=False, stop=True),
                    signal=(nh == 1))
            ek = p.dve(lambda v, b=b, ob=ob, xs=xs: v.tensor_tensor(
                out=outt[:, ob * 1024:(ob + 1) * 1024], in0=pp[b][:, :],
                in1=xt[:, xs * 1024:(xs + 1) * 1024], op=ALU.add),
                after=[mmk, out_busy[ob]], signal=True)
            pp_busy[b] = ek
            tt = tt0 + i
            stok = p.dma("sp", x_out[tt * 128:(tt + 1) * 128, :],
                         outt[:, ob * 1024:(ob + 1) * 1024], after=[ek])
            out_busy[ob] = stok
            xt_busy[xs] = ek
            ep_toks.append(stok)
        inter_busy = ep_toks[-1]
        ln2T_busy = ep_toks[-1]
        ttT_busy = ep_toks[-1]
    return ep_toks





S1_KEYS = ("wqT", "wkT", "wvT", "woT", "bo_row")


def build_stage1(nc, p, ctx, es, x_in, x_out, w):
    sb = lambda name, shape, dt: es.enter_context(nc.sbuf_tensor(name, shape, dt))
    ps = lambda name, shape: es.enter_context(nc.psum_tensor(name, shape, F32))
    psb = lambda name, shape: es.enter_context(nc.psum_tensor(name, shape, BF16))

    lnT = sb("s1_lnT", [128, 16384], BF16)
    qT = sb("s1_qT", [128, 16384], BF16)
    vsb = sb("s1_v", [128, 16384], BF16)
    sksb = sb("s1_sk", [128, 16384], BF16)
    wsl = [sb(f"s1_w{i}", [128, 8192], BF16) for i in range(2)]
    bor = sb("s1_bor", [1, 1024], F32)
    xt = sb("s1_xt", [128, 2 * 1024], F32)
    xt2 = sb("s1_xt2", [128, 2 * 1024], F32)
    ln = sb("s1_ln", [128, 2 * 1024], BF16)
    qbf = sb("s1_qbf", [128, 1024], BF16)
    nmx = sb("s1_nmx", [128, 8], F32)
    qs = sb("s1_qs", [128, 8], F32)
    kraw = sb("s1_kraw", [128, 2048], F32)
    qexp = kraw
    kbf = sb("s1_kbf", [128, 2048], BF16)
    knm = sb("s1_knm", [128, 2], F32)
    ctxb = sb("s1_ctxb", [128, 1024], BF16)
    outt = sb("s1_outt", [128, 1024], F32)
    lnsc = ctx.ln_scratch(2, es)

    trp = [psb(f"s1_trp{i}", [128, 1024]) for i in range(2)]
    pq = ps("s1_pq", [128, 1024])
    pv = ps("s1_pv", [128, 1024])
    pk = ps("s1_pk", [128, 512])

    bo_tok = p.dma("sp", bor[:], w["bo_row"])
    fence_all(p, [bo_tok] + list(ctx.ready))

    trp_busy = [None, None]
    xt_busy = [None, None]
    xt2_busy = [None, None]
    ln_busy = [None, None]
    pq_busy = None
    pv_busy = None
    pk_busy = None
    wsl_busy = [None, None]
    lnT_busy = None
    qT_busy = None
    v_busy = None
    sk_busy = None
    qwork_busy = None
    ctx_busy = [None] * 4
    ctxb_busy = None
    out_busy = None
    last = []

    for b in range(BPC):
        base = b * S
        wq_t = p.dma("pool", wsl[0].ap().rearrange("p (c n) -> p c n", n=1024),
                     w["wqT"].rearrange("(c p) n -> p c n", p=128), after=[wsl_busy[0]])
        wv_t = p.dma("pool", wsl[1].ap().rearrange("p (c n) -> p c n", n=1024),
                     w["wvT"].rearrange("(c p) n -> p c n", p=128), after=[wsl_busy[1]])
        qmm_last = None
        vmm_last = None
        for tt in range(16):
            sl = tt % 2
            row0 = (base + tt * 128)
            xtok = p.dma("sp", xt[:, sl * 1024:(sl + 1) * 1024],
                         x_in[row0:row0 + 128, :], after=[xt_busy[sl]])
            lnt = emit_ln(ctx, lnsc, sl, xt[:, sl * 1024:(sl + 1) * 1024],
                          ln[:, sl * 1024:(sl + 1) * 1024],
                          after=[xtok, ln_busy[sl]])
            tb = tt % 2
            trk = None
            for c in range(8):
                trk = p.pe(lambda t, c=c, tb=tb, sl=sl: t.transpose(
                    trp[tb][:, c * 128:(c + 1) * 128],
                    ln[:, sl * 1024 + c * 128: sl * 1024 + (c + 1) * 128],
                    ctx.ident_b[:]),
                    after=[lnt, trp_busy[tb], lnT_busy if (tt == 0 and c == 0) else None],
                    signal=(c == 7))
            ln_busy[sl] = trk
            evl = p.dve(lambda v, tb=tb, tt=tt: v.tensor_copy(
                out=lnT.ap().rearrange("p (c n) -> p c n", n=2048)[:, :, tt * 128:(tt + 1) * 128],
                in_=trp[tb].ap().rearrange("p (c n) -> p c n", n=128)),
                after=[trk], signal=True)
            trp_busy[tb] = evl
            xt_busy[sl] = trk
            xt_busy[sl] = lnt
            qk = None
            for nh in range(2):
                for c in range(8):
                    qk = p.pe(lambda t, c=c, nh=nh, tt=tt: t.matmul(
                        pq[:, nh * 512:(nh + 1) * 512],
                        lnT.ap().rearrange("p (c n) -> p c n", n=2048)[:, c, tt * 128:(tt + 1) * 128],
                        wsl[0][:, c * 1024 + nh * 512: c * 1024 + nh * 512 + 512],
                        start=(c == 0), stop=(c == 7)),
                        after=[evl, pq_busy, wq_t] if (c == 0 and nh == 0) else None,
                        signal=(c == 7 and nh == 1))
            qmm_last = qk
            sf_toks = []
            for h in range(8):
                mxk = p.dve(lambda v, h=h: v.tensor_reduce(
                    out=nmx[:, h:h + 1], in_=pq[:, h * 128:(h + 1) * 128],
                    axis=AX.X, op=ALU.max, negate=True),
                    after=[qk, qwork_busy] if h == 0 else None, signal=(h == 7))
            for h in range(8):
                ek = p.act(lambda a, h=h: a.activation(
                    out=qexp[:, h * 128:(h + 1) * 128], in_=pq[:, h * 128:(h + 1) * 128],
                    func=AF.Exp, bias=nmx[:, h:h + 1], accum_out=qs[:, h:h + 1]),
                    after=[mxk] if h == 0 else None, signal=(h == 7))
            rk = p.dve(lambda v: v.reciprocal(out=qs[:], in_=qs[:]), after=[ek], signal=True)
            pq_busy = rk
            qn = None
            for h in range(8):
                qn = p.dve(lambda v, h=h: v.tensor_scalar(
                    out=qbf[:, h * 128:(h + 1) * 128], in0=qexp[:, h * 128:(h + 1) * 128],
                    scalar1=qs[:, h:h + 1], scalar2=None, op0=ALU.mult),
                    after=[rk] if h == 0 else None, signal=(h == 7))
            qwork_busy = qn
            tb2 = tt % 2
            trq = None
            for c in range(8):
                trq = p.pe(lambda t, c=c, tb2=tb2: t.transpose(
                    trp[tb2][:, c * 128:(c + 1) * 128],
                    qbf[:, c * 128:(c + 1) * 128], ctx.ident_b[:]),
                    after=[qn, trp_busy[tb2], qT_busy if (tt == 0 and c == 0) else None],
                    signal=(c == 7))
            evq = p.dve(lambda v, tb2=tb2, tt=tt: v.tensor_copy(
                out=qT.ap().rearrange("p (c n) -> p c n", n=2048)[:, :, tt * 128:(tt + 1) * 128],
                in_=trp[tb2].ap().rearrange("p (c n) -> p c n", n=128)),
                after=[trq], signal=True)
            trp_busy[tb2] = evq
            vk = None
            for nh in range(2):
                for c in range(8):
                    vk = p.pe(lambda t, c=c, nh=nh, tt=tt: t.matmul(
                        pv[:, nh * 512:(nh + 1) * 512],
                        lnT.ap().rearrange("p (c n) -> p c n", n=2048)[:, c, tt * 128:(tt + 1) * 128],
                        wsl[1][:, c * 1024 + nh * 512: c * 1024 + nh * 512 + 512],
                        start=(c == 0), stop=(c == 7)),
                        after=[evl, pv_busy, wv_t, v_busy if tt == 0 else None]
                        if (c == 0 and nh == 0) else None,
                        signal=(c == 7 and nh == 1))
            vmm_last = vk
            evv = p.act(lambda a, tt=tt: a.activation(
                out=vsb[:, tt * 1024:(tt + 1) * 1024], in_=pv[:], func=AF.Copy),
                after=[vk], signal=True)
            pv_busy = evv
        wk_t = p.dma("pool", wsl[0].ap().rearrange("p (c n) -> p c n", n=1024),
                     w["wkT"].rearrange("(c p) n -> p c n", p=128),
                     after=[qmm_last])
        sk_done = []
        prev_nkk = None
        prev_trk2 = None
        for ct in range(8):
            kevs = []
            for chq in range(4):
                kk = None
                for c in range(8):
                    kk = p.pe(lambda t, c=c, ct=ct, chq=chq: t.matmul(
                        pk[:, :],
                        wsl[0][:, c * 1024 + ct * 128: c * 1024 + ct * 128 + 128],
                        lnT.ap().rearrange("p (c n) -> p c n", n=2048)[:, c, chq * 512:(chq + 1) * 512],
                        start=(c == 0), stop=(c == 7)),
                        after=[wk_t, pk_busy] if c == 0 else None,
                        signal=(c == 7))
                kev = p.act(lambda a, chq=chq: a.activation(
                    out=kraw[:, chq * 512:(chq + 1) * 512], in_=pk[:], func=AF.Copy),
                    after=[kk, sk_busy if (ct == 0 and chq == 0) else None,
                           qwork_busy if (ct == 0 and chq == 0) else None,
                           prev_nkk if chq == 0 else None], signal=True)
                pk_busy = kev
                kevs.append(kev)
            mxk = p.dve(lambda v: v.tensor_reduce(
                out=knm[:, 0:1], in_=kraw[:], axis=AX.X, op=ALU.max, negate=True),
                after=kevs, signal=True)
            ekk = p.act(lambda a: a.activation(
                out=kraw[:], in_=kraw[:], func=AF.Exp, bias=knm[:, 0:1],
                accum_out=knm[:, 1:2]), after=[mxk], signal=True)
            rkk = p.dve(lambda v: v.reciprocal(out=knm[:, 1:2], in_=knm[:, 1:2]),
                        after=[ekk], signal=True)
            nkk = p.dve(lambda v: v.tensor_scalar(
                out=kbf[:], in0=kraw[:], scalar1=knm[:, 1:2], scalar2=None,
                op0=ALU.mult), after=[rkk, prev_trk2], signal=True)
            prev_nkk = nkk
            for half in range(2):
                trk2 = None
                for j in range(8):
                    tt = half * 8 + j
                    trk2 = p.pe(lambda t, j=j, tt=tt, half=half: t.transpose(
                        trp[half][:, j * 128:(j + 1) * 128],
                        kbf[:, tt * 128:(tt + 1) * 128], ctx.ident_b[:]),
                        after=[nkk, trp_busy[half]] if j == 0 else None,
                        signal=(j == 7))
                evk = p.dve(lambda v, half=half, ct=ct: v.tensor_copy(
                    out=sksb.ap().rearrange("p (n c) -> p n c", c=1024)[
                        :, half * 8:(half + 1) * 8, ct * 128:(ct + 1) * 128],
                    in_=trp[half].ap().rearrange("p (n c) -> p n c", c=128)),
                    after=[trk2], signal=True)
                trp_busy[half] = evk
                sk_done.append(evk)
                prev_trk2 = trk2
        ctx_toks = []
        prev_ce = None
        for h in range(8):
            cs = h % 4
            ck = None
            for tt in range(16):
                ck = p.pe(lambda t, h=h, tt=tt, cs=cs: t.matmul(
                    pk[:, cs * 128:(cs + 1) * 128],
                    sksb[:, tt * 1024 + h * 128: tt * 1024 + (h + 1) * 128],
                    vsb[:, tt * 1024 + h * 128: tt * 1024 + (h + 1) * 128],
                    start=(tt == 0), stop=(tt == 15)),
                    after=([prev_ce, pk_busy if h == 0 else None] + sk_done +
                           [evv]) if tt == 0 else None,
                    signal=(tt == 15))
            ce = p.dve(lambda v, h=h, cs=cs: v.tensor_copy(
                out=ctxb[:, h * 128:(h + 1) * 128], in_=pk[:, cs * 128:(cs + 1) * 128]),
                after=[ck, ctxb_busy if h == 0 else None], signal=True)
            prev_ce = ce
            ctx_toks.append(ce)
        v_busy = ctx_toks[-1]
        pk_busy = ctx_toks[-1]
        o2_toks = []
        for h in range(8):
            for chq in range(4):
                ok = p.pe(lambda t, h=h, chq=chq: t.matmul(
                    pq[:, (chq % 2) * 512:((chq % 2) + 1) * 512],
                    ctxb[:, h * 128:(h + 1) * 128],
                    qT.ap().rearrange("p (c n) -> p c n", n=2048)[:, h, chq * 512:(chq + 1) * 512],
                    start=True, stop=True),
                    after=[ctx_toks[h], pq_busy if (h == 0 and chq == 0) else None,
                           o2_toks[-2] if len(o2_toks) >= 2 else None],
                    signal=True)
                oe = p.act(lambda a, h=h, chq=chq: a.activation(
                    out=lnT[:, h * 2048 + chq * 512: h * 2048 + chq * 512 + 512],
                    in_=pq[:, (chq % 2) * 512:((chq % 2) + 1) * 512], func=AF.Copy),
                    after=[ok], signal=True)
                o2_toks.append(oe)
        qT_busy = o2_toks[-1]
        pq_busy = o2_toks[-1]
        lnT_busy = None
        wo_t = p.dma("pool", wsl[1].ap().rearrange("p (c n) -> p c n", n=1024),
                     w["woT"].rearrange("(c p) n -> p c n", p=128),
                     after=[vmm_last])
        wsl_busy[0] = None
        fin_toks = []
        for tt in range(16):
            sl = tt % 2
            row0 = base + tt * 128
            x2k = p.dma("sp", xt2[:, sl * 1024:(sl + 1) * 1024],
                        x_in[row0:row0 + 128, :], after=[xt2_busy[sl]])
            fk = None
            for nh in range(2):
                for c in range(8):
                    fk = p.pe(lambda t, c=c, nh=nh, tt=tt: t.matmul(
                        pv[:, nh * 512:(nh + 1) * 512],
                        lnT.ap().rearrange("p (c n) -> p c n", n=2048)[:, c, tt * 128:(tt + 1) * 128],
                        wsl[1][:, c * 1024 + nh * 512: c * 1024 + nh * 512 + 512],
                        start=(c == 0), stop=False),
                        after=[wo_t, pv_busy, o2_toks[-1]] if (c == 0 and nh == 0) else None)
                fk = p.pe(lambda t, nh=nh: t.matmul(
                    pv[:, nh * 512:(nh + 1) * 512], ctx.ones_row[:],
                    bor[:, nh * 512:(nh + 1) * 512], start=False, stop=True),
                    signal=(nh == 1))
            ek = p.dve(lambda v, sl=sl: v.tensor_tensor(
                out=outt[:], in0=pv[:], in1=xt2[:, sl * 1024:(sl + 1) * 1024],
                op=ALU.add), after=[fk, x2k, out_busy], signal=True)
            pv_busy = ek
            xt2_busy[sl] = ek
            stok = p.dma("sp", x_out[row0:row0 + 128, :], outt[:], after=[ek])
            out_busy = stok
            fin_toks.append(stok)
        lnT_busy = fin_toks[-1]
        wsl_busy[0] = fin_toks[-1]
        wsl_busy[1] = fin_toks[-1]
        qT_busy = fin_toks[-1]
        sk_busy = ctx_toks[-1]
        ctxb_busy = o2_toks[-1]
        last = fin_toks
    return last


def build_s1_program():
    nc = bass.Bass("TRN2", target_bir_lowering=False)
    es = contextlib.ExitStack()
    x_in = nc.dram_tensor("x_in", [T, D], F32, kind="ExternalInput").ap()
    x_out = nc.dram_tensor("x_out", [T, D], F32, kind="ExternalOutput").ap()
    shapes = dict(wqT=(1024, 1024), wkT=(1024, 1024), wvT=(1024, 1024),
                  woT=(1024, 1024), bo_row=(1, 1024),
                  ident=(128, 128), ones_row=(1, 128))
    dummy = {k: np.zeros(s, np.float32) for k, s in shapes.items()}
    w = _declare(nc, dummy, S1_KEYS)
    cd = _declare(nc, dummy, ("ident", "ones_row"))
    with es:
        p = Prog()
        ctx = Ctx(nc, p, es, cd)
        toks = build_stage1(nc, p, ctx, es, x_in, x_out, w)
        p.finish(toks)
        p.emit(nc, es)
    return nc



def build_s3_program(debug=False):
    nc = bass.Bass("TRN2", target_bir_lowering=False)
    es = contextlib.ExitStack()
    x_in = nc.dram_tensor("x_in", [T, D], F32, kind="ExternalInput").ap()
    x_out = nc.dram_tensor("x_out", [T, D], F32, kind="ExternalOutput").ap()
    shapes = dict(inprojT=(1024, 4096), xp1=(2048, 16), xp2=(2048, 16), xpd=(2048, 128),
                  alog_packed=(128, 256), conv_cols=(128, 64), d_cols=(128, 16),
                  outprojT=(2048, 1024), Rrep=(16, 512), sel=(128, 512),
                  ident=(128, 128), ones_row=(1, 128))
    dummy = {k: np.zeros(s, np.float32) for k, s in shapes.items()}
    w = _declare(nc, dummy, S3_KEYS)
    cd = _declare(nc, dummy, ("ident", "ones_row"))
    with es:
        p = Prog()
        ctx = Ctx(nc, p, es, cd)
        toks = build_stage3(nc, p, ctx, es, x_in, x_out, w)
        p.finish(toks)
        p.emit(nc, es)
    return nc



S4_KEYS = ("w4", "a4", "b4", "bias4_row")


def build_stage4(nc, p, ctx, es, x_in, x_out, w):
    sb = lambda name, shape, dt: es.enter_context(nc.sbuf_tensor(name, shape, dt))
    ps = lambda name, shape: es.enter_context(nc.psum_tensor(name, shape, F32))
    psb = lambda name, shape: es.enter_context(nc.psum_tensor(name, shape, BF16))

    w4q = sb("s4_w4q", [128, 8 * 1024], BF16)
    a4sb = sb("s4_a4", [128, 8 * 16], BF16)
    b4sb = sb("s4_b4", [16, 1024], BF16)
    bias4 = sb("s4_bias4", [1, 1024], F32)
    wst = sb("s4_wst", [128, 2 * 1024], F32)
    amax = sb("s4_amax", [128, 8], F32)
    am1 = sb("s4_am1", [128, 1], F32)
    amr = sb("s4_amr", [1, 128], F32)
    sc2 = sb("s4_sc2", [1, 2], F32)
    scc = sb("s4_scc", [128, 2], F32)

    wtok = [p.dma("pool", a4sb.ap().rearrange("p (c r) -> p c r", r=16),
                  w["a4"].rearrange("(c p) r -> p c r", p=128)),
            p.dma("pool", b4sb[:], w["b4"]),
            p.dma("sp", bias4[:], w["bias4_row"])]
    fence_all(p, wtok + list(ctx.ready))

    qps = ps("s4_qps", [128, 512])
    ldq = [None, None]
    mx_toks = []
    for j in range(8):
        sl = j % 2
        lt = p.dma("sp", wst[:, sl * 1024:(sl + 1) * 1024],
                   w["w4"][j * 128:(j + 1) * 128, :], after=[ldq[sl]])
        mt = p.dve(lambda v, j=j, sl=sl: v.tensor_reduce(
            out=amax[:, j:j + 1], in_=wst[:, sl * 1024:(sl + 1) * 1024],
            axis=AX.X, op=ALU.max, apply_absolute_value=True),
            after=[lt], signal=True)
        ldq[sl] = mt
        mx_toks.append(mt)
    t1 = p.dve(lambda v: v.tensor_reduce(out=am1[:], in_=amax[:], axis=AX.X,
                                         op=ALU.max), after=mx_toks, signal=True)
    t2 = p.pe(lambda t: t.transpose(qps[0:1, 0:128], am1[:], ctx.ident_f[:]),
              after=[t1], signal=True)
    t3 = p.dve(lambda v: v.tensor_copy(out=amr[:], in_=qps[0:1, 0:128]),
               after=[t2], signal=True)
    t4 = p.dve(lambda v: v.tensor_reduce(out=sc2[:, 0:1], in_=amr[:], axis=AX.X,
                                         op=ALU.max), after=[t3], signal=True)
    t5 = p.dve(lambda v: v.tensor_scalar(out=sc2[:, 0:1], in0=sc2[:, 0:1],
                                         scalar1=1.0 / 7.0, scalar2=None,
                                         op0=ALU.mult), after=[t4], signal=True)
    t6 = p.dve(lambda v: v.reciprocal(out=sc2[:, 1:2], in_=sc2[:, 0:1]),
               after=[t5], signal=True)
    t7 = p.pe(lambda t: t.matmul(qps[:, 0:2], ctx.ones_row[:], sc2[:],
                                 start=True, stop=True), after=[t6, t2], signal=True)
    t8 = p.dve(lambda v: v.tensor_copy(out=scc[:], in_=qps[:, 0:2]),
               after=[t7], signal=True)
    ldq = [t8, t8]
    qdone = []
    for j in range(8):
        sl = j % 2
        lt = p.dma("sp", wst[:, sl * 1024:(sl + 1) * 1024],
                   w["w4"][j * 128:(j + 1) * 128, :], after=[ldq[sl], t8])
        q1 = p.dve(lambda v, j=j, sl=sl: v.tensor_scalar(
            out=wst[:, sl * 1024:(sl + 1) * 1024],
            in0=wst[:, sl * 1024:(sl + 1) * 1024],
            scalar1=scc[:, 1:2], scalar2=RC, op0=ALU.mult, op1=ALU.add),
            after=[lt, t8], signal=True)
        q2 = p.dve(lambda v, j=j, sl=sl: v.tensor_scalar(
            out=w4q[:, j * 1024:(j + 1) * 1024],
            in0=wst[:, sl * 1024:(sl + 1) * 1024],
            scalar1=RC, scalar2=scc[:, 0:1], op0=ALU.subtract, op1=ALU.mult),
            after=[q1], signal=True)
        ldq[sl] = q2
        qdone.append(q2)
    fence_all(p, qdone)

    xt = sb("s4_xt", [128, 2 * 1024], F32)
    ln = sb("s4_ln", [128, 2 * 1024], BF16)
    lnT = sb("s4_lnT", [128, 2 * 1024], BF16)
    sbf = sb("s4_sbf", [128, 16], BF16)
    sT = sb("s4_sT", [16, 2 * 128], BF16)
    outt = sb("s4_outt", [128, 1024], F32)
    lnsc = ctx.ln_scratch(2, es)
    trp = psb("s4_trp", [128, 1024])
    trs = psb("s4_trs", [16, 128])
    sps = ps("s4_sps", [128, 512])
    op_ps = ps("s4_ops", [128, 1024])

    xt_busy = [None, None]
    ln_busy = [None, None]
    lnT_busy = [None, None]
    trp_busy = None
    trs_busy = None
    sps_busy = None
    sbf_busy = None
    sT_busy = [None, None]
    op_busy = None
    out_busy = None
    last = []
    for tt in range(NT):
        sl = tt % 2
        xtok = p.dma("sp", xt[:, sl * 1024:(sl + 1) * 1024],
                     x_in[tt * 128:(tt + 1) * 128, :], after=[xt_busy[sl]])
        lnt = emit_ln(ctx, lnsc, sl, xt[:, sl * 1024:(sl + 1) * 1024],
                      ln[:, sl * 1024:(sl + 1) * 1024],
                      after=[xtok, ln_busy[sl]])
        trk = None
        for c in range(8):
            trk = p.pe(lambda t, c=c, sl=sl: t.transpose(
                trp[:, c * 128:(c + 1) * 128],
                ln[:, sl * 1024 + c * 128: sl * 1024 + (c + 1) * 128],
                ctx.ident_b[:]),
                after=[lnt, trp_busy, lnT_busy[sl]] if c == 0 else None,
                signal=(c == 7))
        ln_busy[sl] = trk
        ev = p.dve(lambda v, sl=sl: v.tensor_copy(
            out=lnT[:, sl * 1024:(sl + 1) * 1024], in_=trp[:]),
            after=[trk], signal=True)
        trp_busy = ev
        sk = None
        for c in range(8):
            sk = p.pe(lambda t, c=c, sl=sl: t.matmul(
                sps[:, 0:16], lnT[:, sl * 1024 + c * 128: sl * 1024 + (c + 1) * 128],
                a4sb[:, c * 16:(c + 1) * 16], start=(c == 0), stop=(c == 7)),
                after=[ev, sps_busy] if c == 0 else None, signal=(c == 7))
        sv = p.dve(lambda v: v.tensor_copy(out=sbf[:], in_=sps[:, 0:16]),
                   after=[sk, sbf_busy], signal=True)
        sps_busy = sv
        st = p.pe(lambda t: t.transpose(trs[:], sbf[:], ctx.ident_b[:]),
                  after=[sv, trs_busy], signal=True)
        sbf_busy = st
        sv2 = p.dve(lambda v, sl=sl: v.tensor_copy(
            out=sT[:, sl * 128:(sl + 1) * 128], in_=trs[:]),
            after=[st, sT_busy[sl]], signal=True)
        trs_busy = sv2
        mmk = None
        for nh in range(2):
            for c in range(8):
                mmk = p.pe(lambda t, c=c, nh=nh, sl=sl: t.matmul(
                    op_ps[:, nh * 512:(nh + 1) * 512],
                    lnT[:, sl * 1024 + c * 128: sl * 1024 + (c + 1) * 128],
                    w4q[:, c * 1024 + nh * 512: c * 1024 + nh * 512 + 512],
                    start=(c == 0), stop=False),
                    after=[op_busy, ev] if (c == 0 and nh == 0) else None)
            mmk = p.pe(lambda t, nh=nh, sl=sl: t.matmul(
                op_ps[:, nh * 512:(nh + 1) * 512], sT[:, sl * 128:(sl + 1) * 128],
                b4sb[:, nh * 512:(nh + 1) * 512], start=False, stop=False),
                after=[sv2])
            mmk = p.pe(lambda t, nh=nh: t.matmul(
                op_ps[:, nh * 512:(nh + 1) * 512], ctx.ones_row[:],
                bias4[:, nh * 512:(nh + 1) * 512], start=False, stop=True),
                signal=(nh == 1))
        lnT_busy[sl] = mmk
        sT_busy[sl] = mmk
        ek = p.dve(lambda v, sl=sl: v.tensor_tensor(
            out=outt[:], in0=op_ps[:], in1=xt[:, sl * 1024:(sl + 1) * 1024],
            op=ALU.add), after=[mmk, out_busy], signal=True)
        op_busy = ek
        xt_busy[sl] = ek
        stok = p.dma("sp", x_out[tt * 128:(tt + 1) * 128, :], outt[:], after=[ek])
        out_busy = stok
        last = [stok]
    return last


def build_s4_program():
    nc = bass.Bass("TRN2", target_bir_lowering=False)
    es = contextlib.ExitStack()
    x_in = nc.dram_tensor("x_in", [T, D], F32, kind="ExternalInput").ap()
    x_out = nc.dram_tensor("x_out", [T, D], F32, kind="ExternalOutput").ap()
    shapes = dict(w4=(1024, 1024), a4=(1024, 16), b4=(16, 1024), bias4_row=(1, 1024),
                  ident=(128, 128), ones_row=(1, 128))
    dummy = {k: np.zeros(s, np.float32) for k, s in shapes.items()}
    w = _declare(nc, dummy, S4_KEYS)
    cd = _declare(nc, dummy, ("ident", "ones_row"))
    with es:
        p = Prog()
        ctx = Ctx(nc, p, es, cd)
        toks = build_stage4(nc, p, ctx, es, x_in, x_out, w)
        p.finish(toks)
        p.emit(nc, es)
    return nc




S3_KEYS = ("inprojT", "xp1", "xp2", "xpd", "alog_packed", "conv_cols", "d_cols",
           "outprojT", "Rrep", "sel")


def build_stage3(nc, p, ctx, es, x_in, x_out, w, dbg=None):
    sb = lambda name, shape, dt: es.enter_context(nc.sbuf_tensor(name, shape, dt))
    ps = lambda name, shape: es.enter_context(nc.psum_tensor(name, shape, F32))
    psb = lambda name, shape: es.enter_context(nc.psum_tensor(name, shape, BF16))

    ipT = sb("s3_ipT", [128, 8 * 4096], BF16)
    opT = sb("s3_opT", [128, 16 * 1024], BF16)
    xp1 = sb("s3_xp1", [128, 16 * 16], BF16)
    xp2 = sb("s3_xp2", [128, 16 * 16], BF16)
    xpd = sb("s3_xpd", [128, 16 * 128], BF16)
    ccol = sb("s3_ccol", [128, 64], F32)
    dcol = sb("s3_dcol", [128, 16], F32)
    atab = sb("s3_atab", [128, 256], F32)
    rrep = sb("s3_rrep", [16, 512], BF16)
    sel = sb("s3_sel", [128, 512], BF16)
    wtok = []
    wtok.append(p.dma("pool", ipT.ap().rearrange("p (c n) -> p c n", n=4096),
                      w["inprojT"].rearrange("(c p) n -> p c n", p=128)))
    wtok.append(p.dma("pool", opT.ap().rearrange("p (c n) -> p c n", n=1024),
                      w["outprojT"].rearrange("(c p) n -> p c n", p=128)))
    wtok.append(p.dma("pool", xp1.ap().rearrange("p (c n) -> p c n", n=16),
                      w["xp1"].rearrange("(c p) n -> p c n", p=128)))
    wtok.append(p.dma("pool", xpd.ap().rearrange("p (c n) -> p c n", n=128),
                      w["xpd"].rearrange("(c p) n -> p c n", p=128)))
    wtok.append(p.dma("pool", xp2.ap().rearrange("p (c n) -> p c n", n=16),
                      w["xp2"].rearrange("(c p) n -> p c n", p=128)))
    wtok.append(p.dma("sp", ccol[:], w["conv_cols"]))
    wtok.append(p.dma("sp", dcol[:], w["d_cols"]))
    atab_tok = p.dma("sp", atab[:], w["alog_packed"])
    wtok.append(atab_tok)
    wtok.append(p.dma("pool", rrep[:], w["Rrep"]))
    wtok.append(p.dma("pool", sel[:], w["sel"]))
    a1 = p.act(lambda a: a.activation(out=atab[:], in_=atab[:], func=AF.Exp),
               after=[atab_tok], signal=True)
    a2 = p.dve(lambda v: v.tensor_scalar(out=atab[:], in0=atab[:], scalar1=-1.0,
                                         scalar2=None, op0=ALU.mult),
               after=[a1], signal=True)
    fence_all(p, wtok + [a2] + list(ctx.ready))

    x3T = sb("s3_x3T", [128, 8 * 512], BF16)
    xh = sb("s3_xh", [128, 16 * 516], BF16)
    ybuf = sb("s3_y", [128, 16 * 512], BF16)
    halo = sb("s3_halo", [128, 16 * 4], BF16)
    szb = sb("s3_sz", [128, 2 * 16 * 512], BF16)
    xcv = sb("s3_xcv", [128, 2 * 16 * 512], BF16)
    dgb = sb("s3_dgb", [128, 2 * 4 * 128], BF16)
    bcB = sb("s3_bcB", [32, 512], F32)
    bcC = sb("s3_bcC", [16, 512], F32)
    bcp = sb("s3_bcp", [16, 512], BF16)
    srelu = sb("s3_srelu", [128, 512], F32)
    sfp = sb("s3_sfp", [128, 512], F32)
    dbc = sb("s3_dbc", [128, 2 * 512], BF16)
    bcpk = sb("s3_bcpk", [128, 2 * 4 * 512], BF16)
    ebuf = sb("s3_E", [128, 2 * 512], BF16)
    bbuf = sb("s3_EB", [128, 2 * 512], BF16)
    t1b = sb("s3_t1", [128, 2 * 512], BF16)
    xt = sb("s3_xt", [128, 2 * 1024], F32)
    xtf = sb("s3_xtf", [128, 2 * 1024], F32)
    ln = sb("s3_ln", [128, 2 * 1024], BF16)
    outt = sb("s3_outt", [128, 1024], F32)
    lnsc = ctx.ln_scratch(2, es)

    trp = psb("s3_trp", [128, 1024])
    pIP = [ps(f"s3_pIP{i}", [128, 512]) for i in range(2)]
    pSSM = [ps(f"s3_pSSM{i}", [128, 512]) for i in range(2)]
    pXP = ps("s3_pXP", [128, 512])
    pMISC = [ps(f"s3_pMISC{i}", [128, 512]) for i in range(2)]

    B = dict(trp=None, pIP=[None, None], pSSM=[None, None], pXP=None,
             pMISC=[None, None], xt=[None, None], xtf=[None, None],
             ln=[None, None], x3T=None, xh=None, halo=None,
             sz=[None, None], xcv=[None, None], dgb=[None, None],
             bc=None, bcp=None, sfp=None, srelu=None,
             dbc=[None, None], bcpk=[None, None],
             eb=[None, None], bb=[None, None], t1=[None, None],
             y=None, out=None)
    st = {}
    last = []

    def phases_AD(chk):
        cp = chk % 2
        ci = chk % 4
        tok0 = chk * 512
        evl_toks = []
        for i in range(4):
            sl = i % 2
            row0 = tok0 + i * 128
            xtok = p.dma("sp", xt[:, sl * 1024:(sl + 1) * 1024],
                         x_in[row0:row0 + 128, :], after=[B["xt"][sl]])
            lnt = emit_ln(ctx, lnsc, sl, xt[:, sl * 1024:(sl + 1) * 1024],
                          ln[:, sl * 1024:(sl + 1) * 1024],
                          after=[xtok, B["ln"][sl]])
            B["xt"][sl] = lnt
            trk = None
            for c in range(8):
                trk = p.pe(lambda t, c=c, sl=sl: t.transpose(
                    trp[:, c * 128:(c + 1) * 128],
                    ln[:, sl * 1024 + c * 128: sl * 1024 + (c + 1) * 128],
                    ctx.ident_b[:]),
                    after=[lnt, B["trp"], B["x3T"] if (i == 0 and c == 0) else None],
                    signal=(c == 7))
            B["ln"][sl] = trk
            evl = p.dve(lambda v, i=i: v.tensor_copy(
                out=x3T.ap().rearrange("p (c n) -> p c n", n=512)[:, :, i * 128:(i + 1) * 128],
                in_=trp.ap().rearrange("p (c n) -> p c n", n=128)),
                after=[trk], signal=True)
            B["trp"] = evl
            evl_toks.append(evl)
            yield
        if ci == 0:
            hk = p.dve(lambda v: v.memset(
                xh.ap().rearrange("p (c n) -> p c n", n=516)[:, :, 1:4], 0.0),
                after=[B["xh"]], signal=True)
        else:
            hk = p.dve(lambda v: v.tensor_copy(
                out=xh.ap().rearrange("p (c n) -> p c n", n=516)[:, :, 1:4],
                in_=halo.ap().rearrange("p (c n) -> p c n", n=4)[:, :, 0:3]),
                after=[B["halo"], B["xh"]], signal=True)
        ip_toks = []
        sz_toks = []
        for o in range(32):
            b = o % 2
            mm = None
            for c in range(8):
                mm = p.pe(lambda t, o=o, c=c, b=b: t.matmul(
                    pIP[b][:, :],
                    ipT[:, c * 4096 + o * 128: c * 4096 + (o + 1) * 128],
                    x3T[:, c * 512:(c + 1) * 512],
                    start=(c == 0), stop=(c == 7)),
                    after=([B["pIP"][b]] + evl_toks) if c == 0 else None,
                    signal=(c == 7))
            if o < 16:
                ev = p.act(lambda a, o=o, b=b: a.activation(
                    out=xh[:, o * 516 + 4: o * 516 + 516], in_=pIP[b][:, :],
                    func=AF.Copy),
                    after=[mm, hk if o < 2 else None, B["xh"] if o < 2 else None],
                    signal=True)
                ip_toks.append(ev)
            else:
                ev = p.act(lambda a, o=o, b=b, cp=cp: a.activation(
                    out=szb[:, cp * 8192 + (o - 16) * 512: cp * 8192 + (o - 15) * 512],
                    in_=pIP[b][:, :], func=AF.Silu),
                    after=[mm, B["sz"][cp] if o == 16 else None], signal=True)
                sz_toks.append(ev)
            B["pIP"][b] = ev
            yield
        hs = p.dve(lambda v: v.tensor_copy(
            out=halo.ap().rearrange("p (c n) -> p c n", n=4)[:, :, 0:3],
            in_=xh.ap().rearrange("p (c n) -> p c n", n=516)[:, :, 513:516]),
            after=ip_toks, signal=True)
        B["halo"] = hs
        cv_toks = []
        for c in range(16):
            b = c % 2
            dg = None
            for j in range(4):
                dg = p.dve(lambda v, c=c, j=j, b=b: v.tensor_scalar(
                    out=dgb[:, b * 512 + j * 128: b * 512 + (j + 1) * 128],
                    in0=ctx.ident_b[:], scalar1=ccol[:, c * 4 + j: c * 4 + j + 1],
                    scalar2=None, op0=ALU.mult),
                    after=[B["dgb"][b]] if j == 0 else None, signal=(j == 3))
            mm = None
            for j in range(4):
                mm = p.pe(lambda t, c=c, j=j, b=b: t.matmul(
                    pMISC[b][:, :],
                    dgb[:, b * 512 + j * 128: b * 512 + (j + 1) * 128],
                    xh[:, c * 516 + 1 + j: c * 516 + 513 + j],
                    start=(j == 0), stop=(j == 3)),
                    after=[dg, B["pMISC"][b], ip_toks[c], hk] if j == 0 else None,
                    signal=(j == 3))
            B["dgb"][b] = mm
            ev = p.act(lambda a, c=c, b=b, cp=cp: a.activation(
                out=xcv[:, cp * 8192 + c * 512: cp * 8192 + (c + 1) * 512],
                in_=pMISC[b][:, :], func=AF.Silu),
                after=[mm, B["xcv"][cp] if c < 2 else None], signal=True)
            B["pMISC"][b] = ev
            cv_toks.append(ev)
            yield
        m1 = None
        for c in range(16):
            m1 = p.pe(lambda t, c=c, cp=cp: t.matmul(
                pXP[0:16, :], xp1[:, c * 16:(c + 1) * 16],
                xcv[:, cp * 8192 + c * 512: cp * 8192 + (c + 1) * 512],
                start=(c == 0), stop=(c == 15)),
                after=[B["pXP"], cv_toks[c]] if c == 0 else [cv_toks[c]],
                signal=(c == 15))
        e1 = p.act(lambda a: a.activation(out=bcB[0:16, :], in_=pXP[0:16, :],
                                          func=AF.Copy),
                   after=[m1, B["bc"]], signal=True)
        yield
        m2 = None
        for c in range(16):
            m2 = p.pe(lambda t, c=c, cp=cp: t.matmul(
                pXP[0:16, :], xp2[:, c * 16:(c + 1) * 16],
                xcv[:, cp * 8192 + c * 512: cp * 8192 + (c + 1) * 512],
                start=(c == 0), stop=(c == 15)),
                after=[e1] if c == 0 else None,
                signal=(c == 15))
        e2 = p.act(lambda a: a.activation(out=bcC[:], in_=pXP[0:16, :], func=AF.Copy),
                   after=[m2], signal=True)
        yield
        m3 = None
        for c in range(16):
            m3 = p.pe(lambda t, c=c, cp=cp: t.matmul(
                pXP[:, :], xpd[:, c * 128:(c + 1) * 128],
                xcv[:, cp * 8192 + c * 512: cp * 8192 + (c + 1) * 512],
                start=(c == 0), stop=(c == 15)),
                after=[e2] if c == 0 else None,
                signal=(c == 15))
        d1 = p.act(lambda a: a.activation(out=sfp[:], in_=pXP[:, :], func=AF.Abs),
                   after=[m3, B["sfp"]], signal=True)
        d2 = p.act(lambda a: a.activation(out=sfp[:], in_=sfp[:], func=AF.Exp,
                                          scale=-1.0), after=[d1], signal=True)
        d3 = p.act(lambda a: a.activation(out=sfp[:], in_=sfp[:], func=AF.Ln,
                                          bias=ctx.one_col[:]), after=[d2], signal=True)
        d4 = p.act(lambda a: a.activation(out=srelu[:], in_=pXP[:, :], func=AF.Relu),
                   after=[d3, B["srelu"]], signal=True)
        B["pXP"] = d4
        bke = p.dve(lambda v, cp=cp: v.tensor_tensor(
            out=dbc[:, cp * 512:(cp + 1) * 512], in0=sfp[:], in1=srelu[:],
            op=ALU.add), after=[d4, B["dbc"][cp]], signal=True)
        B["srelu"] = bke
        B["sfp"] = bke
        yield
        bck = p.dve(lambda v: v.tensor_tensor(out=bcp[:], in0=bcB[0:16, :],
                                              in1=bcC[:], op=ALU.mult),
                    after=[e1, e2, B["bcp"]], signal=True)
        B["bc"] = bck
        bcpk_toks = []
        bcpk_mms = []
        for gidx in range(4):
            b = gidx % 2
            mk = p.pe(lambda t, gidx=gidx, b=b: t.matmul(
                pMISC[b][:, :], rrep[:, gidx * 128:(gidx + 1) * 128], bcp[:],
                start=True, stop=True),
                after=[bck, B["pMISC"][b]], signal=True)
            bcpk_mms.append(mk)
            ek = p.dve(lambda v, gidx=gidx, b=b, cp=cp: v.tensor_copy(
                out=bcpk[:, cp * 2048 + gidx * 512: cp * 2048 + (gidx + 1) * 512],
                in_=pMISC[b][:, :]),
                after=[mk, B["bcpk"][cp] if gidx < 2 else None], signal=True)
            B["pMISC"][b] = ek
            bcpk_toks.append(ek)
            yield
        B["bcp"] = bcpk_mms[-1]
        B["xh"] = hs
        st[chk] = dict(cp=cp, tok0=tok0, bke=bke, bcpk_toks=bcpk_toks,
                       sz_toks=sz_toks, hs=hs, cv_last=cv_toks[-1])

    def phases_EF(chk, box):
        s = st.pop(chk)
        cp, tok0, bke = s["cp"], s["tok0"], s["bke"]
        bcpk_toks, sz_toks, hs = s["bcpk_toks"], s["sz_toks"], s["hs"]
        st_toks = []
        exp_last = None
        for it in range(16):
            b = it % 2
            scrub = p.dve(lambda v, b=b: v.memset(pSSM[b][:, :], 0.0),
                          after=[B["pSSM"][b]], signal=True)
            mm_last = None
            for lc in range(4):
                for gidx in range(4):
                    es_ = (it * 16 + lc * 4 + gidx) % 2
                    col = (it * 4 + lc) * 4 + gidx
                    ex = p.act(lambda a, col=col, es_=es_, cp=cp: a.activation(
                        out=ebuf[:, es_ * 512:(es_ + 1) * 512],
                        in_=dbc[:, cp * 512:(cp + 1) * 512],
                        func=AF.Exp, scale=atab[:, col:col + 1]),
                        after=[bke, B["eb"][es_]], signal=True)
                    exp_last = ex
                    eb = p.dve(lambda v, gidx=gidx, es_=es_, cp=cp: v.tensor_tensor(
                        out=bbuf[:, es_ * 512:(es_ + 1) * 512],
                        in0=ebuf[:, es_ * 512:(es_ + 1) * 512],
                        in1=bcpk[:, cp * 2048 + gidx * 512: cp * 2048 + (gidx + 1) * 512],
                        op=ALU.mult),
                        after=[ex, bcpk_toks[gidx], B["bb"][es_]], signal=True)
                    B["eb"][es_] = eb
                    mm = p.pe(lambda t, lc=lc, es_=es_, b=b: t.matmul(
                        pSSM[b][:, :], sel[:, lc * 128:(lc + 1) * 128],
                        bbuf[:, es_ * 512:(es_ + 1) * 512],
                        start=(lc == 0 and gidx == 0), stop=(lc == 3 and gidx == 3)),
                        after=[eb, scrub if (lc == 0 and gidx == 0) else None],
                        signal=True)
                    B["bb"][es_] = mm
                    mm_last = mm
                yield
            tb = it % 2
            s1 = p.dve(lambda v, it=it, b=b, tb=tb, cp=cp: v.scalar_tensor_tensor(
                out=t1b[:, tb * 512:(tb + 1) * 512], in0=pSSM[b][:, :],
                scalar=dcol[:, it:it + 1],
                in1=xcv[:, cp * 8192 + it * 512: cp * 8192 + (it + 1) * 512],
                op0=ALU.add, op1=ALU.mult),
                after=[mm_last, B["t1"][tb]], signal=True)
            B["pSSM"][b] = s1
            yk = p.dve(lambda v, it=it, tb=tb, cp=cp: v.tensor_tensor(
                out=ybuf[:, it * 512:(it + 1) * 512],
                in0=t1b[:, tb * 512:(tb + 1) * 512],
                in1=szb[:, cp * 8192 + it * 512: cp * 8192 + (it + 1) * 512],
                op=ALU.mult),
                after=[s1, sz_toks[it], B["y"] if it == 0 else None], signal=True)
            B["t1"][tb] = yk
            st_toks.append(yk)
        B["dbc"][cp] = exp_last
        B["sz"][cp] = st_toks[-1]
        B["xcv"][cp] = st_toks[-1]
        B["bcpk"][cp] = st_toks[-1]
        f_toks = []
        fk_last = None
        for sub in range(4):
            sl = sub % 2
            row0 = tok0 + sub * 128
            x2k = p.dma("sp", xtf[:, sl * 1024:(sl + 1) * 1024],
                        x_in[row0:row0 + 128, :], after=[B["xtf"][sl]])
            eks = []
            for nh in range(2):
                fk = None
                for i in range(16):
                    fk = p.pe(lambda t, i=i, nh=nh, sub=sub: t.matmul(
                        pMISC[nh][:, :],
                        ybuf[:, i * 512 + sub * 128: i * 512 + (sub + 1) * 128],
                        opT[:, i * 1024 + nh * 512: i * 1024 + nh * 512 + 512],
                        start=(i == 0), stop=(i == 15)),
                        after=([B["pMISC"][nh]] + (st_toks if sub == 0 else []))
                        if i == 0 else None,
                        signal=(i == 15))
                ekn = p.dve(lambda v, sl=sl, nh=nh: v.tensor_tensor(
                    out=outt[:, nh * 512:(nh + 1) * 512], in0=pMISC[nh][:, :],
                    in1=xtf[:, sl * 1024 + nh * 512: sl * 1024 + nh * 512 + 512],
                    op=ALU.add),
                    after=[fk, x2k, B["out"] if nh == 0 else None], signal=True)
                B["pMISC"][nh] = ekn
                eks.append(ekn)
                fk_last = fk
                yield
            B["xtf"][sl] = eks[-1]
            stok = p.dma("sp", x_out[row0:row0 + 128, :], outt[:, 0:1024],
                         after=eks)
            B["out"] = stok
            f_toks.append(stok)
        B["y"] = fk_last
        box.extend(f_toks)

    box = []
    for chk in range(17):
        gens = []
        if chk >= 1:
            box = []
            gens.append(phases_EF(chk - 1, box))
        if chk < 16:
            gens.append(phases_AD(chk))
        while gens:
            for gg in list(gens):
                try:
                    next(gg)
                except StopIteration:
                    gens.remove(gg)
        last = box if box else last
    return last


def build_s3_program(debug=False):
    nc = bass.Bass("TRN2", target_bir_lowering=False)
    es = contextlib.ExitStack()
    x_in = nc.dram_tensor("x_in", [T, D], F32, kind="ExternalInput").ap()
    x_out = nc.dram_tensor("x_out", [T, D], F32, kind="ExternalOutput").ap()
    dbg = None
    if debug:
        dd = lambda n, s: nc.dram_tensor(n, list(s), F32, kind="ExternalOutput").ap()
        dbg = dict(xh=dd("d_xh", (128, 16 * 516)), sz=dd("d_sz", (128, 16 * 512)),
                   xcv=dd("d_xcv", (128, 16 * 512)), bcB=dd("d_bcB", (16, 512)),
                   bcC=dd("d_bcC", (16, 512)), ddel=dd("d_ddel", (1, 512)),
                   dbc=dd("d_dbc", (128, 512)), bcpk=dd("d_bcpk", (128, 4 * 512)),
                   y=dd("d_y", (128, 16 * 512)), ebuf=dd("d_ebuf", (128, 1024)),
                   bbuf=dd("d_bbuf", (128, 1024)), t1=dd("d_t1", (128, 512)),
                   sdump=dd("d_sdump", (128, 512)), atab=dd("d_atab", (128, 256)))
        if debug == "override":
            dbg["ov_dbc"] = nc.dram_tensor("ov_dbc", [128, 512], F32, kind="ExternalInput").ap()
            dbg["ov_bcpk"] = nc.dram_tensor("ov_bcpk", [128, 4 * 512], F32, kind="ExternalInput").ap()
    shapes = dict(inprojT=(1024, 4096), xp1=(2048, 16), xp2=(2048, 16), xpd=(2048, 128),
                  alog_packed=(128, 256), conv_cols=(128, 64), d_cols=(128, 16),
                  outprojT=(2048, 1024), Rrep=(16, 512), sel=(128, 512),
                  ident=(128, 128), ones_row=(1, 128))
    dummy = {k: np.zeros(s, np.float32) for k, s in shapes.items()}
    w = _declare(nc, dummy, S3_KEYS)
    cd = _declare(nc, dummy, ("ident", "ones_row"))
    with es:
        p = Prog()
        ctx = Ctx(nc, p, es, cd)
        toks = build_stage3(nc, p, ctx, es, x_in, x_out, w, dbg=dbg)
        p.finish(toks)
        p.emit(nc, es)
    return nc



S4_KEYS = ("w4", "a4", "b4", "bias4_row")


def build_stage4(nc, p, ctx, es, x_in, x_out, w):
    sb = lambda name, shape, dt: es.enter_context(nc.sbuf_tensor(name, shape, dt))
    ps = lambda name, shape: es.enter_context(nc.psum_tensor(name, shape, F32))
    psb = lambda name, shape: es.enter_context(nc.psum_tensor(name, shape, BF16))

    w4q = sb("s4_w4q", [128, 8 * 1024], BF16)
    a4sb = sb("s4_a4", [128, 8 * 16], BF16)
    b4sb = sb("s4_b4", [16, 1024], BF16)
    bias4 = sb("s4_bias4", [1, 1024], F32)
    wst = sb("s4_wst", [128, 2 * 1024], F32)
    amax = sb("s4_amax", [128, 8], F32)
    am1 = sb("s4_am1", [128, 1], F32)
    amr = sb("s4_amr", [1, 128], F32)
    sc2 = sb("s4_sc2", [1, 2], F32)
    scc = sb("s4_scc", [128, 2], F32)

    wtok = [p.dma("pool", a4sb.ap().rearrange("p (c r) -> p c r", r=16),
                  w["a4"].rearrange("(c p) r -> p c r", p=128)),
            p.dma("pool", b4sb[:], w["b4"]),
            p.dma("sp", bias4[:], w["bias4_row"])]
    fence_all(p, wtok + list(ctx.ready))

    qps = ps("s4_qps", [128, 512])
    ldq = [None, None]
    mx_toks = []
    for j in range(8):
        sl = j % 2
        lt = p.dma("sp", wst[:, sl * 1024:(sl + 1) * 1024],
                   w["w4"][j * 128:(j + 1) * 128, :], after=[ldq[sl]])
        mt = p.dve(lambda v, j=j, sl=sl: v.tensor_reduce(
            out=amax[:, j:j + 1], in_=wst[:, sl * 1024:(sl + 1) * 1024],
            axis=AX.X, op=ALU.max, apply_absolute_value=True),
            after=[lt], signal=True)
        ldq[sl] = mt
        mx_toks.append(mt)
    t1 = p.dve(lambda v: v.tensor_reduce(out=am1[:], in_=amax[:], axis=AX.X,
                                         op=ALU.max), after=mx_toks, signal=True)
    t2 = p.pe(lambda t: t.transpose(qps[0:1, 0:128], am1[:], ctx.ident_f[:]),
              after=[t1], signal=True)
    t3 = p.dve(lambda v: v.tensor_copy(out=amr[:], in_=qps[0:1, 0:128]),
               after=[t2], signal=True)
    t4 = p.dve(lambda v: v.tensor_reduce(out=sc2[:, 0:1], in_=amr[:], axis=AX.X,
                                         op=ALU.max), after=[t3], signal=True)
    t5 = p.dve(lambda v: v.tensor_scalar(out=sc2[:, 0:1], in0=sc2[:, 0:1],
                                         scalar1=1.0 / 7.0, scalar2=None,
                                         op0=ALU.mult), after=[t4], signal=True)
    t6 = p.dve(lambda v: v.reciprocal(out=sc2[:, 1:2], in_=sc2[:, 0:1]),
               after=[t5], signal=True)
    t7 = p.pe(lambda t: t.matmul(qps[:, 0:2], ctx.ones_row[:], sc2[:],
                                 start=True, stop=True), after=[t6, t2], signal=True)
    t8 = p.dve(lambda v: v.tensor_copy(out=scc[:], in_=qps[:, 0:2]),
               after=[t7], signal=True)
    ldq = [t8, t8]
    qdone = []
    for j in range(8):
        sl = j % 2
        lt = p.dma("sp", wst[:, sl * 1024:(sl + 1) * 1024],
                   w["w4"][j * 128:(j + 1) * 128, :], after=[ldq[sl], t8])
        q1 = p.dve(lambda v, j=j, sl=sl: v.tensor_scalar(
            out=wst[:, sl * 1024:(sl + 1) * 1024],
            in0=wst[:, sl * 1024:(sl + 1) * 1024],
            scalar1=scc[:, 1:2], scalar2=RC, op0=ALU.mult, op1=ALU.add),
            after=[lt, t8], signal=True)
        q2 = p.dve(lambda v, j=j, sl=sl: v.tensor_scalar(
            out=w4q[:, j * 1024:(j + 1) * 1024],
            in0=wst[:, sl * 1024:(sl + 1) * 1024],
            scalar1=RC, scalar2=scc[:, 0:1], op0=ALU.subtract, op1=ALU.mult),
            after=[q1], signal=True)
        ldq[sl] = q2
        qdone.append(q2)
    fence_all(p, qdone)

    xt = sb("s4_xt", [128, 2 * 1024], F32)
    ln = sb("s4_ln", [128, 2 * 1024], BF16)
    lnT = sb("s4_lnT", [128, 2 * 1024], BF16)
    sbf = sb("s4_sbf", [128, 16], BF16)
    sT = sb("s4_sT", [16, 2 * 128], BF16)
    outt = sb("s4_outt", [128, 1024], F32)
    lnsc = ctx.ln_scratch(2, es)
    trp = psb("s4_trp", [128, 1024])
    trs = psb("s4_trs", [16, 128])
    sps = ps("s4_sps", [128, 512])
    op_ps = ps("s4_ops", [128, 1024])

    xt_busy = [None, None]
    ln_busy = [None, None]
    lnT_busy = [None, None]
    trp_busy = None
    trs_busy = None
    sps_busy = None
    sbf_busy = None
    sT_busy = [None, None]
    op_busy = None
    out_busy = None
    last = []
    for tt in range(NT):
        sl = tt % 2
        xtok = p.dma("sp", xt[:, sl * 1024:(sl + 1) * 1024],
                     x_in[tt * 128:(tt + 1) * 128, :], after=[xt_busy[sl]])
        lnt = emit_ln(ctx, lnsc, sl, xt[:, sl * 1024:(sl + 1) * 1024],
                      ln[:, sl * 1024:(sl + 1) * 1024],
                      after=[xtok, ln_busy[sl]])
        trk = None
        for c in range(8):
            trk = p.pe(lambda t, c=c, sl=sl: t.transpose(
                trp[:, c * 128:(c + 1) * 128],
                ln[:, sl * 1024 + c * 128: sl * 1024 + (c + 1) * 128],
                ctx.ident_b[:]),
                after=[lnt, trp_busy, lnT_busy[sl]] if c == 0 else None,
                signal=(c == 7))
        ln_busy[sl] = trk
        ev = p.dve(lambda v, sl=sl: v.tensor_copy(
            out=lnT[:, sl * 1024:(sl + 1) * 1024], in_=trp[:]),
            after=[trk], signal=True)
        trp_busy = ev
        sk = None
        for c in range(8):
            sk = p.pe(lambda t, c=c, sl=sl: t.matmul(
                sps[:, 0:16], lnT[:, sl * 1024 + c * 128: sl * 1024 + (c + 1) * 128],
                a4sb[:, c * 16:(c + 1) * 16], start=(c == 0), stop=(c == 7)),
                after=[ev, sps_busy] if c == 0 else None, signal=(c == 7))
        sv = p.dve(lambda v: v.tensor_copy(out=sbf[:], in_=sps[:, 0:16]),
                   after=[sk, sbf_busy], signal=True)
        sps_busy = sv
        st = p.pe(lambda t: t.transpose(trs[:], sbf[:], ctx.ident_b[:]),
                  after=[sv, trs_busy], signal=True)
        sbf_busy = st
        sv2 = p.dve(lambda v, sl=sl: v.tensor_copy(
            out=sT[:, sl * 128:(sl + 1) * 128], in_=trs[:]),
            after=[st, sT_busy[sl]], signal=True)
        trs_busy = sv2
        mmk = None
        for nh in range(2):
            for c in range(8):
                mmk = p.pe(lambda t, c=c, nh=nh, sl=sl: t.matmul(
                    op_ps[:, nh * 512:(nh + 1) * 512],
                    lnT[:, sl * 1024 + c * 128: sl * 1024 + (c + 1) * 128],
                    w4q[:, c * 1024 + nh * 512: c * 1024 + nh * 512 + 512],
                    start=(c == 0), stop=False),
                    after=[op_busy, ev] if (c == 0 and nh == 0) else None)
            mmk = p.pe(lambda t, nh=nh, sl=sl: t.matmul(
                op_ps[:, nh * 512:(nh + 1) * 512], sT[:, sl * 128:(sl + 1) * 128],
                b4sb[:, nh * 512:(nh + 1) * 512], start=False, stop=False),
                after=[sv2])
            mmk = p.pe(lambda t, nh=nh: t.matmul(
                op_ps[:, nh * 512:(nh + 1) * 512], ctx.ones_row[:],
                bias4[:, nh * 512:(nh + 1) * 512], start=False, stop=True),
                signal=(nh == 1))
        lnT_busy[sl] = mmk
        sT_busy[sl] = mmk
        ek = p.dve(lambda v, sl=sl: v.tensor_tensor(
            out=outt[:], in0=op_ps[:], in1=xt[:, sl * 1024:(sl + 1) * 1024],
            op=ALU.add), after=[mmk, out_busy], signal=True)
        op_busy = ek
        xt_busy[sl] = ek
        stok = p.dma("sp", x_out[tt * 128:(tt + 1) * 128, :], outt[:], after=[ek])
        out_busy = stok
        last = [stok]
    return last


def build_s4_program():
    nc = bass.Bass("TRN2", target_bir_lowering=False)
    es = contextlib.ExitStack()
    x_in = nc.dram_tensor("x_in", [T, D], F32, kind="ExternalInput").ap()
    x_out = nc.dram_tensor("x_out", [T, D], F32, kind="ExternalOutput").ap()
    shapes = dict(w4=(1024, 1024), a4=(1024, 16), b4=(16, 1024), bias4_row=(1, 1024),
                  ident=(128, 128), ones_row=(1, 128))
    dummy = {k: np.zeros(s, np.float32) for k, s in shapes.items()}
    w = _declare(nc, dummy, S4_KEYS)
    cd = _declare(nc, dummy, ("ident", "ones_row"))
    with es:
        p = Prog()
        ctx = Ctx(nc, p, es, cd)
        toks = build_stage4(nc, p, ctx, es, x_in, x_out, w)
        p.finish(toks)
        p.emit(nc, es)
    return nc




S3_KEYS = ("inprojT", "xp1", "xp2", "xpd", "alog_packed", "conv_cols", "d_cols",
           "outprojT", "Rrep", "sel")


def build_stage3(nc, p, ctx, es, x_in, x_out, w, dbg=None):
    sb = lambda name, shape, dt: es.enter_context(nc.sbuf_tensor(name, shape, dt))
    ps = lambda name, shape: es.enter_context(nc.psum_tensor(name, shape, F32))
    psb = lambda name, shape: es.enter_context(nc.psum_tensor(name, shape, BF16))

    ipT = sb("s3_ipT", [128, 8 * 4096], BF16)
    opT = sb("s3_opT", [128, 16 * 1024], BF16)
    xp1 = sb("s3_xp1", [128, 16 * 16], BF16)
    xp2 = sb("s3_xp2", [128, 16 * 16], BF16)
    xpd = sb("s3_xpd", [128, 16 * 128], BF16)
    ccol = sb("s3_ccol", [128, 64], F32)
    dcol = sb("s3_dcol", [128, 16], F32)
    atab = sb("s3_atab", [128, 256], F32)
    rrep = sb("s3_rrep", [16, 512], BF16)
    sel = sb("s3_sel", [128, 512], BF16)
    wtok = []
    wtok.append(p.dma("pool", ipT.ap().rearrange("p (c n) -> p c n", n=4096),
                      w["inprojT"].rearrange("(c p) n -> p c n", p=128)))
    wtok.append(p.dma("pool", opT.ap().rearrange("p (c n) -> p c n", n=1024),
                      w["outprojT"].rearrange("(c p) n -> p c n", p=128)))
    wtok.append(p.dma("pool", xp1.ap().rearrange("p (c n) -> p c n", n=16),
                      w["xp1"].rearrange("(c p) n -> p c n", p=128)))
    wtok.append(p.dma("pool", xpd.ap().rearrange("p (c n) -> p c n", n=128),
                      w["xpd"].rearrange("(c p) n -> p c n", p=128)))
    wtok.append(p.dma("pool", xp2.ap().rearrange("p (c n) -> p c n", n=16),
                      w["xp2"].rearrange("(c p) n -> p c n", p=128)))
    wtok.append(p.dma("sp", ccol[:], w["conv_cols"]))
    wtok.append(p.dma("sp", dcol[:], w["d_cols"]))
    atab_tok = p.dma("sp", atab[:], w["alog_packed"])
    wtok.append(atab_tok)
    wtok.append(p.dma("pool", rrep[:], w["Rrep"]))
    wtok.append(p.dma("pool", sel[:], w["sel"]))
    a1 = p.act(lambda a: a.activation(out=atab[:], in_=atab[:], func=AF.Exp),
               after=[atab_tok], signal=True)
    a2 = p.dve(lambda v: v.tensor_scalar(out=atab[:], in0=atab[:], scalar1=-1.0,
                                         scalar2=None, op0=ALU.mult),
               after=[a1], signal=True)
    fence_all(p, wtok + [a2] + list(ctx.ready))

    x3T = sb("s3_x3T", [128, 8 * 512], BF16)
    xh = sb("s3_xh", [128, 16 * 516], BF16)
    halo = sb("s3_halo", [128, 16 * 4], BF16)
    szb = sb("s3_sz", [128, 16 * 512], BF16)
    xcv = sb("s3_xcv", [128, 16 * 512], BF16)
    dgb = sb("s3_dgb", [128, 2 * 4 * 128], BF16)
    bcB = sb("s3_bcB", [32, 512], F32)
    bcC = sb("s3_bcC", [16, 512], F32)
    bcp = sb("s3_bcp", [16, 512], BF16)
    srelu = sb("s3_srelu", [128, 512], F32)
    sfp = sb("s3_sfp", [128, 512], F32)
    dbc = sb("s3_dbc", [128, 512], BF16)
    bcpk = sb("s3_bcpk", [128, 4 * 512], BF16)
    ebuf = sb("s3_E", [128, 2 * 512], BF16)
    bbuf = sb("s3_EB", [128, 2 * 512], BF16)
    t1b = sb("s3_t1", [128, 2 * 512], BF16)
    sdump = sb("s3_sdump", [128, 512], F32) if dbg is not None else None
    xt = sb("s3_xt", [128, 2 * 1024], F32)
    ln = sb("s3_ln", [128, 2 * 1024], BF16)
    outt = sb("s3_outt", [128, 1024], F32)
    lnsc = ctx.ln_scratch(2, es)

    trp = [psb(f"s3_trp{i}", [128, 1024]) for i in range(2)]
    psA = [ps(f"s3_psA{i}", [128, 512]) for i in range(2)]
    psB = [ps(f"s3_psB{i}", [128, 512]) for i in range(2)]
    pC = ps("s3_pC", [32, 512])
    pD = ps("s3_pD", [128, 512])

    trp_busy = [None, None]
    psA_busy = [None, None]
    psB_busy = [None, None]
    pC_busy = None
    pD_busy = None
    xt_busy = [None, None]
    ln_busy = [None, None]
    x3T_busy = None
    xh_busy = None
    halo_busy = None
    sz_busy = None
    xcv_busy = None
    dgb_busy = [None, None]
    bc_busy = None
    bcp_busy = None
    dbc_busy = None
    srelu_busy = None
    sfp_busy = None
    bcpk_busy = None
    eb_busy = [None, None]
    bb_busy = [None, None]
    t1_busy = [None, None]
    out_busy = None
    y_busy = None
    last = []

    for chk in range(16):
        bi = chk // 4
        ci = chk % 4
        tok0 = chk * 512
        evl_toks = []
        for i in range(4):
            sl = i % 2
            row0 = tok0 + i * 128
            xtok = p.dma("sp", xt[:, sl * 1024:(sl + 1) * 1024],
                         x_in[row0:row0 + 128, :], after=[xt_busy[sl]])
            lnt = emit_ln(ctx, lnsc, sl, xt[:, sl * 1024:(sl + 1) * 1024],
                          ln[:, sl * 1024:(sl + 1) * 1024],
                          after=[xtok, ln_busy[sl]])
            trk = None
            for c in range(8):
                trk = p.pe(lambda t, c=c, sl=sl: t.transpose(
                    trp[sl][:, c * 128:(c + 1) * 128],
                    ln[:, sl * 1024 + c * 128: sl * 1024 + (c + 1) * 128],
                    ctx.ident_b[:]),
                    after=[lnt, trp_busy[sl], x3T_busy if (i == 0 and c == 0) else None],
                    signal=(c == 7))
            ln_busy[sl] = trk
            xt_busy[sl] = lnt
            evl = p.dve(lambda v, sl=sl, i=i: v.tensor_copy(
                out=x3T.ap().rearrange("p (c n) -> p c n", n=512)[:, :, i * 128:(i + 1) * 128],
                in_=trp[sl].ap().rearrange("p (c n) -> p c n", n=128)),
                after=[trk], signal=True)
            trp_busy[sl] = evl
            evl_toks.append(evl)
        if ci == 0:
            hk = p.dve(lambda v: v.memset(
                xh.ap().rearrange("p (c n) -> p c n", n=516)[:, :, 1:4], 0.0),
                after=[xh_busy, y_busy], signal=True)
        else:
            hk = p.dve(lambda v: v.tensor_copy(
                out=xh.ap().rearrange("p (c n) -> p c n", n=516)[:, :, 1:4],
                in_=halo.ap().rearrange("p (c n) -> p c n", n=4)[:, :, 0:3]),
                after=[halo_busy, xh_busy, y_busy], signal=True)
        ip_toks = []
        sz_toks = []
        for o in range(32):
            b = o % 2
            mm = None
            for c in range(8):
                mm = p.pe(lambda t, o=o, c=c, b=b: t.matmul(
                    psA[b][:, :],
                    ipT[:, c * 4096 + o * 128: c * 4096 + (o + 1) * 128],
                    x3T[:, c * 512:(c + 1) * 512],
                    start=(c == 0), stop=(c == 7)),
                    after=([psA_busy[b]] + evl_toks) if c == 0 else None,
                    signal=(c == 7))
            if o < 16:
                ev = p.act(lambda a, o=o, b=b: a.activation(
                    out=xh[:, o * 516 + 4: o * 516 + 516], in_=psA[b][:, :],
                    func=AF.Copy), after=[mm, hk if o < 2 else None], signal=True)
                ip_toks.append(ev)
            else:
                ev = p.act(lambda a, o=o, b=b: a.activation(
                    out=szb[:, (o - 16) * 512:(o - 15) * 512], in_=psA[b][:, :],
                    func=AF.Silu), after=[mm, sz_busy if o == 16 else None],
                    signal=True)
                sz_toks.append(ev)
            psA_busy[b] = ev
        hs = p.dve(lambda v: v.tensor_copy(
            out=halo.ap().rearrange("p (c n) -> p c n", n=4)[:, :, 0:3],
            in_=xh.ap().rearrange("p (c n) -> p c n", n=516)[:, :, 513:516]),
            after=ip_toks, signal=True)
        halo_busy = hs
        cv_toks = []
        for c in range(16):
            b = c % 2
            dg = None
            for j in range(4):
                dg = p.dve(lambda v, c=c, j=j, b=b: v.tensor_scalar(
                    out=dgb[:, b * 512 + j * 128: b * 512 + (j + 1) * 128],
                    in0=ctx.ident_b[:], scalar1=ccol[:, c * 4 + j: c * 4 + j + 1],
                    scalar2=None, op0=ALU.mult),
                    after=[dgb_busy[b]] if j == 0 else None, signal=(j == 3))
            mm = None
            for j in range(4):
                mm = p.pe(lambda t, c=c, j=j, b=b: t.matmul(
                    psB[b][:, :],
                    dgb[:, b * 512 + j * 128: b * 512 + (j + 1) * 128],
                    xh[:, c * 516 + 1 + j: c * 516 + 513 + j],
                    start=(j == 0), stop=(j == 3)),
                    after=[dg, psB_busy[b], ip_toks[c], hk] if j == 0 else None,
                    signal=(j == 3))
            dgb_busy[b] = mm
            ev = p.act(lambda a, c=c, b=b: a.activation(
                out=xcv[:, c * 512:(c + 1) * 512], in_=psB[b][:, :], func=AF.Silu),
                after=[mm, xcv_busy if c < 2 else None], signal=True)
            psB_busy[b] = ev
            cv_toks.append(ev)
        if dbg is not None and chk == 0:
            p.dma("pool", dbg["xh"], xh[:], after=ip_toks + [hk])
            p.dma("pool", dbg["sz"], szb[:], after=sz_toks)
            p.dma("pool", dbg["xcv"], xcv[:], after=cv_toks)
        m1 = None
        for c in range(16):
            m1 = p.pe(lambda t, c=c: t.matmul(
                pC[0:16, :], xp1[:, c * 16:(c + 1) * 16], xcv[:, c * 512:(c + 1) * 512],
                start=(c == 0), stop=(c == 15)),
                after=[pC_busy, cv_toks[c]] if c == 0 else [cv_toks[c]],
                signal=(c == 15))
        m2 = None
        for c in range(16):
            m2 = p.pe(lambda t, c=c: t.matmul(
                pD[0:16, :], xp2[:, c * 16:(c + 1) * 16], xcv[:, c * 512:(c + 1) * 512],
                start=(c == 0), stop=(c == 15)),
                after=[pD_busy] if c == 0 else None,
                signal=(c == 15))
        e1 = p.act(lambda a: a.activation(out=bcB[0:16, :], in_=pC[0:16, :], func=AF.Copy),
                   after=[m1, bc_busy], signal=True)
        e2 = p.act(lambda a: a.activation(out=bcC[:], in_=pD[0:16, :], func=AF.Copy),
                   after=[m2, bc_busy], signal=True)
        pC_busy = e1
        m3 = None
        for c in range(16):
            m3 = p.pe(lambda t, c=c: t.matmul(
                pD[:, :], xpd[:, c * 128:(c + 1) * 128], xcv[:, c * 512:(c + 1) * 512],
                start=(c == 0), stop=(c == 15)),
                after=[e2] if c == 0 else None,
                signal=(c == 15))
        d1 = p.act(lambda a: a.activation(out=sfp[:], in_=pD[:, :], func=AF.Abs),
                   after=[m3, sfp_busy], signal=True)
        d2 = p.act(lambda a: a.activation(out=sfp[:], in_=sfp[:], func=AF.Exp,
                                          scale=-1.0), after=[d1], signal=True)
        d3 = p.act(lambda a: a.activation(out=sfp[:], in_=sfp[:], func=AF.Ln,
                                          bias=ctx.one_col[:]),
                   after=[d2], signal=True)
        d4 = p.act(lambda a: a.activation(out=srelu[:], in_=pD[:, :], func=AF.Relu),
                   after=[d3, srelu_busy], signal=True)
        pD_busy = d4
        bke = p.dve(lambda v: v.tensor_tensor(out=dbc[:], in0=sfp[:], in1=srelu[:],
                                              op=ALU.add), after=[d4, dbc_busy], signal=True)
        srelu_busy = bke
        sfp_busy = bke
        bck = p.dve(lambda v: v.tensor_tensor(out=bcp[:], in0=bcB[0:16, :],
                                              in1=bcC[:], op=ALU.mult),
                    after=[e1, e2, bcp_busy], signal=True)
        bc_busy = bck
        bcpk_toks = []
        bcpk_mms = []
        for gidx in range(4):
            b = gidx % 2
            mk = p.pe(lambda t, gidx=gidx, b=b: t.matmul(
                psB[b][:, :], rrep[:, gidx * 128:(gidx + 1) * 128], bcp[:],
                start=True, stop=True),
                after=[bck, psB_busy[b]], signal=True)
            bcpk_mms.append(mk)
            ek = p.dve(lambda v, gidx=gidx, b=b: v.tensor_copy(
                out=bcpk[:, gidx * 512:(gidx + 1) * 512], in_=psB[b][:, :]),
                after=[mk, bcpk_busy if gidx < 2 else None], signal=True)
            psB_busy[b] = ek
            bcpk_toks.append(ek)
        bcp_busy = bcpk_mms[-1]
        if dbg is not None and chk == 0:
            p.dma("sp", dbg["atab"], atab[:], after=[a2])
            p.dma("sp", dbg["bcB"], bcB[0:16, :], after=[bck])
            p.dma("sp", dbg["bcC"], bcC[:], after=[bck])
            p.dma("pool", dbg["ddel"], dbc[0:1, :], after=[bke, ("e.dve", -1)])
            p.dma("pool", dbg["dbc"], dbc[:], after=[bke])
            p.dma("pool", dbg["bcpk"], bcpk[:], after=bcpk_toks)
        ov1 = ov2 = None
        if dbg is not None and chk == 0 and "ov_dbc" in dbg:
            ov1 = p.dma("sp", dbc[:], dbg["ov_dbc"], after=[bke])
            ov2 = p.dma("pool", bcpk[:], dbg["ov_bcpk"], after=bcpk_toks)
        st_toks = []
        exp_last = None
        for it in range(16):
            b = it % 2
            scrub = p.dve(lambda v, b=b: v.memset(psA[b][:, :], 0.0),
                          after=[psA_busy[b]], signal=True)
            psA_busy[b] = scrub
            mm_last = None
            for lc in range(4):
                for gidx in range(4):
                    es_ = (it * 16 + lc * 4 + gidx) % 2
                    col = (it * 4 + lc) * 4 + gidx
                    ex = p.act(lambda a, col=col, es_=es_: a.activation(
                        out=ebuf[:, es_ * 512:(es_ + 1) * 512], in_=dbc[:],
                        func=AF.Exp, scale=atab[:, col:col + 1]),
                        after=[bke, ov1, eb_busy[es_],
                               mm_last if SERIAL_E else None], signal=True)
                    exp_last = ex
                    eb = p.dve(lambda v, gidx=gidx, es_=es_: v.tensor_tensor(
                        out=bbuf[:, es_ * 512:(es_ + 1) * 512],
                        in0=ebuf[:, es_ * 512:(es_ + 1) * 512],
                        in1=bcpk[:, gidx * 512:(gidx + 1) * 512], op=ALU.mult),
                        after=[ex, bcpk_toks[gidx], ov2, bb_busy[es_]], signal=True)
                    eb_busy[es_] = eb
                    mm = p.pe(lambda t, lc=lc, es_=es_, b=b: t.matmul(
                        psA[b][:, :], sel[:, lc * 128:(lc + 1) * 128],
                        bbuf[:, es_ * 512:(es_ + 1) * 512],
                        start=(lc == 0 and gidx == 0), stop=(lc == 3 and gidx == 3)),
                        after=[eb, psA_busy[b] if (lc == 0 and gidx == 0) else None],
                        signal=True)
                    bb_busy[es_] = mm
                    mm_last = mm
                    if dbg is not None and chk == 0 and it == 0 and lc == 0 and gidx == 3:
                        dt1 = p.dma("pool", dbg["ebuf"], ebuf[:], after=[eb])
                        dt2 = p.dma("pool", dbg["bbuf"], bbuf[:], after=[mm])
                        eb_busy = [dt1, dt1]
                        bb_busy = [dt2, dt2]
            tb = it % 2
            if dbg is not None and chk == 0 and it == 0:
                sdk = p.dve(lambda v, b=b: v.tensor_copy(out=sdump[:], in_=psA[b][:, :]),
                            after=[mm_last], signal=True)
                p.dma("sp", dbg["sdump"], sdump[:], after=[sdk])
            s1 = p.dve(lambda v, it=it, b=b, tb=tb: v.scalar_tensor_tensor(
                out=t1b[:, tb * 512:(tb + 1) * 512], in0=psA[b][:, :],
                scalar=dcol[:, it:it + 1], in1=xcv[:, it * 512:(it + 1) * 512],
                op0=ALU.add, op1=ALU.mult),
                after=[mm_last, t1_busy[tb]], signal=True)
            psA_busy[b] = s1
            yk = p.dve(lambda v, it=it, tb=tb: v.tensor_tensor(
                out=xh[:, it * 516: it * 516 + 512],
                in0=t1b[:, tb * 512:(tb + 1) * 512],
                in1=szb[:, it * 512:(it + 1) * 512], op=ALU.mult),
                after=[s1, sz_toks[it], hs if it == 0 else None,
                       y_busy if it == 0 else None], signal=True)
            t1_busy[tb] = yk
            st_toks.append(yk)
            if dbg is not None and chk == 0 and it == 0:
                p.dma("pool", dbg["t1"], t1b[:, tb * 512:(tb + 1) * 512], after=[s1, yk])
        dbc_busy = exp_last
        sz_busy = st_toks[-1]
        xcv_busy = st_toks[-1]
        bcpk_busy = st_toks[-1]
        xh_busy = st_toks[-1]
        if dbg is not None and chk == 0:
            p.dma("pool", dbg["y"], xh.ap().rearrange("p (c n) -> p c n", n=516)[:, :, 0:512],
                  after=st_toks)
        f_toks = []
        for sub in range(4):
            sl = sub % 2
            row0 = tok0 + sub * 128
            x2k = p.dma("sp", xt[:, sl * 1024:(sl + 1) * 1024],
                        x_in[row0:row0 + 128, :], after=[xt_busy[sl]])
            eks = []
            for nh in range(2):
                fk = None
                for i in range(16):
                    fk = p.pe(lambda t, i=i, nh=nh, sub=sub: t.matmul(
                        psB[nh][:, :],
                        xh[:, i * 516 + sub * 128: i * 516 + (sub + 1) * 128],
                        opT[:, i * 1024 + nh * 512: i * 1024 + nh * 512 + 512],
                        start=(i == 0), stop=(i == 15)),
                        after=([psB_busy[nh]] + (st_toks if sub == 0 else []))
                        if i == 0 else None,
                        signal=(i == 15))
                ekn = p.dve(lambda v, sl=sl, nh=nh: v.tensor_tensor(
                    out=outt[:, nh * 512:(nh + 1) * 512], in0=psB[nh][:, :],
                    in1=xt[:, sl * 1024 + nh * 512: sl * 1024 + nh * 512 + 512],
                    op=ALU.add),
                    after=[fk, x2k, out_busy if nh == 0 else None], signal=True)
                psB_busy[nh] = ekn
                eks.append(ekn)
            xt_busy[sl] = eks[-1]
            stok = p.dma("sp", x_out[row0:row0 + 128, :], outt[:, 0:1024],
                         after=eks)
            out_busy = stok
            f_toks.append(stok)
        y_busy = f_toks[-1]
        x3T_busy = evl_toks[-1]
        last = [f_toks[-1]]
    return last



def _prep_host(inputs):
    g = {k: np.ascontiguousarray(np.asarray(v, dtype=np.float32)) for k, v in inputs.items()}
    h = {}
    h["x0"] = g["L0_attn_x"].reshape(32 * 2048, 1024)
    h["wqT"] = np.ascontiguousarray(g["L0_attn_Wq"].T)
    h["wkT"] = np.ascontiguousarray(g["L0_attn_Wk"].T)
    h["wvT"] = np.ascontiguousarray(g["L0_attn_Wv"].T)
    h["woT"] = np.ascontiguousarray(g["L0_attn_Wo"].T)
    h["bo_row"] = g["L0_attn_bo"].reshape(1, 1024)
    h["wiT"] = np.ascontiguousarray(g["L0_ffn_wi_W"].T)
    h["wo_rhs"] = np.ascontiguousarray(g["L0_ffn_wo_W"].T)
    h["acat"] = np.ascontiguousarray(g["L0_ffn_lora_A"].reshape(32, 1024).T)
    h["bcat"] = np.ascontiguousarray(
        g["L0_ffn_lora_B"].transpose(0, 2, 1).reshape(32, 4096))
    h["gate_rhs"] = np.ascontiguousarray(g["L0_ffn_gate_W"].T)
    h["wi_b"] = np.ascontiguousarray(g["L0_ffn_wi_b"].reshape(32, 128).T)
    h["gb_row"] = g["L0_ffn_gate_b"].reshape(1, 8)
    h["wob_row"] = g["L0_ffn_wo_b"].reshape(1, 1024)
    h["inprojT"] = np.ascontiguousarray(g["L1_attn_in_proj_w"].T)
    xpw = g["L1_attn_x_proj_w"]
    h["xp1"] = np.ascontiguousarray(xpw[1:17].T)
    h["xp2"] = np.ascontiguousarray(xpw[17:33].T)
    h["xpd"] = np.ascontiguousarray(np.tile(xpw[0:1].T, (1, 128)))
    alog = g["L1_attn_A_log"]
    pidx = np.arange(128)
    cols = np.arange(256)
    cc, gg = cols // 4, cols % 4
    h["alog_packed"] = np.ascontiguousarray(
        alog[32 * cc[None, :] + (pidx % 32)[:, None],
             4 * gg[None, :] + (pidx // 32)[:, None]])
    h["conv_cols"] = np.ascontiguousarray(
        g["L1_attn_conv_w"].reshape(16, 128, 4).transpose(1, 0, 2).reshape(128, 64))
    h["d_cols"] = np.ascontiguousarray(g["L1_attn_D"].reshape(16, 128).T)
    h["outprojT"] = np.ascontiguousarray(g["L1_attn_out_proj_w"].T)
    h["w4"] = g["L1_ffn_kernel"]
    h["a4"] = g["L1_ffn_a"]
    h["b4"] = g["L1_ffn_b"]
    h["bias4_row"] = g["L1_ffn_bias"].reshape(1, 1024)
    h["ident"] = np.eye(128, dtype=np.float32)
    h["ones_row"] = np.ones((1, 128), dtype=np.float32)
    R = np.zeros((16, 512), dtype=np.float32)
    for gidx in range(4):
        for pp_ in range(128):
            R[4 * gidx + pp_ // 32, gidx * 128 + pp_] = 1.0
    h["Rrep"] = R
    sel4 = np.zeros((128, 4, 128), dtype=np.float32)
    pidx2 = np.arange(128)
    for lc in range(4):
        sel4[pidx2, lc, 32 * lc + pidx2 % 32] = 1.0
    h["sel"] = np.ascontiguousarray(sel4.reshape(128, 512))
    return h


CONST_KEYS = ("ident", "ones_row", "Rrep", "sel")


def _declare(nc, h, names):
    return {k: nc.dram_tensor(k, list(h[k].shape), F32, kind="ExternalInput").ap()
            for k in names}


S2_KEYS = ("wiT", "wo_rhs", "acat", "bcat", "gate_rhs", "wi_b", "gb_row", "wob_row")


def build_s2_program():
    nc = bass.Bass("TRN2", target_bir_lowering=False)
    es = contextlib.ExitStack()
    h_shapes = None
    x_in = nc.dram_tensor("x_in", [T, D], F32, kind="ExternalInput").ap()
    x_out = nc.dram_tensor("x_out", [T, D], F32, kind="ExternalOutput").ap()
    import numpy as _np
    dummy = {k: _np.zeros(s, _np.float32) for k, s in dict(
        wiT=(1024, 4096), wo_rhs=(4096, 1024), acat=(1024, 32), bcat=(32, 4096),
        gate_rhs=(1024, 8), wi_b=(128, 32), gb_row=(1, 8), wob_row=(1, 1024),
        ident=(128, 128), ones_row=(1, 128)).items()}
    w = _declare(nc, dummy, S2_KEYS)
    cd = _declare(nc, dummy, ("ident", "ones_row"))
    with es:
        p = Prog()
        ctx = Ctx(nc, p, es, cd)
        toks = build_stage2(nc, p, ctx, es, x_in, x_out, w)
        p.finish(toks)
        p.emit(nc, es)
    return nc


ALL_W_KEYS = None


def build_full_program():
    nc = bass.Bass("TRN2", target_bir_lowering=False)
    es = contextlib.ExitStack()
    x0 = nc.dram_tensor("x_in", [T, D], F32, kind="ExternalInput").ap()
    y = nc.dram_tensor("y", [T, D], F32, kind="ExternalOutput").ap()
    x1 = nc.dram_tensor("x1", [T, D], F32, kind="Internal").ap()
    x2 = nc.dram_tensor("x2", [T, D], F32, kind="Internal").ap()
    x3 = nc.dram_tensor("x3", [T, D], F32, kind="Internal").ap()
    shapes = dict(
        wqT=(1024, 1024), wkT=(1024, 1024), wvT=(1024, 1024), woT=(1024, 1024),
        bo_row=(1, 1024),
        wiT=(1024, 4096), wo_rhs=(4096, 1024), acat=(1024, 32), bcat=(32, 4096),
        gate_rhs=(1024, 8), wi_b=(128, 32), gb_row=(1, 8), wob_row=(1, 1024),
        inprojT=(1024, 4096), xp1=(2048, 16), xp2=(2048, 16), xpd=(2048, 128),
        alog_packed=(128, 256), conv_cols=(128, 64), d_cols=(128, 16),
        outprojT=(2048, 1024), Rrep=(16, 512), sel=(128, 512),
        w4=(1024, 1024), a4=(1024, 16), b4=(16, 1024), bias4_row=(1, 1024),
        ident=(128, 128), ones_row=(1, 128))
    dummy = {k: np.zeros(s, np.float32) for k, s in shapes.items()}
    allk = S1_KEYS + S2_KEYS + S3_KEYS + S4_KEYS
    w = _declare(nc, dummy, allk)
    cd = _declare(nc, dummy, ("ident", "ones_row"))
    with es:
        p = Prog()
        ctx = Ctx(nc, p, es, cd)
        with contextlib.ExitStack() as es1:
            build_stage1(nc, p, ctx, es1, x0, x1, w)
        p.full_barrier()
        with contextlib.ExitStack() as es2:
            build_stage2(nc, p, ctx, es2, x1, x2, w)
        p.full_barrier()
        with contextlib.ExitStack() as es3:
            build_stage3(nc, p, ctx, es3, x2, x3, w)
        p.full_barrier()
        with contextlib.ExitStack() as es4:
            toks = build_stage4(nc, p, ctx, es4, x3, y, w)
        p.finish(toks)
        p.emit(nc, es)
    return nc


FUSED = True

_PROG_CACHE = {}


def _get_prog(name, builder):
    if name not in _PROG_CACHE:
        _PROG_CACHE[name] = builder()
    return _PROG_CACHE[name]


def kernel(**inputs):
    h = _prep_host(inputs)
    x = h["x0"]
    shards = [np.ascontiguousarray(x[c * T:(c + 1) * T]) for c in range(NCORES)]
    consts = {k: h[k] for k in ("ident", "ones_row")}

    if FUSED:
        nc = _get_prog("full", build_full_program)
        allk = S1_KEYS + S2_KEYS + S3_KEYS + S4_KEYS
        wall = {k: h[k] for k in allk}
        im = [dict(**wall, **consts, x_in=s) for s in shards]
        r = run_bass_kernel_spmd(nc, im, core_ids=list(range(NCORES)))
        ys = [r.results[c]["y"] for c in range(NCORES)]
        return np.concatenate(ys, axis=0).reshape(32, 2048, 1024).astype(np.float32)

    nc1 = _get_prog("s1", build_s1_program)
    w1 = {k: h[k] for k in S1_KEYS}
    im = [dict(**w1, **consts, x_in=s) for s in shards]
    r = run_bass_kernel_spmd(nc1, im, core_ids=list(range(NCORES)))
    xs1 = [r.results[c]["x_out"] for c in range(NCORES)]

    nc2 = _get_prog("s2", build_s2_program)
    w2 = {k: h[k] for k in S2_KEYS}
    im = [dict(**w2, **consts, x_in=s) for s in xs1]
    r = run_bass_kernel_spmd(nc2, im, core_ids=list(range(NCORES)))
    xs2 = [r.results[c]["x_out"] for c in range(NCORES)]

    nc3 = _get_prog("s3", build_s3_program)
    w3 = {k: h[k] for k in S3_KEYS}
    im = [dict(**w3, **consts, x_in=s) for s in xs2]
    r = run_bass_kernel_spmd(nc3, im, core_ids=list(range(NCORES)))
    xs3 = [r.results[c]["x_out"] for c in range(NCORES)]

    nc4 = _get_prog("s4", build_s4_program)
    w4 = {k: h[k] for k in S4_KEYS}
    im = [dict(**w4, **consts, x_in=s) for s in xs3]
    r = run_bass_kernel_spmd(nc4, im, core_ids=list(range(NCORES)))
    xs4 = [r.results[c]["x_out"] for c in range(NCORES)]

    out = np.concatenate(xs4, axis=0).reshape(32, 2048, 1024).astype(np.float32)
    return out
